# version 04
# baseline (speedup 1.0000x reference)
import time

import numpy as np
import ml_dtypes

import concourse.bass as bass
import concourse.mybir as mybir
from concourse.bass_utils import run_bass_kernel_spmd

F32 = mybir.dt.float32
BF16 = mybir.dt.bfloat16
I32 = mybir.dt.int32
AF = mybir.ActivationFunctionType
ALU = mybir.AluOpType
AX = mybir.AxisListType

BF = ml_dtypes.bfloat16
NEG = -1e30

CONSTS = [1e-5, 1e-6, -0.5, 0.5, 1.0, -1.0, 2.0, 0.1, -0.1,
          float(1.0 / 2048.0), float(128 ** -0.5), float(-(128 ** -0.5)),
          float(1.0 / 128.0), 0.0, -2.4260151319598084, -2.0] + [7.0] * 16


class Cfg:
    def __init__(self, T_full=4096, taps=(), solo=False, detect=True, priv=False):
        self.solo = solo
        self.detect = detect
        self.priv = priv
        self.T = T_full
        self.To = T_full // 2
        self.D = 2048
        self.nD = 16
        self.C = 128
        self.nCh = T_full // 128
        self.TCH = 512
        self.nTCo = self.To // self.TCH
        self.nTCf = self.T // self.TCH
        self.taps = set(taps)



def host_pack(inputs, cfg, core, nonce):
    s = core & 1
    b = core // 2
    To = cfg.To
    f32 = lambda a: np.ascontiguousarray(np.asarray(a, np.float32))
    bf = lambda a: np.ascontiguousarray(np.asarray(a).astype(BF))

    d = {}
    x = np.asarray(inputs["L0_attn_a_x"])
    d["x0T"] = f32(x[b, s * To:(s + 1) * To, :].T)

    wqkv = np.asarray(inputs["L0_attn_a_w_qkv"])
    d["a_wqkv"] = f32(np.concatenate([
        wqkv[:, s * 1024:(s + 1) * 1024],
        wqkv[:, 2048 + s * 1024:2048 + (s + 1) * 1024],
        wqkv[:, 4096 + s * 2048:4096 + (s + 1) * 2048]], axis=1))
    d["a_wz"] = f32(np.asarray(inputs["L0_attn_a_w_z"])[:, s * 2048:(s + 1) * 2048])
    d["a_wba"] = f32(np.concatenate([
        np.asarray(inputs["L0_attn_a_w_b"])[:, s * 16:(s + 1) * 16],
        np.asarray(inputs["L0_attn_a_w_a"])[:, s * 16:(s + 1) * 16]], axis=1))
    convw = np.asarray(inputs["L0_attn_a_conv_w"])
    d["a_convT"] = f32(np.concatenate([
        convw[:, s * 1024:(s + 1) * 1024],
        convw[:, 2048 + s * 1024:2048 + (s + 1) * 1024],
        convw[:, 4096 + s * 2048:4096 + (s + 1) * 2048]], axis=1).T)
    d["a_miscT"] = f32(np.stack([
        np.asarray(inputs["L0_attn_a_dt_bias"])[s * 16:(s + 1) * 16],
        np.asarray(inputs["L0_attn_a_A_log"])[s * 16:(s + 1) * 16]], axis=1))
    d["a_normwT"] = f32(np.asarray(inputs["L0_attn_a_norm_w"])[:, None])
    d["a_wout"] = f32(inputs["L0_attn_a_w_out"])

    d["f_wq"] = f32(np.asarray(inputs["L0_ffn_Wq"]).T[:, s * 1024:(s + 1) * 1024])
    d["f_wk"] = f32(np.asarray(inputs["L0_ffn_Wk"]).T[:, s * 1024:(s + 1) * 1024])
    d["f_wv"] = f32(np.asarray(inputs["L0_ffn_Wv"]).T[:, s * 1024:(s + 1) * 1024])
    d["f_wo"] = f32(inputs["L0_ffn_Wo"])
    d["f_A"] = f32(inputs["L0_ffn_A"])
    d["f_B"] = f32(inputs["L0_ffn_B"])
    d["f_lnwT"] = f32(np.stack([np.asarray(inputs["L0_ffn_ln1_w"]),
                                np.asarray(inputs["L0_ffn_ln2_w"])], axis=1))
    d["f_win"] = f32(np.asarray(inputs["L0_ffn_W_in_base"]).T)
    d["f_wup"] = f32(np.asarray(inputs["L0_ffn_W_up_base"]).T)
    d["f_wdown"] = f32(np.asarray(inputs["L0_ffn_W_down"]).T)

    rot = lambda a: np.concatenate([a[s * 1024:(s + 1) * 1024],
                                    a[(1 - s) * 1024:(2 - s) * 1024]], axis=0)
    wx = rot(np.asarray(inputs["L0_attn_b_Wx"]))
    d["b_wx"] = f32(np.concatenate([wx[:, :256],
                                    wx[:, 256 + s * 16:256 + (s + 1) * 16]], axis=1))
    d["b_wg"] = f32(rot(np.asarray(inputs["L0_attn_b_Wg"]))[:, s * 1024:(s + 1) * 1024])
    d["b_bgT"] = f32(np.asarray(inputs["L0_attn_b_bg"])[s * 1024:(s + 1) * 1024].reshape(8, 128).T)
    d["b_wo"] = f32(inputs["L0_attn_b_Wo"])
    d["b_AT"] = f32(np.asarray(inputs["L0_attn_b_A"])[s * 16:(s + 1) * 16].T)
    dp = np.asarray(inputs["L0_attn_b_Dp"])[s * 16:(s + 1) * 16]
    dpc = np.zeros((128, 8), np.float32)
    for p in range(8):
        dpc[:64, p] = dp[2 * p]
        dpc[64:, p] = dp[2 * p + 1]
    d["b_dpc"] = f32(dpc)

    d["c_ident"] = bf(np.eye(128))
    d["c_ones"] = bf(np.ones((128, 128)))
    ii, jj = np.meshgrid(np.arange(128), np.arange(128), indexing="ij")
    d["c_masks"] = bf(np.stack([
        np.where(jj < ii, 0.0, NEG),
        np.where(jj > ii, 0.0, NEG),
        np.where(jj >= ii, 0.0, NEG),
    ]))
    kk2, cc2 = np.meshgrid(np.arange(128), np.arange(896), indexing="ij")
    d["c_cmask"] = bf(np.where(cc2 - 384 >= kk2, 0.0, NEG))
    sel = np.zeros((16, 16 * 128), np.float32)
    for h in range(16):
        sel[h, h * 128:(h + 1) * 128] = 1.0
    d["c_sel"] = bf(sel)
    selk = np.zeros((128, 8 * 16), np.float32)
    selq = np.zeros((128, 8 * 16), np.float32)
    for r in range(8):
        selk[:, r * 16 + r] = 1.0
        selq[:, r * 16 + 8 + r] = 1.0
    d["c_selk"] = bf(selk)
    d["c_selq"] = bf(selq)
    exp2 = np.zeros((8, 16), np.float32)
    for r in range(8):
        exp2[r, 2 * r] = 1.0
        exp2[r, 2 * r + 1] = 1.0
    d["c_exp2"] = bf(exp2)
    selg = np.zeros((16, 8 * 128), np.float32)
    for ft in range(8):
        selg[2 * ft, ft * 128:ft * 128 + 64] = 1.0
        selg[2 * ft + 1, ft * 128 + 64:(ft + 1) * 128] = 1.0
    d["c_selg"] = bf(selg)
    d["c_scal"] = f32(np.tile(np.array([CONSTS], np.float32), (128, 1)))
    d["par"] = np.ascontiguousarray(
        np.array([[s, nonce & 0x3FFFFFFF, 0, 0]], np.int32))
    return d



ENGS = ("sync", "tensor", "vector", "scalar", "gpsimd")
QSEM = {"sync": "q_ld", "scalar": "q_sc", "gpsimd": "q_gp", "vector": "q_dv"}


class K:
    def __init__(self, cfg):
        self.cfg = cfg
        self.nc = bass.Bass("TRN2", target_bir_lowering=False,
                            detect_race_conditions=cfg.detect)
        self.prog = {e: [] for e in ENGS}
        self.sems = {"prolog": 0}
        self.semh = {}
        self.dram = {}
        self.sbuf = {}
        self.regs = {}
        self.flagseq = {}
        self._nflags = 0
        self.sbuf_bytes = 0

    def dram_t(self, name, shape, dtype, **kw):
        h = self.nc.dram_tensor(name, shape, dtype, **kw)
        self.dram[name] = h
        return h

    def sb(self, name, shape, dtype):
        h = self.nc.alloc_sbuf_tensor(name, list(shape), dtype)
        self.sbuf[name] = h
        self.sbuf_bytes += int(np.prod(shape[1:])) * mybir.dt.size(dtype)
        return h

    def psum(self, name, shape=(128, 512), dtype=F32):
        h = self.nc.alloc_psum_tensor(name, list(shape), dtype)
        self.sbuf[name] = h
        return h

    def sem(self, name):
        if name not in self.sems:
            self.sems[name] = 0
        return name

    def flag_id(self, name):
        if name not in self.flagseq:
            self.flagseq[name] = self._nflags
            self._nflags += 1
        return self.flagseq[name]

    def op(self, eng, fn):
        self.prog[eng].append(fn)

    AUTOSEM = {"vector": "dv", "scalar": "ac", "gpsimd": "gp"}

    def emit(self, eng, call, sem=None, by=1):
        auto = self.AUTOSEM.get(eng)
        if auto is not None:
            self.sem(auto)
            prev = self.sems[auto]
            self.sems[auto] += 1
            def fn(e, call=call, auto=auto, prev=prev):
                if prev:
                    e.wait_ge(self.semh[auto], prev)
                call(e).then_inc(self.semh[auto], 1)
            self.op(eng, fn)
            if sem is not None and sem != auto:
                raise ValueError(f"compute op on {eng} counts on '{auto}'")
            return self.sems[auto]
        if sem is not None:
            self.sem(sem)
            self.sems[sem] += by
            def fn(e, call=call, sem=sem, by=by):
                call(e).then_inc(self.semh[sem], by)
        else:
            def fn(e, call=call):
                call(e)
        self.op(eng, fn)
        return self.sems[sem] if sem else None

    def wait(self, eng, sem, value=None):
        self.sem(sem)
        v = self.sems[sem] if value is None else value
        self.op(eng, lambda e, sem=sem, v=v: e.wait_ge(self.semh[sem], v))

    def dma(self, q, out, in_, sem=None, **kw):
        sem = QSEM[q] if sem is None else sem
        self.sem(sem)
        self.sems[sem] += 16
        def fn(e, out=out, in_=in_, kw=kw, sem=sem):
            e.dma_start(out=out, in_=in_, **kw).then_inc(self.semh[sem], 16)
        self.op(q, fn)
        return self.sems[sem]

    def pdma(self, q, pairs, sem=None):
        sem = QSEM[q] if sem is None else sem
        self.sem(sem)
        self.sems[sem] += 16 * len(pairs)
        tgt = self.sems[sem]
        def fn(e, pairs=pairs, sem=sem):
            pr = self.regs[(e.engine, "pr")]
            with e.If_eq(pr, 0):
                for (o0, i0, _o1, _i1) in pairs:
                    e.dma_start(out=o0, in_=i0).then_inc(self.semh[sem], 16)
            with e.Else():
                for (_o0, _i0, o1, i1) in pairs:
                    e.dma_start(out=o1, in_=i1).then_inc(self.semh[sem], 16)
        self.op(q, fn)
        return tgt

    def flag_write(self, name):
        idx = self.flag_id(name)
        self.sem("q_fl")
        prev = self.sems["q_fl"]
        self.sems["q_fl"] += 16
        fl = self.sbuf["flagout"]
        sh = self.dram["sh_flags"]
        def fn(e, idx=idx, prev=prev):
            nr = self.regs[(e.engine, "nonce")]
            tmp = self.regs[(e.engine, "tmp")]
            pr = self.regs[(e.engine, "pr")]
            e.wait_ge(self.semh["q_fl"], prev)
            e.reg_add(tmp, nr, idx + 1)
            e.store(fl[0:1, 0:1], tmp)
            with e.If_eq(pr, 0):
                e.dma_start(out=sh.ap()[0:1, idx:idx + 1], in_=fl[0:1, 0:1]).then_inc(self.semh["q_fl"], 16)
            with e.Else():
                e.dma_start(out=sh.ap()[1:2, idx:idx + 1], in_=fl[0:1, 0:1]).then_inc(self.semh["q_fl"], 16)
        self.op("gpsimd", fn)

    def flag_poll(self, name, done_sem="xs"):
        idx = self.flag_id(name)
        if self.cfg.solo:
            self.sem(done_sem)
            self.sems[done_sem] += 1
            self.op("gpsimd", lambda e, ds=done_sem: e.sem_inc(self.semh[ds], 1))
            return self.sems[done_sem]
        self.sem(done_sem)
        self.sem("q_pl")
        self.sems[done_sem] += 1
        fl = self.sbuf["flagin"]
        sh = self.dram["sh_flags"]
        def fn(e, idx=idx, done_sem=done_sem):
            pr = self.regs[(e.engine, "pr")]
            nr = self.regs[(e.engine, "nonce")]
            tmp = self.regs[(e.engine, "tmp")]
            fr = self.regs[(e.engine, "fr")]
            diff = self.regs[(e.engine, "diff")]
            cnt = self.regs[(e.engine, "cnt")]
            e.reg_add(tmp, nr, idx + 1)
            e.reg_mov(diff, 1)
            with e.While(diff):
                e.reg_add(cnt, cnt, 16)
                with e.If_eq(pr, 0):
                    e.dma_start(out=fl[0:1, 0:1], in_=sh.ap()[1:2, idx:idx + 1]).then_inc(self.semh["q_pl"], 16)
                with e.Else():
                    e.dma_start(out=fl[0:1, 0:1], in_=sh.ap()[0:1, idx:idx + 1]).then_inc(self.semh["q_pl"], 16)
                e.wait_ge(self.semh["q_pl"], cnt)
                e.reg_load(fr, fl[0:1, 0:1])
                e.reg_alu(diff, fr, tmp, ALU.bitwise_xor)
            e.sem_inc(self.semh[done_sem], 1)
        self.op("gpsimd", fn)
        return self.sems[done_sem]

    def tap(self, name, ap):
        if name not in self.cfg.taps:
            return
        sh = [ap.shape[0], int(np.prod(ap.shape[1:]))]
        t = self.dram_t(f"tap_{name}", sh, ap.dtype, kind="ExternalOutput")
        for e in ("vector", "scalar"):
            self.wait("gpsimd", self.AUTOSEM[e], self.sems.get(self.AUTOSEM[e], 0))
        self.sem("tput")
        st = self.dma("gpsimd", t.ap(), ap, sem="tput")
        self.wait("gpsimd", "tput", st)
        self.wait("vector", "tput", st)
        self.wait("scalar", "tput", st)

    def build(self):
        nc = self.nc
        for n in self.sems:
            self.semh[n] = nc.alloc_semaphore(n)
        blk = nc.Block()
        block = blk.__enter__()

        def prolog(e):
            eng = e.engine
            for rn in ("pr", "nonce", "tmp", "fr", "diff", "cnt"):
                self.regs[(eng, rn)] = e.alloc_register(f"r_{rn}")
            e.wait_ge(self.semh["prolog"], 16)
            e.reg_load(self.regs[(eng, "pr")], self.sbuf["sb_par"][0:1, 0:1])
            e.reg_load(self.regs[(eng, "nonce")], self.sbuf["sb_par"][0:1, 1:2])
            e.reg_mov(self.regs[(eng, "cnt")], 0)

        def mk(engname):
            def run(e):
                if engname == "sync":
                    e.dma_start(out=self.sbuf["sb_par"][:],
                                in_=self.dram["par"].ap()).then_inc(self.semh["prolog"], 16)
                prolog(e)
                for fn in self.prog[engname]:
                    fn(e)
            return run

        block.sync(mk("sync"))
        block.tensor(mk("tensor"))
        block.vector(mk("vector"))
        block.scalar(mk("scalar"))
        block.gpsimd(mk("gpsimd"))
        blk.__exit__(None, None, None)
        return nc


def dview(t, p=128):
    return t.ap().rearrange("(n p) t -> p n t", p=p)



def declare(k):
    cfg = k.cfg
    T, To, D = cfg.T, cfg.To, cfg.D
    dt = k.dram_t
    dt("x0T", [D, To], F32, kind="ExternalInput")
    dt("a_wqkv", [D, 4096], F32, kind="ExternalInput")
    dt("a_wz", [D, 2048], F32, kind="ExternalInput")
    dt("a_wba", [D, 32], F32, kind="ExternalInput")
    dt("a_convT", [4096, 4], F32, kind="ExternalInput")
    dt("a_miscT", [16, 2], F32, kind="ExternalInput")
    dt("a_normwT", [128, 1], F32, kind="ExternalInput")
    dt("a_wout", [4096, D], F32, kind="ExternalInput")
    dt("f_wq", [D, 1024], F32, kind="ExternalInput")
    dt("f_wk", [D, 1024], F32, kind="ExternalInput")
    dt("f_wv", [D, 1024], F32, kind="ExternalInput")
    dt("f_wo", [D, D], F32, kind="ExternalInput")
    dt("f_A", [D, 16], F32, kind="ExternalInput")
    dt("f_B", [16, 8192], F32, kind="ExternalInput")
    dt("f_lnwT", [D, 2], F32, kind="ExternalInput")
    dt("f_win", [D, 8192], F32, kind="ExternalInput")
    dt("f_wup", [D, 8192], F32, kind="ExternalInput")
    dt("f_wdown", [8192, D], F32, kind="ExternalInput")
    dt("b_wx", [D, 272], F32, kind="ExternalInput")
    dt("b_wg", [D, 1024], F32, kind="ExternalInput")
    dt("b_bgT", [128, 8], F32, kind="ExternalInput")
    dt("b_wo", [D, D], F32, kind="ExternalInput")
    dt("b_AT", [128, 16], F32, kind="ExternalInput")
    dt("b_dpc", [128, 8], F32, kind="ExternalInput")
    dt("c_ident", [128, 128], BF16, kind="ExternalInput")
    dt("c_ones", [128, 128], BF16, kind="ExternalInput")
    dt("c_masks", [3, 128, 128], BF16, kind="ExternalInput")
    dt("c_cmask", [128, 896], BF16, kind="ExternalInput")
    dt("c_sel", [16, 2048], BF16, kind="ExternalInput")
    dt("c_selk", [128, 128], BF16, kind="ExternalInput")
    dt("c_selq", [128, 128], BF16, kind="ExternalInput")
    dt("c_exp2", [8, 16], BF16, kind="ExternalInput")
    dt("c_selg", [16, 1024], BF16, kind="ExternalInput")
    dt("c_scal", [128, 32], F32, kind="ExternalInput")
    dt("par", [1, 4], I32, kind="ExternalInput")
    dt("outT", [D, To], F32, kind="ExternalOutput")
    if "n0" in cfg.taps:
        dt("tap_n0", [D, T], BF16, kind="ExternalOutput")
    if "featA" in cfg.taps:
        dt("tap_featA", [4096, T], BF16, kind="ExternalOutput")
    if "n1" in cfg.taps:
        dt("tap_n1", [D, T], BF16, kind="ExternalOutput")
        dt("tap_nrs", [D, T], BF16, kind="ExternalOutput")
    if "n0b" in cfg.taps:
        dt("tap_n0b", [D, T], BF16, kind="ExternalOutput")
    if "featB" in cfg.taps:
        dt("tap_featB", [D, T], BF16, kind="ExternalOutput")
    dt("resid", [D, To], F32)
    dt("aw_qkv", [D, 4096], BF16)
    dt("aw_z", [D, 2048], BF16)
    dt("aw_ba", [D, 32], BF16)
    dt("aw_out", [4096, D], BF16)
    dt("fw_q", [D, 1024], BF16)
    dt("fw_k", [D, 1024], BF16)
    dt("fw_v", [D, 1024], BF16)
    dt("fw_A", [D, 16], BF16)
    dt("fw_woa", [D, 16], BF16)
    dt("fw_B", [16, 8192], BF16)
    dt("fw_in", [D, 8192], BF16)
    dt("fw_up", [D, 8192], BF16)
    dt("fw_down", [8192, D], BF16)
    dt("bw_x", [D, 272], BF16)
    dt("bw_g", [D, 1024], BF16)
    dt("bw_o", [D, D], BF16)
    dt("qT_scr", [1024, T], BF16)
    shkw = {} if cfg.priv else {"addr_space": "Shared"}
    dt("sh_n0", [D, T], BF16, **shkw)
    dt("sh_n1", [D, T], BF16, **shkw)
    dt("sh_nrs", [D, T], BF16, **shkw)
    dt("sh_featA", [4096, T], BF16, **shkw)
    dt("sh_featB", [D, T], BF16, **shkw)
    dt("sh_woa", [2, 16 * T], F32, **shkw)
    dt("sh_flags", [2, 64], I32, **shkw)

    sb = k.sb
    sb("sb_par", (1, 4), I32)
    sb("kconst", (128, 32), F32)
    for i, v in enumerate(CONSTS):
        k.nc.const_aps.register(F32, v, k.sbuf["kconst"][:, i:i + 1])
        k.nc.const_aps.register(BF16, v, k.sbuf["kconst"][:, i:i + 1])
    sb("flagout", (1, 16), I32)
    sb("flagin", (1, 16), I32)
    sb("ident", (128, 128), BF16)
    sb("ones", (128, 128), BF16)
    sb("masks", (128, 3 * 128), BF16)
    sb("cmask", (128, 896), BF16)
    sb("sel", (16, 2048), BF16)
    sb("selk", (128, 128), BF16)
    sb("selq", (128, 128), BF16)
    sb("exp2", (8, 16), BF16)
    sb("selg", (16, 1024), BF16)
    sb("lnw", (128, 2 * 16), F32)
    sb("miscT", (16, 2), F32)
    sb("normw", (128, 1), F32)
    sb("bAT", (128, 16), F32)
    sb("bdpc", (128, 8), F32)
    sb("bbg", (128, 8), F32)
    sb("convw", (128, 32 * 4), F32)
    sb("nch", (128, 16384), BF16)
    sb("wbuf", (128, 12288), BF16)
    sb("bigb", (128, 36864), BF16)
    sb("rch", (128, 8192), F32)
    sb("nst", (128, 8192), BF16)
    sb("rows", (1, 3584), F32)
    sb("rows2", (1, 1536), BF16)
    sb("tails", (128, 128), BF16)
    sb("colsA", (128, 256), F32)
    sb("colsB", (128, 320), BF16)
    sb("f32s", (128, 512), F32)
    assert k.sbuf_bytes <= 212600, k.sbuf_bytes

    for i in range(7):
        k.psum(f"pb{i}")
    k.psum("pb7", (128, 1024), BF16)


def load_consts(k):
    cfg = k.cfg
    d, s = k.dram, k.sbuf
    k.dma("sync", s["kconst"][:], d["c_scal"].ap(), sem="cld")
    k.dma("sync", s["ident"][:], d["c_ident"].ap(), sem="cld")
    k.dma("sync", s["ones"][:], d["c_ones"].ap(), sem="cld")
    k.dma("sync", s["masks"][:].rearrange("p (m c) -> p m c", m=3), d["c_masks"].ap().rearrange("m p c -> p m c"), sem="cld")
    k.dma("sync", s["cmask"][:], d["c_cmask"].ap(), sem="cld")
    k.dma("sync", s["sel"][:], d["c_sel"].ap(), sem="cld")
    k.dma("sync", s["selk"][:], d["c_selk"].ap(), sem="cld")
    k.dma("sync", s["selq"][:], d["c_selq"].ap(), sem="cld")
    k.dma("sync", s["exp2"][:], d["c_exp2"].ap(), sem="cld")
    k.dma("sync", s["selg"][:], d["c_selg"].ap(), sem="cld")
    k.dma("sync", s["lnw"][:].rearrange("p (n c) -> p n c", n=16), d["f_lnwT"].ap().rearrange("(n p) c -> p n c", p=128), sem="cld")
    k.dma("sync", s["miscT"][:], d["a_miscT"].ap(), sem="cld")
    k.dma("sync", s["normw"][:], d["a_normwT"].ap(), sem="cld")
    k.dma("sync", s["bAT"][:], d["b_AT"].ap(), sem="cld")
    k.dma("sync", s["bdpc"][:], d["b_dpc"].ap(), sem="cld")
    k.dma("sync", s["bbg"][:], d["b_bgT"].ap(), sem="cld")
    k.dma("sync", s["convw"][:].rearrange("p (n c) -> p n c", n=32), d["a_convT"].ap().rearrange("(n p) c -> p n c", p=128), sem="cld")
    k.dma("scalar", d["resid"].ap(), d["x0T"].ap(), sem="rcp")


def cast_weights(k):
    cfg = k.cfg
    d, s = k.dram, k.sbuf
    jobs = [
        ("a_wqkv", "aw_qkv", 4096, None),
        ("a_wz", "aw_z", 2048, None),
        ("a_wba", "aw_ba", 32, None),
        ("a_wout", "aw_out", 2048, "normw"),
        ("f_wq", "fw_q", 1024, "ln1"),
        ("f_wk", "fw_k", 1024, "ln1"),
        ("f_wv", "fw_v", 1024, "ln1"),
        ("f_A", "fw_A", 16, "ln2"),
        ("f_B", "fw_B", 8192, None),
        ("f_win", "fw_in", 8192, "ln2"),
        ("f_wup", "fw_up", 8192, "ln2"),
        ("f_wdown", "fw_down", 2048, None),
        ("b_wx", "bw_x", 272, None),
        ("b_wg", "bw_g", 1024, None),
        ("b_wo", "bw_o", 2048, None),
    ]
    CH = 4096
    chunks = []
    for (src, dst, ncols, fold) in jobs:
        srct = d[src]
        nrow = srct.shape[0]
        nrt = (nrow + 127) // 128
        for rt in range(nrt):
            nr = min(128, nrow - rt * 128)
            for c0 in range(0, ncols, CH):
                cw = min(CH, ncols - c0)
                chunks.append((src, dst, fold, rt, nr, c0, cw))
    LQ = ("sync", "scalar")
    SQ = ("gpsimd", "gpsimd")
    stv = {0: 0, 1: 0}
    for i, (src, dst, fold, rt, nr, c0, cw) in enumerate(chunks):
        sl = i % 2
        lo = sl * 4096
        if i >= 2:
            k.wait(LQ[sl], f"wb{sl}", stv[sl])
        ld = k.dma(LQ[sl], s["rch"][0:nr, lo:lo + cw],
                   d[src].ap()[rt * 128:rt * 128 + nr, c0:c0 + cw], sem=f"wl{sl}")
        k.wait("vector", f"wl{sl}", ld)
        if fold == "ln1":
            scal = s["lnw"][:, 2 * rt:2 * rt + 1]
        elif fold == "ln2":
            scal = s["lnw"][:, 2 * rt + 1:2 * rt + 2]
        elif fold == "normw":
            scal = s["normw"][:, 0:1]
        else:
            scal = None
        if scal is None:
            dv = k.emit("vector",
                        lambda e, cw=cw, nr=nr, lo=lo: e.tensor_copy(
                            s["nst"][0:nr, lo:lo + cw], s["rch"][0:nr, lo:lo + cw]),
                        "dv")
        else:
            dv = k.emit("vector",
                        lambda e, cw=cw, scal=scal, nr=nr, lo=lo: e.tensor_scalar_mul(
                            s["nst"][0:nr, lo:lo + cw], s["rch"][0:nr, lo:lo + cw], scal),
                        "dv")
        k.wait(SQ[sl], "dv", dv)
        stv[sl] = k.dma(SQ[sl], d[dst].ap()[rt * 128:rt * 128 + nr, c0:c0 + cw],
                        s["nst"][0:nr, lo:lo + cw], sem=f"wb{sl}")
    for sl in range(2):
        k.wait("sync", f"wb{sl}", stv[sl])
        k.wait("vector", f"wb{sl}", stv[sl])
    pb0 = s["pb0"]
    la = k.dma("sync", s["f32s"][:, 0:256].rearrange("p (n c) -> p n c", n=16),
               d["f_A"].ap().rearrange("(n p) c -> p n c", p=128), sem="wld")
    k.wait("tensor", "wld", la)
    for ct in range(16):
        for dt_ in range(16):
            slot = (ct * 16 + dt_) % 2
            if ct * 16 + dt_ >= 2:
                k.wait("sync", "pe", k.sems.get("pe", 0))
            lw = k.dma("sync", s["rch"][:, slot * 128:slot * 128 + 128],
                       d["f_wo"].ap()[dt_ * 128:(dt_ + 1) * 128, ct * 128:(ct + 1) * 128],
                       sem=f"wc{slot}")
            k.wait("tensor", f"wc{slot}", lw)
            k.emit("tensor",
                   lambda e, dt_=dt_, slot=slot: e.matmul(
                       pb0[:, 0:16], s["rch"][:, slot * 128:slot * 128 + 128],
                       s["f32s"][:, dt_ * 16:(dt_ + 1) * 16],
                       start=(dt_ == 0), stop=(dt_ == 15)),
                   "pe")
        k.wait("vector", "pe", k.sems["pe"])
        dv = k.emit("vector", lambda e: e.tensor_copy(s["nst"][:, 0:16], pb0[:, 0:16]), "dv")
        k.wait("gpsimd", "dv", dv)
        st = k.dma("gpsimd", d["fw_woa"].ap()[ct * 128:(ct + 1) * 128, 0:16], s["nst"][:, 0:16], sem="wst")
        k.wait("vector", "wst", st)
        k.wait("tensor", "dv", dv)
    k.wait("sync", "wst", k.sems["wst"])



def emit_ln_chunk(k, nw, with_rs):
    cfg = k.cfg
    s = k.sbuf
    nD = cfg.nD
    pb5, pb6, pb7 = s["pb5"], s["pb6"], s["pb2"]
    stg = s["bigb"][:, 36352:36352 + nw]
    
    for t in range(nD):
        k.emit("vector",
               lambda e, t=t: e.tensor_copy(s["nst"][:, t * nw:(t + 1) * nw],
                                            s["rch"][:, t * nw:(t + 1) * nw]),
               "dv")
    dvc = k.sems["dv"]
    k.wait("tensor", "dv", dvc)
    k.wait("scalar", "dv", dvc)
    for t in range(nD):
        ac = k.emit("scalar",
                    lambda e, t=t: e.activation(out=stg, in_=s["nst"][:, t * nw:(t + 1) * nw],
                                                func=AF.Square),
                    "ac")
        k.wait("tensor", "ac", ac)
        k.emit("tensor",
               lambda e, t=t: e.matmul(pb6[0:1, 0:nw], s["ones"][:, 0:1],
                                       s["nst"][:, t * nw:(t + 1) * nw],
                                       start=(t == 0), stop=(t == nD - 1)),
               None)
        pe = k.emit("tensor",
                    lambda e, t=t: e.matmul(pb7[0:1, 0:nw], s["ones"][:, 0:1], stg,
                                            start=(t == 0), stop=(t == nD - 1)),
                    "pe")
        if t < nD - 1:
            k.wait("scalar", "pe", pe)
    k.wait("vector", "pe", k.sems["pe"])
    k.emit("vector", lambda e: e.tensor_scalar_mul(s["rows"][0:1, 0 * 512:0 * 512 + nw], pb6[0:1, 0:nw], 1.0 / cfg.D), None)
    k.emit("vector", lambda e: e.tensor_scalar_mul(s["rows"][0:1, 1 * 512:1 * 512 + nw], pb7[0:1, 0:nw], 1.0 / cfg.D), None)
    k.emit("vector", lambda e: e.scalar_tensor_tensor(
        out=s["rows"][0:1, 2 * 512:2 * 512 + nw], in0=s["rows"][0:1, 0 * 512:0 * 512 + nw], scalar=-1.0, in1=s["rows"][0:1, 0 * 512:0 * 512 + nw],
        op0=ALU.mult, op1=ALU.mult), None)
    dv = k.emit("vector", lambda e: e.tensor_add(s["rows"][0:1, 2 * 512:2 * 512 + nw], s["rows"][0:1, 2 * 512:2 * 512 + nw], s["rows"][0:1, 1 * 512:1 * 512 + nw]), "dv")
    k.wait("scalar", "dv", dv)
    k.emit("scalar", lambda e: e.activation(out=s["rows"][0:1, 3 * 512:3 * 512 + nw], in_=s["rows"][0:1, 2 * 512:2 * 512 + nw], func=AF.Ln, bias=1e-5), None)
    ac = k.emit("scalar", lambda e: e.activation(out=s["rows"][0:1, 3 * 512:3 * 512 + nw], in_=s["rows"][0:1, 3 * 512:3 * 512 + nw], func=AF.Exp, scale=-0.5), "ac")
    k.wait("vector", "ac", ac)
    if with_rs:
        k.emit("vector", lambda e: e.tensor_mul(s["rows"][0:1, 5 * 512:5 * 512 + nw], s["rows"][0:1, 3 * 512:3 * 512 + nw], s["rows"][0:1, 3 * 512:3 * 512 + nw]), None)
        dv = k.emit("vector", lambda e: e.tensor_mul(s["rows"][0:1, 5 * 512:5 * 512 + nw], s["rows"][0:1, 5 * 512:5 * 512 + nw], s["rows"][0:1, 2 * 512:2 * 512 + nw]), "dv")
        k.wait("scalar", "dv", dv)
        k.emit("scalar", lambda e: e.activation(out=s["rows"][0:1, 6 * 512:6 * 512 + nw], in_=s["rows"][0:1, 5 * 512:5 * 512 + nw], func=AF.Ln, bias=1e-6), None)
        ac = k.emit("scalar", lambda e: e.activation(out=s["rows"][0:1, 6 * 512:6 * 512 + nw], in_=s["rows"][0:1, 6 * 512:6 * 512 + nw], func=AF.Exp, scale=-0.5), "ac")
        k.wait("vector", "ac", ac)
        k.emit("vector", lambda e: e.tensor_copy(s["rows2"][0:1, 2 * 512:2 * 512 + nw], s["rows"][0:1, 6 * 512:6 * 512 + nw]), None)
    k.emit("vector", lambda e: e.tensor_copy(s["rows2"][0:1, 0 * 512:0 * 512 + nw], s["rows"][0:1, 3 * 512:3 * 512 + nw]), None)
    dvv = k.emit("vector", lambda e: e.tensor_copy(s["rows2"][0:1, 1 * 512:1 * 512 + nw], s["rows"][0:1, 0 * 512:0 * 512 + nw]), "dv")
    k.wait("tensor", "dv", dvv)
    k.emit("tensor", lambda e: e.matmul(pb6[:, 0:nw], s["ones"][0:1, :], s["rows2"][0:1, 0 * 512:0 * 512 + nw], start=True, stop=True), None)
    pe = k.emit("tensor", lambda e: e.matmul(pb7[:, 0:nw], s["ones"][0:1, :], s["rows2"][0:1, 1 * 512:1 * 512 + nw], start=True, stop=True), "pe")
    k.wait("vector", "pe", pe)
    dv = k.emit("vector", lambda e: e.tensor_copy(s["f32s"][:, 0:nw], pb6[:, 0:nw]), "dv")
    for t in range(nD):
        k.emit("vector",
               lambda e, t=t: e.tensor_sub(pb5[:, 0:nw], s["rch"][:, t * nw:(t + 1) * nw], pb7[:, 0:nw]),
               None)
        k.emit("vector",
               lambda e, t=t: e.tensor_mul(s["nst"][:, t * nw:(t + 1) * nw], pb5[:, 0:nw], s["f32s"][:, 0:nw]),
               "dv")
    return k.sems["dv"]


def emit_nrs_tiles(k, nw, dst_slice_fn):
    s = k.sbuf
    pb4 = s["pb4"]
    k.wait("tensor", "dv", k.sems["dv"])
    pe = k.emit("tensor", lambda e: e.matmul(pb4[:, 0:nw], s["ones"][0:1, :], s["rows2"][0:1, 2 * 512:2 * 512 + nw], start=True, stop=True), "pe")
    k.wait("vector", "pe", pe)
    for t in range(k.cfg.nD):
        k.emit("vector",
               lambda e, t=t: e.tensor_mul(dst_slice_fn(t), s["nst"][:, t * nw:(t + 1) * nw], pb4[:, 0:nw]),
               "dv")
    return k.sems["dv"]



def phase_ln0(k):
    cfg = k.cfg
    s, d = k.sbuf, k.dram
    nw = cfg.TCH
    shn = dview(d["sh_n0"])
    for c in range(cfg.nTCo):
        ld = k.dma("sync", s["rch"][:].rearrange("p (n w) -> p n w", n=16),
                   dview(d["resid"])[:, :, c * nw:(c + 1) * nw], sem="lnld")
        k.wait("vector", "lnld", ld)
        k.wait("scalar", "lnld", ld)
        k.wait("tensor", "lnld", ld)
        dv = emit_ln_chunk(k, nw, with_rs=False)
        k.wait("gpsimd", "dv", dv)
        st = k.pdma("gpsimd", [(
            shn[:, :, c * nw:(c + 1) * nw],
            s["nst"][:].rearrange("p (n w) -> p n w", n=16),
            shn[:, :, cfg.To + c * nw:cfg.To + (c + 1) * nw],
            s["nst"][:].rearrange("p (n w) -> p n w", n=16),
        )], sem="nsh")
        k.wait("vector", "nsh", st)
        k.wait("sync", "nsh", st)
        k.wait("scalar", "nsh", st)
    k.wait("gpsimd", "nsh", k.sems["nsh"])
    k.flag_write("n0_a")



BG_DG = 0
BG_MX = 1040
BG_VT = 25616
BG_KT = 25872
BG_XP = 26896
BG_XT = 27920
BG_UU = 28944
BG_SH = 29968
BG_SL = 32016
BG_EA = 34064
BG_CO = 34576
NS_R = 0
NS_TB = 1152
NS_E = 1536
NS_PT = 2560
NS_KP = 3584
NS_BV = 3840
NS_CR = 4352
RC_BA = 0
RC_T1 = 512
RC_G = 2048
RC_LNB = 2560
RC_BETA = 3072
RC_LNS = 3584
RC_LNS16 = 4096
RC_CH = 4608
RC_GE = 5120
RC_ONES = 5248
RC_SP = 5376


def phase_attn_a(k):
    cfg = k.cfg
    s, d = k.sbuf, k.dram
    nD = cfg.nD
    pb = [s[f"pb{i}"] for i in range(8)]
    big, nst, rch = s["bigb"], s["nst"], s["rch"]

    def MM(out, lhsT, rhs, start, stop, pe=False):
        return k.emit("tensor", lambda e, out=out, lhsT=lhsT, rhs=rhs,
                      start=start, stop=stop:
                      e.matmul(out, lhsT, rhs, start=start, stop=stop),
                      "pe" if pe else None)

    def mxsc(ft):
        return big[:, BG_MX + ft * 512: BG_MX + (ft + 1) * 512]

    k.emit("vector", lambda e: e.memset(big[:, BG_SH:BG_SH + 4096], 0))
    k.emit("gpsimd", lambda e: e.memset(s["tails"][:], 0))
    k.emit("gpsimd", lambda e: e.memset(rch[0:16, RC_ONES:RC_ONES + 128], 1.0))
    k.wait("vector", "gp", k.sems["gp"])
    k.wait("scalar", "gp", k.sems["gp"])
    k.emit("scalar", lambda e: e.activation(out=s["colsA"][0:16, 255:256],
                                            in_=s["miscT"][0:16, 1:2], func=AF.Exp))
    k.wait("vector", "ac", k.sems["ac"])
    k.emit("vector", lambda e: e.tensor_scalar_mul(
        s["colsA"][0:16, 254:255], s["colsA"][0:16, 255:256], -1.0))
    k.wait("gpsimd", "rcp", k.sems["rcp"])

    k.tap("dp0", s["bdpc"][:])
    k.flag_poll("n0_a", "xs")
    k.wait("sync", "xs", k.sems["xs"])

    shn = dview(d["sh_n0"])
    wq = dview(d["aw_qkv"])
    wz = dview(d["aw_z"])
    wba = dview(d["aw_ba"])
    featA = dview(d["sh_featA"])

    pe_sc = {}
    for sc in range(cfg.nTCf):
        slot = sc % 2
        nview = s["nch"][:, slot * 8192: slot * 8192 + 8192].rearrange(
            "p (n w) -> p n w", n=16)
        if sc >= 2:
            k.wait("sync", "pe", pe_sc[sc - 2])
        ln = k.dma("sync", nview, shn[:, :, sc * 512:(sc + 1) * 512], sem=f"na{slot}")
        k.wait("tensor", f"na{slot}", ln)
        k.wait("tensor", "ac", k.sems["ac"])
        k.wait("tensor", "dv", k.sems["dv"])

        acq = {}
        pes_ft = {}
        for ft in range(48):
            wsl = ft % 6
            wq_ = ("sync", "scalar")[wsl % 2]
            wtile = s["wbuf"][:, wsl * 2048: wsl * 2048 + 2048]
            if ft >= 6:
                k.wait(wq_, "pe", pes_ft[ft - 6])
            elif ft <= 1 and sc > 0:
                k.wait(wq_, "pe", k.sems["pe"])
            src = wq[:, :, ft * 128:(ft + 1) * 128] if ft < 32 else \
                wz[:, :, (ft - 32) * 128:(ft - 31) * 128]
            lw = k.dma(wq_, wtile.rearrange("p (n c) -> p n c", n=16), src,
                       sem=f"wa{wsl}")
            k.wait("tensor", f"wa{wsl}", lw)
            if ft >= 2 and (ft - 2) in acq:
                k.wait("tensor", "ac", acq[ft - 2])
            pbank = pb[ft % 2]
            for dt_ in range(nD):
                pes_ft[ft] = MM(pbank[:, 0:512], wtile[:, dt_ * 128:(dt_ + 1) * 128],
                                nview[:, dt_, :], dt_ == 0, dt_ == nD - 1,
                                pe=(dt_ == nD - 1))
            if ft >= 32:
                k.wait("scalar", "pe", k.sems["pe"])
                acq[ft] = k.emit("scalar", lambda e, ft=ft, pbank=pbank: e.activation(
                    out=mxsc(ft), in_=pbank[:, 0:512], func=AF.Silu))
                continue
            crs = NS_CR + (ft % 2) * 520
            k.wait("scalar", "pe", k.sems["pe"])
            k.emit("scalar", lambda e, crs=crs, ft=ft: e.copy(
                nst[:, crs:crs + 3], s["tails"][:, ft * 4: ft * 4 + 3]))
            k.emit("scalar", lambda e, crs=crs, pbank=pbank: e.copy(
                nst[:, crs + 3:crs + 515], pbank[:, 0:512]))
            ac = k.emit("scalar", lambda e, crs=crs, ft=ft: e.copy(
                s["tails"][:, ft * 4: ft * 4 + 3], nst[:, crs + 512:crs + 515]))
            acc = s["f32s"][:, 0:512] if ft % 2 == 0 else rch[:, 7680:8192]
            k.wait("vector", "ac", ac)
            k.emit("vector", lambda e, crs=crs, ft=ft, acc=acc: e.tensor_scalar_mul(
                acc, nst[:, crs:crs + 512], s["convw"][:, ft * 4: ft * 4 + 1]))
            for kk in range(1, 4):
                k.emit("vector", lambda e, crs=crs, ft=ft, kk=kk, acc=acc:
                       e.scalar_tensor_tensor(
                           out=acc, in0=nst[:, crs + kk: crs + kk + 512],
                           scalar=s["convw"][:, ft * 4 + kk: ft * 4 + kk + 1],
                           in1=acc, op0=ALU.mult, op1=ALU.add))
            dv = k.emit("vector", lambda e: e.engine_nop())
            k.wait("scalar", "dv", dv)
            acq[ft] = k.emit("scalar", lambda e, ft=ft, acc=acc: e.activation(
                out=mxsc(ft), in_=acc, func=AF.Silu))
        k.wait("tensor", "ac", k.sems["ac"])
        if sc == 0:
            k.tap("mx0", big[:, BG_MX:BG_MX + 24576])

        k.wait("sync", "pe", k.sems["pe"])
        lb = k.dma("sync", s["wbuf"][:, 0:512].rearrange("p (n c) -> p n c", n=16),
                   wba[:, :, 0:32], sem="wba")
        k.wait("tensor", "wba", lb)
        k.wait("tensor", "dv", k.sems["dv"])
        for dt_ in range(nD):
            MM(pb[4][0:16, 0:512], s["wbuf"][:, dt_ * 32:dt_ * 32 + 16],
               nview[:, dt_, :], dt_ == 0, dt_ == nD - 1, pe=(dt_ == nD - 1))
        for dt_ in range(nD):
            MM(pb[5][0:16, 0:512], s["wbuf"][:, dt_ * 32 + 16:dt_ * 32 + 32],
               nview[:, dt_, :], dt_ == 0, dt_ == nD - 1, pe=(dt_ == nD - 1))
        k.wait("vector", "pe", k.sems["pe"])
        k.emit("vector", lambda e: e.tensor_copy(rch[0:16, RC_BA:RC_BA + 512],
                                                 pb[4][0:16, 0:512]))
        k.emit("vector", lambda e: e.tensor_copy(rch[32:48, RC_BA:RC_BA + 512],
                                                 pb[5][0:16, 0:512]))

        bsl = rch[0:16, RC_BA:RC_BA + 512]
        asl = rch[32:48, RC_BA:RC_BA + 512]
        T1 = rch[0:16, RC_T1:RC_T1 + 512]
        T2 = rch[0:16, RC_T1 + 512:RC_T1 + 1024]
        T3 = rch[0:16, RC_T1 + 1024:RC_T1 + 1536]

        k.wait("scalar", "dv", k.sems["dv"])
        k.emit("scalar", lambda e: e.activation(out=rch[0:16, RC_BETA:RC_BETA + 512],
                                                in_=bsl, func=AF.Sigmoid))
        k.emit("scalar", lambda e: e.activation(out=T1, in_=bsl, func=AF.Identity,
                                                scale=-1.0))
        ac = k.emit("scalar", lambda e: e.activation(out=T2, in_=T1, func=AF.Abs))
        k.wait("scalar", "ac", ac)
        k.emit("scalar", lambda e: e.activation(out=T2, in_=T2, func=AF.Exp, scale=-1.0))
        ac = k.emit("scalar", lambda e: e.activation(out=T2, in_=T2, func=AF.Ln, bias=1.0))
        k.wait("vector", "ac", ac)
        k.emit("vector", lambda e: e.tensor_scalar_max(T3, T1, 0.0))
        k.emit("vector", lambda e: e.tensor_add(T3, T3, T2))
        dv = k.emit("vector", lambda e: e.tensor_scalar_mul(
            rch[0:16, RC_LNB:RC_LNB + 512], T3, -1.0))
        k.wait("scalar", "dv", dv)
        acb = k.emit("scalar", lambda e: e.activation(out=T1, in_=asl, func=AF.Identity,
                                                      bias=s["miscT"][0:16, 0:1]))
        k.wait("scalar", "ac", acb)
        ac = k.emit("scalar", lambda e: e.activation(out=T2, in_=T1, func=AF.Abs))
        k.wait("scalar", "ac", ac)
        k.emit("scalar", lambda e: e.activation(out=T2, in_=T2, func=AF.Exp, scale=-1.0))
        ac = k.emit("scalar", lambda e: e.activation(out=T2, in_=T2, func=AF.Ln, bias=1.0))
        k.wait("vector", "ac", ac)
        k.emit("vector", lambda e: e.tensor_scalar_max(T3, T1, 0.0))
        k.emit("vector", lambda e: e.tensor_add(T3, T3, T2))
        k.emit("vector", lambda e: e.tensor_scalar_mul(
            rch[0:16, RC_G:RC_G + 512], T3, s["colsA"][0:16, 254:255]))

        for r in range(8):
            ac = k.emit("scalar", lambda e, r=r: e.activation(
                out=nst[:, NS_CR:NS_CR + 512], in_=mxsc(8 + r), func=AF.Square))
            k.wait("tensor", "ac", ac)
            MM(pb[5][0:16, 0:512], s["selk"][:, r * 16:(r + 1) * 16],
               nst[:, NS_CR:NS_CR + 512], r == 0, False, pe=True)
            k.wait("scalar", "pe", k.sems["pe"])
            ac = k.emit("scalar", lambda e, r=r: e.activation(
                out=nst[:, NS_CR:NS_CR + 512], in_=mxsc(r), func=AF.Square))
            k.wait("tensor", "ac", ac)
            MM(pb[5][0:16, 0:512], s["selq"][:, r * 16:(r + 1) * 16],
               nst[:, NS_CR:NS_CR + 512], False, r == 7, pe=True)
            if r < 7:
                k.wait("scalar", "pe", k.sems["pe"])
        k.wait("scalar", "pe", k.sems["pe"])
        k.emit("scalar", lambda e: e.activation(out=rch[0:16, RC_LNS:RC_LNS + 512],
                                                in_=pb[5][0:16, 0:512], func=AF.Ln,
                                                bias=1e-6))
        ac = k.emit("scalar", lambda e: e.activation(
            out=rch[0:16, RC_LNS:RC_LNS + 512],
            in_=rch[0:16, RC_LNS:RC_LNS + 512], func=AF.Identity, scale=-0.5))
        k.wait("vector", "ac", ac)
        dv = k.emit("vector", lambda e: e.tensor_copy(
            nst[0:8, NS_TB:NS_TB + 512], rch[0:8, RC_LNS:RC_LNS + 512]))
        k.wait("tensor", "dv", dv)
        MM(pb[6][0:16, 0:512], s["exp2"][0:8, 0:16], nst[0:8, NS_TB:NS_TB + 512],
           True, True, pe=True)
        k.wait("vector", "pe", k.sems["pe"])
        k.emit("vector", lambda e: e.tensor_copy(
            rch[0:16, RC_LNS16:RC_LNS16 + 512], pb[6][0:16, 0:512]))

        if sc == 0:
            k.tap("rows0", rch[:, 0:8192])
        for c in range(4):
            emit_dn_chunk(k, sc, c)
            ch = sc * 4 + c
            k.wait("gpsimd", "dv", k.sems["dv"])
            st = k.pdma("gpsimd", [(
                featA[:, 0:16, ch * 128:(ch + 1) * 128],
                big[:, BG_CO:BG_CO + 2048].rearrange("p (n t) -> p n t", n=16),
                featA[:, 16:32, ch * 128:(ch + 1) * 128],
                big[:, BG_CO:BG_CO + 2048].rearrange("p (n t) -> p n t", n=16),
            )], sem="fsh")
            k.wait("vector", "fsh", st)
        pe_sc[sc] = k.sems["pe"]

    k.wait("gpsimd", "fsh", k.sems["fsh"])
    k.flag_write("featA")


def emit_dn_chunk(k, sc, c):
    cfg = k.cfg
    s = k.sbuf
    pb = [s[f"pb{i}"] for i in range(8)]
    big, nst, rch = s["bigb"], s["nst"], s["rch"]
    cA, cB = s["colsA"], s["colsB"]
    LNC = float(np.log(128.0 ** -0.5))
    c0 = c * 128

    def MM(out, lhsT, rhs, start, stop, pe=False):
        return k.emit("tensor", lambda e, out=out, lhsT=lhsT, rhs=rhs,
                      start=start, stop=stop:
                      e.matmul(out, lhsT, rhs, start=start, stop=stop),
                      "pe" if pe else None)

    def mx(ft, w=128):
        return big[:, BG_MX + ft * 512 + c0: BG_MX + ft * 512 + c0 + w]

    G = rch[0:16, RC_CH:RC_CH + 128]
    GLS = rch[0:16, RC_CH + 128:RC_CH + 256]
    NGS = rch[0:16, RC_CH + 256:RC_CH + 384]
    GRT = rch[0:16, RC_CH + 384:RC_CH + 512]
    LNS16c = rch[0:16, RC_LNS16 + c0:RC_LNS16 + c0 + 128]
    SP1 = rch[0:16, RC_SP:RC_SP + 128]
    SP2 = rch[0:16, RC_SP + 128:RC_SP + 256]

    k.emit("vector", lambda e: e.tensor_tensor_scan(
        out=G, data0=rch[0:16, RC_ONES:RC_ONES + 128],
        data1=rch[0:16, RC_G + c0:RC_G + c0 + 128], initial=0.0,
        op0=ALU.mult, op1=ALU.add))
    k.emit("vector", lambda e: e.tensor_add(
        GLS, G, rch[0:16, RC_LNB + c0:RC_LNB + c0 + 128]))
    k.emit("vector", lambda e: e.tensor_add(GLS, GLS, LNS16c))
    k.emit("vector", lambda e: e.scalar_tensor_tensor(
        out=NGS, in0=G, scalar=-1.0, in1=LNS16c, op0=ALU.mult, op1=ALU.add))
    k.emit("vector", lambda e: e.scalar_tensor_tensor(
        out=GRT, in0=G, scalar=-1.0, in1=LNS16c, op0=ALU.mult, op1=ALU.add))
    k.emit("vector", lambda e: e.tensor_scalar_add(GRT, GRT, G[0:16, 127:128]))
    k.emit("vector", lambda e: e.tensor_scalar_mul(
        rch[0:16, RC_GE:RC_GE + 128], rch[0:16, RC_ONES:RC_ONES + 128],
        G[0:16, 127:128]))

    def split3(src, p0, p1, p2):
        (b0, o0), (b1, o1), (b2, o2) = p0, p1, p2
        stage = nst[0:16, NS_CR:NS_CR + 128]
        k.emit("vector", lambda e: e.tensor_copy(stage, src))
        k.emit("vector", lambda e: e.tensor_sub(SP1, src, stage))
        k.emit("vector", lambda e, b0=b0, o0=o0: e.tensor_copy(
            nst[b0:b0 + 16, o0:o0 + 128], stage))
        k.emit("vector", lambda e: e.tensor_copy(stage, SP1))
        k.emit("vector", lambda e: e.tensor_sub(SP2, SP1, stage))
        k.emit("vector", lambda e, b1=b1, o1=o1: e.tensor_copy(
            nst[b1:b1 + 16, o1:o1 + 128], stage))
        return k.emit("vector", lambda e, b2=b2, o2=o2: e.tensor_copy(
            nst[b2:b2 + 16, o2:o2 + 128], SP2))

    split3(G, (0, NS_R), (0, NS_R + 128), (0, NS_R + 256))
    split3(GLS, (0, NS_R + 384), (0, NS_R + 512), (0, NS_R + 640))
    split3(NGS, (0, NS_R + 768), (0, NS_R + 896), (0, NS_R + 1024))
    split3(G, (0, NS_TB), (32, NS_TB), (64, NS_TB))
    split3(GRT, (0, NS_TB + 128), (32, NS_TB + 128), (64, NS_TB + 128))
    k.emit("vector", lambda e: e.tensor_copy(
        nst[0:16, NS_TB + 256:NS_TB + 384], rch[0:16, RC_LNS + c0:RC_LNS + c0 + 128]))
    k.emit("vector", lambda e: e.tensor_copy(
        nst[32:48, NS_TB + 256:NS_TB + 384], rch[0:16, RC_BETA + c0:RC_BETA + c0 + 128]))
    dv = k.emit("vector", lambda e: e.tensor_copy(
        nst[64:80, NS_TB + 256:NS_TB + 384], rch[0:16, RC_GE:RC_GE + 128]))

    k.wait("tensor", "dv", dv)
    for bi in range(3):
        k.emit("tensor", lambda e, bi=bi: e.transpose(
            pb[7][:, bi * 128:bi * 128 + 96],
            nst[0:96, NS_TB + bi * 128:NS_TB + bi * 128 + 128],
            s["ident"][0:96, 0:96]), "pe" if bi == 2 else None)
    k.wait("vector", "pe", k.sems["pe"])
    k.emit("vector", lambda e: e.tensor_copy(cB[:, 0:96], pb[7][:, 0:96]))
    k.emit("vector", lambda e: e.tensor_copy(cB[:, 96:192], pb[7][:, 128:224]))
    dv = k.emit("vector", lambda e: e.tensor_copy(cB[:, 192:288], pb[7][:, 256:352]))
    k.wait("tensor", "dv", dv)

    k.emit("vector", lambda e: e.tensor_copy(cA[:, 208:224], cB[:, 224:240]))
    k.emit("vector", lambda e: e.tensor_add(cA[:, 0:16], cB[:, 0:16], cB[:, 32:48]))
    k.emit("vector", lambda e: e.tensor_add(cA[:, 0:16], cA[:, 0:16], cB[:, 64:80]))
    k.emit("vector", lambda e: e.tensor_add(cA[:, 16:32], cB[:, 96:112], cB[:, 128:144]))
    k.emit("vector", lambda e: e.tensor_add(cA[:, 16:32], cA[:, 16:32], cB[:, 160:176]))
    lnskr = cB[:, 192:200].unsqueeze(2).broadcast_to([128, 8, 2])
    lnsqr = cB[:, 200:208].unsqueeze(2).broadcast_to([128, 8, 2])
    A16 = lambda o: cA[:, o:o + 16].rearrange("p (q r) -> p q r", q=8)
    k.emit("vector", lambda e: e.tensor_add(A16(32), A16(0), lnsqr))
    dv = k.emit("vector", lambda e: e.engine_nop())
    k.wait("scalar", "dv", dv)
    k.emit("scalar", lambda e: e.activation(out=cA[:, 48:64], in_=cA[:, 32:48],
                                            func=AF.Exp, bias=LNC))
    k.emit("scalar", lambda e: e.activation(
        out=cA[:, 64:80].rearrange("p (q r) -> p q r", q=8), in_=lnsqr,
        func=AF.Exp, bias=LNC))
    k.emit("scalar", lambda e: e.activation(out=cA[:, 80:96], in_=cA[:, 16:32],
                                            func=AF.Exp))
    k.emit("vector", lambda e: e.tensor_add(A16(96), A16(0), lnskr))
    dv = k.emit("vector", lambda e: e.engine_nop())
    k.wait("scalar", "dv", dv)
    ac = k.emit("scalar", lambda e: e.activation(out=cA[:, 112:128], in_=cA[:, 96:112],
                                                 func=AF.Exp))
    k.wait("vector", "ac", ac)
    k.emit("vector", lambda e: e.scalar_tensor_tensor(
        out=cA[:, 128:144], in0=cA[:, 112:128], scalar=-1.0, in1=cB[:, 224:240],
        op0=ALU.mult, op1=ALU.mult))
    acg = k.emit("scalar", lambda e: e.activation(out=cA[:, 144:160],
                                                  in_=cB[:, 256:272], func=AF.Exp))

    k.wait("tensor", "ac", acg)
    k.wait("tensor", "dv", k.sems["dv"])
    for kq in range(8):
        bank = pb[0] if kq < 4 else pb[5]
        lane = (kq % 4) * 128
        MM(bank[:, lane:lane + 128], mx(8 + kq), mx(8 + kq), True, True, pe=(kq == 7))
    for kq in range(8):
        bank = pb[1] if kq < 4 else pb[6]
        lane = (kq % 4) * 128
        MM(bank[:, lane:lane + 128], mx(8 + kq), mx(kq), True, True, pe=(kq == 7))
    for kq in range(8):
        k.emit("tensor", lambda e, kq=kq: e.transpose(
            pb[7][:, (kq % 4) * 128:(kq % 4) * 128 + 128], mx(8 + kq), s["ident"][:]),
            "pe" if kq % 4 == 3 else None)
        if kq % 4 == 3:
            k.wait("vector", "pe", k.sems["pe"])
            g0 = (kq // 4) * 4
            dv = k.emit("vector", lambda e, g0=g0: e.tensor_copy(
                big[:, BG_KT + g0 * 128:BG_KT + g0 * 128 + 512], pb[7][:, 0:512]))
            k.wait("tensor", "dv", dv)

    RT = lambda off: nst[0:16, off:off + 128]

    for g in range(2):
        hb = g * 8
        akk = pb[0] if g == 0 else pb[5]
        aqk = pb[1] if g == 0 else pb[6]
        k.wait("vector", "pe", k.sems["pe"])
        dv = k.emit("vector", lambda e, akk=akk: e.tensor_scalar_mul(
            big[:, BG_EA:BG_EA + 512], akk[:, 0:512], -1.0))
        k.wait("tensor", "dv", dv)

        for orient in range(2):
            for hh in range(8):
                h = hb + hh
                bank = pb[2] if hh < 4 else pb[3]
                lane = (hh % 4) * 128
                out = bank[:, lane:lane + 128]
                selh = s["sel"][:, h * 128:(h + 1) * 128]
                ra = NS_R + 384 if orient == 0 else NS_R + 768
                rb = NS_R + 768 if orient == 0 else NS_R + 384
                for t3 in range(3):
                    MM(out, RT(ra + t3 * 128), selh, t3 == 0, False)
                for t3 in range(3):
                    MM(out, selh, RT(rb + t3 * 128), False, False)
                mki = 0 if orient == 0 else 128
                MM(out, s["ident"][:], s["masks"][:, mki:mki + 128], False, True,
                   pe=(hh == 7))
            k.wait("scalar", "pe", k.sems["pe"])
            k.emit("scalar", lambda e: e.activation(out=nst[:, NS_E:NS_E + 512],
                                                    in_=pb[2][:, 0:512], func=AF.Exp))
            ac = k.emit("scalar", lambda e: e.activation(
                out=nst[:, NS_E + 512:NS_E + 1024], in_=pb[3][:, 0:512], func=AF.Exp))
            k.wait("vector", "ac", ac)
            dst = BG_XP if orient == 0 else BG_XT
            ear = big[:, BG_EA:BG_EA + 512].rearrange(
                "p (q t) -> p q t", q=4).unsqueeze(2).broadcast_to([128, 4, 2, 128])
            dv = k.emit("vector", lambda e, dst=dst, ear=ear: e.tensor_mul(
                big[:, dst:dst + 1024].rearrange("p (q r t) -> p q r t", q=4, r=2),
                nst[:, NS_E:NS_E + 1024].rearrange("p (q r t) -> p q r t", q=4, r=2),
                ear))
            k.wait("tensor", "dv", dv)

        for hh in range(8):
            h = hb + hh
            bank = pb[2] if hh < 4 else pb[3]
            lane = (hh % 4) * 128
            out = bank[:, lane:lane + 128]
            selh = s["sel"][:, h * 128:(h + 1) * 128]
            for t3 in range(3):
                MM(out, selh, RT(NS_R + t3 * 128), t3 == 0, False)
            for t3 in range(3):
                MM(out, RT(NS_R + 768 + t3 * 128), selh, False, False)
            MM(out, s["ident"][:], s["masks"][:, 256:384], False, True, pe=(hh == 7))
        k.wait("scalar", "pe", k.sems["pe"])
        k.emit("scalar", lambda e: e.activation(out=nst[:, NS_E:NS_E + 512],
                                                in_=pb[2][:, 0:512], func=AF.Exp))
        ac = k.emit("scalar", lambda e: e.activation(
            out=nst[:, NS_E + 512:NS_E + 1024], in_=pb[3][:, 0:512], func=AF.Exp))
        k.wait("vector", "ac", ac)
        aqr = aqk[:, 0:512].rearrange("p (q t) -> p q t", q=4).unsqueeze(2) \
            .broadcast_to([128, 4, 2, 128])
        dv = k.emit("vector", lambda e, aqr=aqr: e.tensor_mul(
            nst[:, NS_PT:NS_PT + 1024].rearrange("p (q r t) -> p q r t", q=4, r=2),
            nst[:, NS_E:NS_E + 1024].rearrange("p (q r t) -> p q r t", q=4, r=2),
            aqr))
        k.wait("tensor", "dv", dv)

        for qh in range(2):
            for i4 in range(4):
                hh = qh * 4 + i4
                h = hb + hh
                k.emit("tensor", lambda e, i4=i4, h=h: e.transpose(
                    pb[7][:, i4 * 128:i4 * 128 + 128], mx(16 + h),
                    s["ident"][:]), "pe" if i4 == 3 else None)
            k.wait("vector", "pe", k.sems["pe"])
            for i4 in range(4):
                h = hb + qh * 4 + i4
                k.emit("vector", lambda e, i4=i4, h=h: e.tensor_scalar_mul(
                    nst[:, NS_BV + i4 * 128:NS_BV + (i4 + 1) * 128],
                    pb[7][:, i4 * 128:i4 * 128 + 128], cA[:, 208 + h:209 + h]))
            dv = k.emit("vector", lambda e: e.engine_nop())
            k.wait("tensor", "dv", dv)
            for i4 in range(4):
                hh = qh * 4 + i4
                h = hb + hh
                kqt = mx(8 + g * 4 + hh // 2)
                MM(pb[4][:, i4 * 128:i4 * 128 + 128], kqt,
                   big[:, BG_SH + h * 128:BG_SH + (h + 1) * 128], True, False)
                MM(pb[4][:, i4 * 128:i4 * 128 + 128], kqt,
                   big[:, BG_SL + h * 128:BG_SL + (h + 1) * 128], False, True,
                   pe=(i4 == 3))
            k.wait("vector", "pe", k.sems["pe"])
            for i4 in range(4):
                hh = qh * 4 + i4
                h = hb + hh
                k.emit("vector", lambda e, i4=i4, hh=hh, h=h: e.scalar_tensor_tensor(
                    out=big[:, BG_UU + hh * 128:BG_UU + (hh + 1) * 128],
                    in0=pb[4][:, i4 * 128:i4 * 128 + 128],
                    scalar=cA[:, 128 + h:129 + h],
                    in1=nst[:, NS_BV + i4 * 128:NS_BV + (i4 + 1) * 128],
                    op0=ALU.mult, op1=ALU.add))
            dv = k.emit("vector", lambda e: e.engine_nop())
            k.wait("tensor", "dv", dv)

        for j in range(7):
            for hh in range(8):
                bank = pb[2] if hh < 4 else pb[3]
                lane = (hh % 4) * 128
                MM(bank[:, lane:lane + 128],
                   big[:, BG_XT + hh * 128:BG_XT + (hh + 1) * 128],
                   big[:, BG_UU + hh * 128:BG_UU + (hh + 1) * 128], True, False)
                MM(bank[:, lane:lane + 128], s["ident"][:],
                   big[:, BG_UU + hh * 128:BG_UU + (hh + 1) * 128], False, True,
                   pe=(hh == 7))
            k.wait("vector", "pe", k.sems["pe"])
            k.emit("vector", lambda e: e.tensor_copy(
                big[:, BG_UU:BG_UU + 512], pb[2][:, 0:512]))
            dv = k.emit("vector", lambda e: e.tensor_copy(
                big[:, BG_UU + 512:BG_UU + 1024], pb[3][:, 0:512]))
            k.wait("tensor", "dv", dv)
            if j < 6:
                for hh in range(8):
                    bank = pb[4] if hh < 4 else pb[0]
                    lane = (hh % 4) * 128
                    MM(bank[:, lane:lane + 128],
                       big[:, BG_XT + hh * 128:BG_XT + (hh + 1) * 128],
                       big[:, BG_XP + hh * 128:BG_XP + (hh + 1) * 128], True, True,
                       pe=(hh == 7))
                for hh in range(8):
                    bank = pb[2] if hh < 4 else pb[3]
                    lane = (hh % 4) * 128
                    MM(bank[:, lane:lane + 128],
                       big[:, BG_XP + hh * 128:BG_XP + (hh + 1) * 128],
                       big[:, BG_XT + hh * 128:BG_XT + (hh + 1) * 128], True, True,
                       pe=(hh == 7))
                k.wait("vector", "pe", k.sems["pe"])
                k.emit("vector", lambda e: e.tensor_copy(
                    big[:, BG_XP:BG_XP + 512], pb[4][:, 0:512]))
                k.emit("vector", lambda e: e.tensor_copy(
                    big[:, BG_XP + 512:BG_XP + 1024], pb[0][:, 0:512]))
                k.emit("vector", lambda e: e.tensor_copy(
                    big[:, BG_XT:BG_XT + 512], pb[2][:, 0:512]))
                dv = k.emit("vector", lambda e: e.tensor_copy(
                    big[:, BG_XT + 512:BG_XT + 1024], pb[3][:, 0:512]))
                k.wait("tensor", "dv", dv)

        for hh in range(8):
            h = hb + hh
            lane = (hh % 4) * 128
            vsl = BG_VT + (hh % 2) * 128
            MM(pb[2][:, lane:lane + 128], nst[:, NS_PT + hh * 128:NS_PT + (hh + 1) * 128],
               big[:, BG_UU + hh * 128:BG_UU + (hh + 1) * 128], True, True)
            MM(pb[3][:, lane:lane + 128], mx(g * 4 + hh // 2),
               big[:, BG_SH + h * 128:BG_SH + (h + 1) * 128], True, False)
            MM(pb[3][:, lane:lane + 128], mx(g * 4 + hh // 2),
               big[:, BG_SL + h * 128:BG_SL + (h + 1) * 128], False, True, pe=True)
            k.wait("vector", "pe", k.sems["pe"])
            k.emit("vector", lambda e, lane=lane, h=h: e.tensor_scalar_mul(
                s["f32s"][:, lane:lane + 128], pb[3][:, lane:lane + 128],
                cA[:, 48 + h:49 + h]))
            dv = k.emit("vector", lambda e, lane=lane, h=h, vsl=vsl: e.scalar_tensor_tensor(
                out=big[:, vsl:vsl + 128], in0=pb[2][:, lane:lane + 128],
                scalar=cA[:, 64 + h:65 + h], in1=s["f32s"][:, lane:lane + 128],
                op0=ALU.mult, op1=ALU.add))
            k.wait("scalar", "dv", dv)
            ac = k.emit("scalar", lambda e, vsl=vsl, h=h, lane=lane: e.activation(
                out=s["f32s"][:, lane:lane + 128], in_=big[:, vsl:vsl + 128],
                func=AF.Square, accum_out=cA[:, 160 + h:161 + h]))
            k.wait("scalar", "ac", ac)
            k.emit("scalar", lambda e, h=h: e.activation(
                out=cA[:, 176 + h:177 + h], in_=cA[:, 160 + h:161 + h], func=AF.Ln,
                scale=0.0078125, bias=1e-6))
            ac = k.emit("scalar", lambda e, h=h: e.activation(
                out=cA[:, 176 + h:177 + h], in_=cA[:, 176 + h:177 + h], func=AF.Exp,
                scale=-0.5))
            k.wait("vector", "ac", ac)
            osl = BG_VT + ((hh % 2) ^ 1) * 128
            dv = k.emit("vector", lambda e, vsl=vsl, osl=osl, h=h: e.tensor_scalar_mul(
                big[:, osl:osl + 128], big[:, vsl:vsl + 128], cA[:, 176 + h:177 + h]))
            k.wait("tensor", "dv", dv)
            k.emit("tensor", lambda e, osl=osl, lane=lane: e.transpose(
                pb[7][:, lane:lane + 128], big[:, osl:osl + 128], s["ident"][:]),
                "pe")
            k.wait("vector", "pe", k.sems["pe"])
            dv = k.emit("vector", lambda e, lane=lane, h=h: e.tensor_mul(
                big[:, BG_CO + h * 128:BG_CO + (h + 1) * 128],
                pb[7][:, lane:lane + 128], mx(32 + h)))
            k.emit("vector", lambda e, h=h: e.tensor_scalar_mul(
                nst[:, NS_KP:NS_KP + 128],
                big[:, BG_KT + (h // 2) * 128:BG_KT + (h // 2) * 128 + 128],
                cA[:, 80 + h:81 + h]))
            dv = k.emit("vector", lambda e, h=h: e.tensor_scalar_mul(
                nst[:, NS_KP + 128:NS_KP + 256], s["ident"][:], cA[:, 144 + h:145 + h]))
            k.wait("tensor", "dv", dv)
            MM(pb[4][:, lane:lane + 128], nst[:, NS_KP:NS_KP + 128],
               big[:, BG_UU + hh * 128:BG_UU + (hh + 1) * 128], True, False)
            MM(pb[4][:, lane:lane + 128], nst[:, NS_KP + 128:NS_KP + 256],
               big[:, BG_SH + h * 128:BG_SH + (h + 1) * 128], False, False)
            MM(pb[4][:, lane:lane + 128], nst[:, NS_KP + 128:NS_KP + 256],
               big[:, BG_SL + h * 128:BG_SL + (h + 1) * 128], False, True, pe=True)
            k.wait("vector", "pe", k.sems["pe"])
            k.emit("vector", lambda e, lane=lane, h=h: e.tensor_copy(
                big[:, BG_SH + h * 128:BG_SH + (h + 1) * 128], pb[4][:, lane:lane + 128]))
            dv = k.emit("vector", lambda e, lane=lane, h=h: e.tensor_sub(
                big[:, BG_SL + h * 128:BG_SL + (h + 1) * 128],
                pb[4][:, lane:lane + 128],
                big[:, BG_SH + h * 128:BG_SH + (h + 1) * 128]))
            k.wait("tensor", "dv", dv)


def phase_wout(k, featname, wname, nvt, pollflag, outflag):
    cfg = k.cfg
    s, d = k.sbuf, k.dram
    pb = [s[f"pb{i}"] for i in range(8)]
    big = s["bigb"]
    nw = cfg.TCH
    feat = dview(d[featname])
    wo = d[wname].ap().rearrange("(v p) d -> p v d", p=128)
    shn = dview(d["sh_n1"])
    shr = dview(d["sh_nrs"])

    def MM(out, lhsT, rhs, start, stop, pe=False):
        return k.emit("tensor", lambda e, out=out, lhsT=lhsT, rhs=rhs,
                      start=start, stop=stop:
                      e.matmul(out, lhsT, rhs, start=start, stop=stop),
                      "pe" if pe else None)

    k.flag_poll(pollflag, "xs")
    k.wait("sync", "xs", k.sems["xs"])

    for co in range(cfg.nTCo):
        k.wait("sync", "rwb", k.sems.get("rwb", 0))
        lr = k.dma("sync", s["rch"][:].rearrange("p (n w) -> p n w", n=16),
                   dview(d["resid"])[:, :, co * nw:(co + 1) * nw], sem="lnld")
        fa = k.pdma("sync", [(
            big[:, 0:nvt * 512].rearrange("p (n w) -> p n w", n=nvt),
            feat[:, :, co * nw:(co + 1) * nw],
            big[:, 0:nvt * 512].rearrange("p (n w) -> p n w", n=nvt),
            feat[:, :, cfg.To + co * nw:cfg.To + (co + 1) * nw],
        )], sem="fld")
        k.wait("tensor", "fld", fa)
        k.wait("vector", "lnld", lr)
        pes_dt = {}
        for dt_ in range(cfg.nD):
            wsl = dt_ % 3
            wq_ = ("sync", "scalar")[wsl % 2]
            if dt_ >= 3:
                k.wait(wq_, "pe", pes_dt[dt_ - 3])
            elif co > 0 and dt_ <= 1:
                k.wait(wq_, "pe", k.sems["pe"])
            lw = k.dma(wq_,
                       s["wbuf"][:, wsl * 4096:wsl * 4096 + nvt * 128].rearrange(
                           "p (v c) -> p v c", v=nvt),
                       wo[:, :, dt_ * 128:(dt_ + 1) * 128], sem=f"wo{wsl}")
            k.wait("tensor", f"wo{wsl}", lw)
            pbank = pb[dt_ % 2]
            if dt_ >= 2:
                k.wait("tensor", "dv", k.sems["dv"])
            for vt in range(nvt):
                pes_dt[dt_] = MM(
                    pbank[:, 0:512],
                    s["wbuf"][:, wsl * 4096 + vt * 128:wsl * 4096 + (vt + 1) * 128],
                    big[:, vt * 512:(vt + 1) * 512], vt == 0, vt == nvt - 1,
                    pe=(vt == nvt - 1))
            k.wait("vector", "pe", k.sems["pe"])
            k.emit("vector", lambda e, dt_=dt_, pbank=pbank: e.tensor_add(
                s["rch"][:, dt_ * nw:(dt_ + 1) * nw],
                s["rch"][:, dt_ * nw:(dt_ + 1) * nw], pbank[:, 0:512]))
        dv = k.emit("vector", lambda e: e.engine_nop())
        k.wait("scalar", "dv", dv)
        k.wait("tensor", "dv", dv)
        rw = k.dma("scalar", dview(d["resid"])[:, :, co * nw:(co + 1) * nw],
                   s["rch"][:].rearrange("p (n w) -> p n w", n=16), sem="rwb")
        dv = emit_ln_chunk(k, nw, with_rs=True)
        k.wait("gpsimd", "dv", dv)
        st = k.pdma("gpsimd", [(
            shn[:, :, co * nw:(co + 1) * nw],
            s["nst"][:].rearrange("p (n w) -> p n w", n=16),
            shn[:, :, cfg.To + co * nw:cfg.To + (co + 1) * nw],
            s["nst"][:].rearrange("p (n w) -> p n w", n=16),
        )], sem="nsh")
        dv = emit_nrs_tiles(k, nw, lambda t: big[:, 16384 + t * nw:16384 + (t + 1) * nw])
        k.wait("gpsimd", "dv", dv)
        st = k.pdma("gpsimd", [(
            shr[:, :, co * nw:(co + 1) * nw],
            big[:, 16384:24576].rearrange("p (n w) -> p n w", n=16),
            shr[:, :, cfg.To + co * nw:cfg.To + (co + 1) * nw],
            big[:, 16384:24576].rearrange("p (n w) -> p n w", n=16),
        )], sem="nsh")
        k.wait("vector", "nsh", st)
        k.wait("sync", "nsh", st)
    k.wait("sync", "rwb", k.sems["rwb"])
    k.wait("gpsimd", "nsh", k.sems["nsh"])
    k.flag_write(outflag)




def phase_ffn(k, ci):
    cfg = k.cfg
    s, d = k.sbuf, k.dram
    pb = [s[f"pb{i}"] for i in range(8)]
    big, nst, rch = s["bigb"], s["nst"], s["rch"]
    T, To, nD = cfg.T, cfg.To, cfg.nD
    nQC = T // 512
    KR, VR = 0, 2 * T
    QS, PS, OS, QI = 4 * T, 4 * T + 1024, 4 * T + 2048, 4 * T + 2560
    RSQ = float(128 ** -0.5)

    def MM(out, lhsT, rhs, start, stop, pe=False):
        return k.emit("tensor", lambda e, out=out, lhsT=lhsT, rhs=rhs,
                      start=start, stop=stop:
                      e.matmul(out, lhsT, rhs, start=start, stop=stop),
                      "pe" if pe else None)

    woav = big[:, 28672:36864].bitcast(F32)
    shr = dview(d["sh_nrs"])
    shn1 = dview(d["sh_n1"])
    wqv, wkv, wvv = dview(d["fw_q"]), dview(d["fw_k"]), dview(d["fw_v"])
    qsc = dview(d["qT_scr"])
    woa_rows = d["fw_woa"].ap()

    if ci == 0:
        k.tap("dp1", s["bdpc"][:])
    k.flag_poll(f"n1_f{ci + 1}", "xs")
    k.wait("sync", "xs", k.sems["xs"])
    k.emit("vector", lambda e: e.memset(woav[0:16, 0:T], 0.0))

    for r in range(4):
        k.wait("sync", "pe", k.sems["pe"])
        lw = k.dma("sync", s["wbuf"][:, 0:4096].rearrange("p (n c) -> p n c", n=16),
                   wqv[:, :, r * 256:(r + 1) * 256], sem="wa0")
        lw = k.dma("sync", s["wbuf"][:, 4096:8192].rearrange("p (n c) -> p n c", n=16),
                   wkv[:, :, r * 256:(r + 1) * 256], sem="wa0")
        lw = k.dma("sync", s["wbuf"][:, 8192:12288].rearrange("p (n c) -> p n c", n=16),
                   wvv[:, :, r * 256:(r + 1) * 256], sem="wa0")
        cbv = s["colsB"][:, 0:32].rearrange("p (n c) -> p n c", n=2)
        wrv = woa_rows.rearrange("(n p) c -> p n c", p=128)
        fa = k.pdma("sync", [(
            cbv, wrv[:, 2 * r:2 * r + 2, :],
            cbv, wrv[:, 8 + 2 * r:8 + 2 * r + 2, :])], sem="fld")
        k.wait("tensor", "wa0", lw)
        k.wait("tensor", "fld", fa)
        k.emit("vector", lambda e: e.memset(s["rows"][0:1, 16:20], 0.0))

        pe_tc = {}
        for tc in range(cfg.nTCf):
            slot = tc % 2
            nview = s["nch"][:, slot * 8192: slot * 8192 + 8192].rearrange(
                "p (n w) -> p n w", n=16)
            if tc >= 2:
                k.wait("sync", "pe", pe_tc[tc - 2])
            ln = k.dma("sync", nview, shr[:, :, tc * 512:(tc + 1) * 512],
                       sem=f"na{slot}")
            k.wait("tensor", f"na{slot}", ln)
            k.wait("tensor", "dv", k.sems["dv"])
            k.wait("tensor", "ac", k.sems["ac"])
            for h2 in range(2):
                for dt_ in range(nD):
                    MM(pb[h2][:, 0:512],
                       s["wbuf"][:, 4096 + dt_ * 256 + h2 * 128:
                                 4096 + dt_ * 256 + (h2 + 1) * 128],
                       nview[:, dt_, :], dt_ == 0, dt_ == nD - 1, pe=(dt_ == nD - 1))
                k.wait("vector", "pe", k.sems["pe"])
                k.emit("vector", lambda e, h2=h2, tc=tc: e.tensor_copy(
                    big[:, KR + h2 * T + tc * 512: KR + h2 * T + (tc + 1) * 512],
                    pb[h2][:, 0:512]))
                for dt_ in range(nD):
                    MM(pb[2 + h2][:, 0:512],
                       s["wbuf"][:, dt_ * 256 + h2 * 128: dt_ * 256 + (h2 + 1) * 128],
                       nview[:, dt_, :], dt_ == 0, dt_ == nD - 1, pe=(dt_ == nD - 1))
                k.wait("vector", "pe", k.sems["pe"])
                k.wait("vector", f"qs{h2}", k.sems.get(f"qs{h2}", 0))
                dv = k.emit("vector", lambda e, h2=h2: e.tensor_copy(
                    big[:, QS + h2 * 512: QS + (h2 + 1) * 512], pb[2 + h2][:, 0:512]))
                k.wait("scalar", "dv", dv)
                ac = k.emit("scalar", lambda e, h2=h2: e.activation(
                    out=nst[:, 0:512], in_=big[:, QS + h2 * 512: QS + (h2 + 1) * 512],
                    func=AF.Square))
                k.wait("tensor", "ac", ac)
                pe = MM(pb[6][0:1, 0:512], s["ones"][:, 0:1], nst[:, 0:512],
                        True, True, pe=True)
                k.wait("vector", "pe", pe)
                k.emit("vector", lambda e: e.reduce_max(
                    s["rows"][0:1, 24:25], pb[6][0:1, 0:512], AX.X))
                k.emit("vector", lambda e, h2=h2: e.tensor_max(
                    s["rows"][0:1, 16 + h2:17 + h2], s["rows"][0:1, 16 + h2:17 + h2],
                    s["rows"][0:1, 24:25]))
                k.wait("gpsimd", "dv", dv)
                k.dma("gpsimd", qsc[:, 2 * r + h2, tc * 512:(tc + 1) * 512],
                      big[:, QS + h2 * 512: QS + (h2 + 1) * 512], sem=f"qs{h2}")
            for vp in range(2):
                bank = pb[4 + vp]
                for vt in (2 * vp, 2 * vp + 1):
                    off = (vt % 2) * 256
                    for dt_ in range(nD):
                        MM(bank[:, off:off + 256],
                           nview[:, dt_, vt * 128:(vt + 1) * 128],
                           s["wbuf"][:, 8192 + dt_ * 256: 8192 + (dt_ + 1) * 256],
                           dt_ == 0, dt_ == nD - 1,
                           pe=(dt_ == nD - 1 and vt == 2 * vp + 1))
                k.wait("vector", "pe", k.sems["pe"])
                for vt in (2 * vp, 2 * vp + 1):
                    off = (vt % 2) * 256
                    for h2 in range(2):
                        k.emit("vector",
                               lambda e, h2=h2, vt=vt, tc=tc, bank=bank, off=off:
                               e.tensor_copy(
                                   big[:, VR + h2 * T + (tc * 4 + vt) * 128:
                                       VR + h2 * T + (tc * 4 + vt + 1) * 128],
                                   bank[:, off + h2 * 128: off + (h2 + 1) * 128]))
            pe_tc[tc] = k.sems["pe"]

        for h2 in range(2):
            for tc in range(cfg.nTCf):
                k.wait("scalar", "dv", k.sems["dv"])
                ac = k.emit("scalar", lambda e, h2=h2, tc=tc: e.activation(
                    out=nst[:, 0:512],
                    in_=big[:, KR + h2 * T + tc * 512: KR + h2 * T + (tc + 1) * 512],
                    func=AF.Square))
                k.wait("tensor", "ac", ac)
                pe = MM(pb[6][0:1, 0:512], s["ones"][:, 0:1], nst[:, 0:512],
                        True, True, pe=True)
                k.wait("vector", "pe", pe)
                k.emit("vector", lambda e: e.reduce_max(
                    s["rows"][0:1, 24:25], pb[6][0:1, 0:512], AX.X))
                k.emit("vector", lambda e, h2=h2: e.tensor_max(
                    s["rows"][0:1, 18 + h2:19 + h2], s["rows"][0:1, 18 + h2:19 + h2],
                    s["rows"][0:1, 24:25]))
        for h2 in range(2):
            dv = k.emit("vector", lambda e, h2=h2: e.tensor_mul(
                s["rows"][0:1, 24:25], s["rows"][0:1, 16 + h2:17 + h2],
                s["rows"][0:1, 18 + h2:19 + h2]))
            k.wait("scalar", "dv", dv)
            k.emit("scalar", lambda e: e.activation(
                out=s["rows"][0:1, 24:25], in_=s["rows"][0:1, 24:25], func=AF.Ln))
            ac = k.emit("scalar", lambda e: e.activation(
                out=s["rows"][0:1, 24:25], in_=s["rows"][0:1, 24:25], func=AF.Exp,
                scale=0.5))
            k.wait("vector", "ac", ac)
            dv = k.emit("vector", lambda e: e.tensor_copy(
                s["rows2"][0:1, 0:1], s["rows"][0:1, 24:25]))
            k.wait("tensor", "dv", dv)
            pe = MM(pb[6][:, 0:1], s["ones"][0:1, :], s["rows2"][0:1, 0:1],
                    True, True, pe=True)
            k.wait("vector", "pe", pe)
            k.emit("vector", lambda e, h2=h2: e.tensor_scalar_mul(
                s["colsA"][:, h2:h2 + 1], pb[6][:, 0:1], -RSQ))

        k.wait("sync", "qs0", k.sems.get("qs0", 0))
        k.wait("sync", "qs1", k.sems.get("qs1", 0))
        for qc in range(nQC):
            nkt = 4 * qc + 4
            for h2 in range(2):
                k.wait("sync", "pe", k.sems["pe"])
                lq = k.dma("sync", big[:, QI + h2 * 512: QI + (h2 + 1) * 512],
                           qsc[:, 2 * r + h2, qc * 512:(qc + 1) * 512], sem=f"qi{h2}")
                k.wait("tensor", f"qi{h2}", lq)
                k.wait("tensor", "dv", k.sems["dv"])
                acq = {}
                pes = {}
                for kt in range(nkt):
                    bank = pb[kt % 2]
                    if kt >= 2:
                        k.wait("tensor", "ac", acq[kt - 2])
                    r_off = kt - 4 * qc
                    if r_off >= 0:
                        MM(bank[:, 0:512], big[:, KR + h2 * T + kt * 128:
                                               KR + h2 * T + (kt + 1) * 128],
                           big[:, QI + h2 * 512: QI + (h2 + 1) * 512], True, False)
                        pes[kt] = MM(bank[:, 0:512], s["ident"][:],
                                     s["cmask"][:, 384 - r_off * 128:896 - r_off * 128],
                                     False, True, pe=True)
                    else:
                        pes[kt] = MM(bank[:, 0:512], big[:, KR + h2 * T + kt * 128:
                                                         KR + h2 * T + (kt + 1) * 128],
                                     big[:, QI + h2 * 512: QI + (h2 + 1) * 512],
                                     True, True, pe=True)
                    if kt >= 1:
                        k.wait("tensor", "ac", acq[kt - 1])
                        MM(pb[3][0:1, 0:512], s["ones"][:, 0:1],
                           big[:, PS + ((kt - 1) % 2) * 512: PS + ((kt - 1) % 2 + 1) * 512],
                           kt - 1 == 0, False)
                        MM(pb[2][:, 0:512], big[:, VR + h2 * T + (kt - 1) * 128:
                                                VR + h2 * T + kt * 128],
                           big[:, PS + ((kt - 1) % 2) * 512: PS + ((kt - 1) % 2 + 1) * 512],
                           kt - 1 == 0, False)
                    k.wait("scalar", "pe", pes[kt])
                    acq[kt] = k.emit("scalar", lambda e, kt=kt, bank=bank, h2=h2:
                                     e.activation(
                                         out=big[:, PS + (kt % 2) * 512:
                                                 PS + (kt % 2 + 1) * 512],
                                         in_=bank[:, 0:512], func=AF.Exp,
                                         bias=s["colsA"][:, h2:h2 + 1], scale=RSQ))
                kt = nkt - 1
                k.wait("tensor", "ac", acq[kt])
                MM(pb[3][0:1, 0:512], s["ones"][:, 0:1],
                   big[:, PS + (kt % 2) * 512: PS + (kt % 2 + 1) * 512],
                   kt == 0, True)
                pvs = MM(pb[2][:, 0:512], big[:, VR + h2 * T + kt * 128:
                                              VR + h2 * T + (kt + 1) * 128],
                         big[:, PS + (kt % 2) * 512: PS + (kt % 2 + 1) * 512],
                         kt == 0, True, pe=True)
                k.wait("vector", "pe", pvs)
                dv = k.emit("vector", lambda e: e.reciprocal(
                    s["rows"][0:1, 512:1024], pb[3][0:1, 0:512]))
                k.emit("vector", lambda e: e.tensor_copy(
                    s["rows2"][0:1, 0:512], s["rows"][0:1, 512:1024]))
                dv = k.emit("vector", lambda e: e.engine_nop())
                k.wait("tensor", "dv", dv)
                pe = MM(pb[5][:, 0:512], s["ones"][0:1, :], s["rows2"][0:1, 0:512],
                        True, True, pe=True)
                k.wait("vector", "pe", pe)
                k.emit("vector", lambda e: e.tensor_copy(s["f32s"][:, 0:512],
                                                         pb[5][:, 0:512]))
                dv = k.emit("vector", lambda e: e.tensor_mul(
                    big[:, OS:OS + 512], pb[2][:, 0:512], s["f32s"][:, 0:512]))
                k.wait("tensor", "dv", dv)
                was = MM(pb[4][0:16, 0:512], s["colsB"][:, h2 * 16:(h2 + 1) * 16],
                         big[:, OS:OS + 512], h2 == 0, h2 == 1, pe=(h2 == 1))
                if h2 == 1:
                    k.wait("vector", "pe", was)
                    k.emit("vector", lambda e, qc=qc: e.tensor_add(
                        woav[0:16, qc * 512:(qc + 1) * 512],
                        woav[0:16, qc * 512:(qc + 1) * 512], pb[4][0:16, 0:512]))

    k.wait("gpsimd", "dv", k.sems["dv"])
    shw = d["sh_woa"].ap()
    st = k.pdma("gpsimd", [(
        shw[0:1, :].rearrange("o (r t) -> (o r) t", r=16), woav[0:16, 0:T],
        shw[1:2, :].rearrange("o (r t) -> (o r) t", r=16), woav[0:16, 0:T])],
        sem="fsh")
    k.wait("gpsimd", "fsh", st)
    k.flag_write(f"woa{ci}")
    k.flag_poll(f"woa{ci}", "xs")
    k.wait("gpsimd", "xs", k.sems["xs"])
    w0 = shw[0:1, :].rearrange("o (r t) -> (o r) t", r=16)
    w1 = shw[1:2, :].rearrange("o (r t) -> (o r) t", r=16)
    k.wait("gpsimd", "rwb", k.sems.get("rwb", 0))
    ld = k.pdma("gpsimd", [
        (rch[0:16, 0:To], w0[:, 0:To], rch[0:16, 0:To], w0[:, To:T]),
        (rch[0:16, 8192 - To:8192], w1[:, 0:To], rch[0:16, 8192 - To:8192], w1[:, To:T]),
    ], sem="gld")
    k.wait("vector", "gld", ld)
    dv = k.emit("vector", lambda e: e.tensor_add(
        rch[0:16, 0:To], rch[0:16, 0:To], rch[0:16, 8192 - To:8192]))
    k.wait("scalar", "dv", dv)
    ac = k.emit("scalar", lambda e: e.activation(
        out=rch[0:16, 8192 - To:8192], in_=rch[0:16, 0:To], func=AF.Tanh))
    k.wait("vector", "ac", ac)
    dvm = k.emit("vector", lambda e: e.tensor_scalar_mul(
        s["sel"][0:16, 0:To], rch[0:16, 8192 - To:8192], 0.1))
    if ci == 0:
        k.tap("mod", s["sel"][0:16, 0:To])

    k.wait("sync", "pe", k.sems["pe"])
    la = k.dma("sync", s["colsB"][:, 32:288].rearrange("p (n c) -> p n c", n=16),
               dview(d["fw_A"]), sem="wa0")
    wiv = dview(d["fw_in"])
    wuv = dview(d["fw_up"])
    wdv = d["fw_down"].ap().rearrange("(ft p) c -> p ft c", p=128)
    resv = dview(d["resid"])
    shn0 = dview(d["sh_n0"])
    for co in range(cfg.nTCo):
        nchv = s["nch"][:, 0:8192].rearrange("p (n w) -> p n w", n=16)
        if co == 0:
            k.wait("sync", "dv", dvm)
        k.wait("sync", "nsh", k.sems.get("nsh", 0))
        k.wait("sync", "rwb", k.sems.get("rwb", 0))
        k.wait("sync", "oput", k.sems.get("oput", 0))
        k.wait("sync", "pe", k.sems["pe"])
        k.wait("scalar", "pe", k.sems["pe"])
        ln = k.pdma("sync", [(
            nchv, shr[:, :, co * 512:(co + 1) * 512],
            nchv, shr[:, :, To + co * 512:To + (co + 1) * 512])], sem="na0")
        lr = k.dma("sync", rch[:].rearrange("p (n w) -> p n w", n=16),
                   resv[:, :, co * 512:(co + 1) * 512], sem="lnld")
        lb = k.dma("sync", nst[0:16, 0:8192], d["fw_B"].ap(), sem="wba")
        k.wait("tensor", "na0", ln)
        k.wait("vector", "lnld", lr)
        k.wait("tensor", "wa0", k.sems["wa0"])
        k.wait("tensor", "dv", k.sems["dv"])
        k.wait("tensor", "ac", k.sems["ac"])
        for dt_ in range(nD):
            MM(pb[0][0:16, 0:512], s["colsB"][:, 32 + dt_ * 16: 32 + (dt_ + 1) * 16],
               nchv[:, dt_, :], dt_ == 0, dt_ == nD - 1, pe=(dt_ == nD - 1))
        k.wait("vector", "pe", k.sems["pe"])
        dvh = k.emit("vector", lambda e, co=co: e.tensor_mul(
            s["wbuf"][0:16, 9216:9728], pb[0][0:16, 0:512],
            s["sel"][0:16, co * 512:(co + 1) * 512]))
        k.wait("tensor", "dv", dvh)
        k.wait("tensor", "wba", lb)
        ld0 = k.dma("sync", s["nch"][:, 8192:16384].rearrange("p (n c) -> p n c", n=64),
                    wdv[:, :, 0:128], sem="ws0")
        ac_silu = {}
        dv_gu = {}
        pe_up = {}
        for f in range(64):
            sl = f % 2
            wq_ = ("sync", "scalar")[sl]
            if f >= 2:
                k.wait(wq_, "pe", pe_up[f - 2])
            lw = k.dma(wq_, s["wbuf"][:, sl * 4096: sl * 4096 + 2048]
                       .rearrange("p (n c) -> p n c", n=16),
                       wiv[:, :, f * 128:(f + 1) * 128], sem=f"wm{sl}")
            lw = k.dma(wq_, s["wbuf"][:, sl * 4096 + 2048: sl * 4096 + 4096]
                       .rearrange("p (n c) -> p n c", n=16),
                       wuv[:, :, f * 128:(f + 1) * 128], sem=f"wm{sl}")
            k.wait("tensor", f"wm{sl}", lw)
            gbank = pb[1 + sl]
            if f >= 2:
                k.wait("tensor", "ac", ac_silu[f - 2])
            for dt_ in range(nD):
                MM(gbank[:, 0:512], s["wbuf"][:, sl * 4096 + dt_ * 128:
                                              sl * 4096 + (dt_ + 1) * 128],
                   nchv[:, dt_, :], dt_ == 0, False)
            peg = MM(gbank[:, 0:512], nst[0:16, f * 128:(f + 1) * 128],
                     s["wbuf"][0:16, 9216:9728], False, True, pe=True)
            k.wait("scalar", "pe", peg)
            if f >= 2:
                k.wait("scalar", "dv", dv_gu[f - 2])
            ac_silu[f] = k.emit("scalar", lambda e, sl=sl, gbank=gbank: e.activation(
                out=s["wbuf"][:, 8192 + sl * 512: 8192 + (sl + 1) * 512],
                in_=gbank[:, 0:512], func=AF.Silu))
            if f >= 1:
                k.wait("tensor", "dv", dv_gu[f - 1])
            for dt_ in range(nD):
                MM(pb[3][:, 0:512], s["wbuf"][:, sl * 4096 + 2048 + dt_ * 128:
                                              sl * 4096 + 2048 + (dt_ + 1) * 128],
                   nchv[:, dt_, :], dt_ == 0, dt_ == nD - 1, pe=(dt_ == nD - 1))
            pe_up[f] = k.sems["pe"]
            k.wait("vector", "pe", pe_up[f])
            k.wait("vector", "ac", ac_silu[f])
            dv_gu[f] = k.emit("vector", lambda e, f=f, sl=sl: e.tensor_mul(
                big[:, f * 512:(f + 1) * 512], pb[3][:, 0:512],
                s["wbuf"][:, 8192 + sl * 512: 8192 + (sl + 1) * 512]))
        k.wait("gpsimd", "pe", pe_up[63])
        lnr = k.pdma("gpsimd", [(
            nst[:].rearrange("p (n w) -> p n w", n=16),
            shn1[:, :, co * 512:(co + 1) * 512],
            nst[:].rearrange("p (n w) -> p n w", n=16),
            shn1[:, :, To + co * 512:To + (co + 1) * 512])], sem="gld")
        k.wait("tensor", "dv", dv_gu[63])
        pe_dn = {}
        for dt_ in range(nD):
            ssl = dt_ % 2
            swq = ("sync", "scalar")[ssl]
            if dt_ >= 1:
                if dt_ == 1:
                    k.wait(swq, "pe", pe_up[63])
                else:
                    k.wait(swq, "pe", pe_dn[dt_ - 2])
                stv = (s["nch"][:, 8192:16384] if ssl == 0 else s["wbuf"][:, 0:8192])
                lw = k.dma(swq, stv.rearrange("p (n c) -> p n c", n=64),
                           wdv[:, :, dt_ * 128:(dt_ + 1) * 128], sem=f"ws{ssl}")
            stripe = s["nch"][:, 8192:16384] if ssl == 0 else s["wbuf"][:, 0:8192]
            k.wait("tensor", f"ws{ssl}", k.sems[f"ws{ssl}"])
            bank = pb[4 + ssl]
            if dt_ >= 2:
                k.wait("tensor", "dv", k.sems["dv"])
            for ft in range(64):
                MM(bank[:, 0:512], stripe[:, ft * 128:(ft + 1) * 128],
                   big[:, ft * 512:(ft + 1) * 512], ft == 0, ft == 63, pe=(ft == 63))
            pe_dn[dt_] = k.sems["pe"]
            k.wait("vector", "pe", pe_dn[dt_])
            k.emit("vector", lambda e, dt_=dt_, bank=bank: e.tensor_add(
                rch[:, dt_ * 512:(dt_ + 1) * 512], rch[:, dt_ * 512:(dt_ + 1) * 512],
                bank[:, 0:512]))
        k.wait("vector", "gld", lnr)
        for dt_ in range(nD):
            MMdv = k.emit("vector", lambda e, dt_=dt_: e.tensor_add(
                rch[:, dt_ * 512:(dt_ + 1) * 512], rch[:, dt_ * 512:(dt_ + 1) * 512],
                nst[:, dt_ * 512:(dt_ + 1) * 512]))
        dv = k.emit("vector", lambda e: e.engine_nop())
        k.wait("scalar", "dv", dv)
        k.wait("tensor", "dv", dv)
        if ci == 0:
            rw = k.dma("scalar", resv[:, :, co * 512:(co + 1) * 512],
                       rch[:].rearrange("p (n w) -> p n w", n=16), sem="rwb")
            dv = emit_ln_chunk(k, 512, with_rs=False)
            k.wait("gpsimd", "dv", dv)
            st = k.pdma("gpsimd", [(
                shn0[:, :, co * 512:(co + 1) * 512],
                nst[:].rearrange("p (n w) -> p n w", n=16),
                shn0[:, :, To + co * 512:To + (co + 1) * 512],
                nst[:].rearrange("p (n w) -> p n w", n=16))], sem="nsh")
            k.wait("vector", "nsh", st)
            k.wait("sync", "nsh", st)
            k.wait("scalar", "nsh", st)
        else:
            k.dma("scalar", d["outT"].ap().rearrange("(n p) t -> p n t", p=128)
                  [:, :, co * 512:(co + 1) * 512],
                  rch[:].rearrange("p (n w) -> p n w", n=16), sem="oput")
    if ci == 0:
        k.wait("gpsimd", "nsh", k.sems["nsh"])
        k.flag_write("n0_b")



def phase_attn_b(k):
    cfg = k.cfg
    s, d = k.sbuf, k.dram
    pb = [s[f"pb{i}"] for i in range(8)]
    nst, rch = s["nst"], s["rch"]
    T, nD = cfg.T, cfg.nD
    cA = s["colsA"]

    def MM(out, lhsT, rhs, start, stop, pe=False):
        return k.emit("tensor", lambda e, out=out, lhsT=lhsT, rhs=rhs,
                      start=start, stop=stop:
                      e.matmul(out, lhsT, rhs, start=start, stop=stop),
                      "pe" if pe else None)

    k.tap("dp2", s["bdpc"][:])
    k.flag_poll("n0_b", "xs")
    k.wait("sync", "xs", k.sems["xs"])
    k.wait("sync", "pe", k.sems["pe"])
    k.wait("sync", "dv", k.sems["dv"])
    lx = k.dma("sync", s["wbuf"][:, 0:4352].rearrange("p (n c) -> p n c", n=16),
               dview(d["bw_x"]), sem="wa0")
    k.wait("tensor", "wa0", lx)
    k.wait("vector", "rwb", k.sems.get("rwb", 0))
    k.emit("vector", lambda e: e.memset(cA[:, 0:16], 0.0))
    k.emit("vector", lambda e: e.memset(s["rows"][0:1, 3072:3200], 1.0))
    k.wait("tensor", "dv", k.sems["dv"])
    pe = MM(pb[5][0:16, 0:16], s["ident"][0:16, 0:16], s["ident"][0:16, 0:16],
            True, True, pe=True)
    k.wait("vector", "pe", pe)
    k.emit("vector", lambda e: e.tensor_copy(cA[0:16, 16:32], pb[5][0:16, 0:16]))
    shn = dview(d["sh_n0"])
    wgv = dview(d["bw_g"])
    featB = dview(d["sh_featB"])

    pe_tch = {}
    dv_tch = {}
    for tch in range(cfg.nTCf):
        slot = tch % 2
        nchv = s["nch"][:, slot * 8192: slot * 8192 + 8192].rearrange(
            "p (n w) -> p n w", n=16)
        if tch >= 2:
            k.wait("sync", "pe", pe_tch[tch - 2])
            k.wait("sync", "dv", dv_tch[tch - 2])
        cols = slice(tch * 512, (tch + 1) * 512)
        ln = k.pdma("sync", [
            (nchv[:, 0:8, :], shn[:, 0:8, cols], nchv[:, 0:8, :], shn[:, 8:16, cols]),
            (nchv[:, 8:16, :], shn[:, 8:16, cols], nchv[:, 8:16, :], shn[:, 0:8, cols]),
        ], sem=f"na{slot}")
        k.wait("tensor", f"na{slot}", ln)
        k.wait("tensor", "dv", k.sems["dv"])
        k.wait("tensor", "ac", k.sems["ac"])
        for dt_ in range(nD):
            MM(pb[0][:, 0:512], s["wbuf"][:, dt_ * 272: dt_ * 272 + 128],
               nchv[:, dt_, :], dt_ == 0, dt_ == nD - 1, pe=(dt_ == nD - 1))
        for dt_ in range(nD):
            MM(pb[1][:, 0:512], s["wbuf"][:, dt_ * 272 + 128: dt_ * 272 + 256],
               nchv[:, dt_, :], dt_ == 0, dt_ == nD - 1, pe=(dt_ == nD - 1))
        for dt_ in range(nD):
            MM(pb[2][0:16, 0:512], s["wbuf"][:, dt_ * 272 + 256: dt_ * 272 + 272],
               nchv[:, dt_, :], dt_ == 0, dt_ == nD - 1, pe=(dt_ == nD - 1))
        k.wait("vector", "pe", k.sems["pe"])
        k.emit("vector", lambda e: e.tensor_copy(rch[:, 3072:3584], pb[0][:, 0:512]))
        k.emit("vector", lambda e: e.tensor_copy(nst[:, 1024:1536], pb[1][:, 0:512]))
        dv = k.emit("vector", lambda e: e.tensor_copy(rch[0:16, 3584:4096],
                                                      pb[2][0:16, 0:512]))
        k.wait("tensor", "dv", dv)
        if tch == 0:
            k.tap("bproj", rch[:, 3072:4096])
        for h in range(16):
            k.wait("vector", "pe", k.sems["pe"])
            pe = MM(pb[3][0:1, 0:512], cA[0:16, 16 + h:17 + h],
                    rch[0:16, 3584:4096], True, True, pe=True)
            k.wait("vector", "pe", pe)
            dv0 = k.emit("vector", lambda e: e.tensor_copy(
                s["rows"][0:1, 0:512], pb[3][0:1, 0:512]))
            k.wait("tensor", "dv", dv0)
            pe = MM(pb[3][:, 0:512], s["rows"][0:1, 3072:3200],
                    s["rows"][0:1, 0:512], True, True, pe=True)
            k.wait("vector", "pe", pe)
            dv = k.emit("vector", lambda e: e.tensor_add(
                rch[:, 0:512], pb[3][:, 0:512], rch[:, 3072:3584]))
            k.wait("scalar", "dv", dv)
            k.emit("scalar", lambda e: e.activation(
                out=rch[:, 512:1024], in_=rch[:, 0:512], func=AF.Abs))
            k.emit("scalar", lambda e: e.activation(
                out=rch[:, 512:1024], in_=rch[:, 512:1024], func=AF.Exp, scale=-1.0))
            ac = k.emit("scalar", lambda e: e.activation(
                out=rch[:, 512:1024], in_=rch[:, 512:1024], func=AF.Ln, bias=1.0))
            k.wait("vector", "ac", ac)
            k.emit("vector", lambda e: e.tensor_scalar_max(
                rch[:, 1024:1536], rch[:, 0:512], 0.0))
            k.emit("vector", lambda e: e.tensor_add(
                rch[:, 1024:1536], rch[:, 1024:1536], rch[:, 512:1024]))
            dv = k.emit("vector", lambda e, h=h: e.tensor_scalar_mul(
                rch[:, 0:512], rch[:, 1024:1536], s["bAT"][:, h:h + 1]))
            k.wait("scalar", "dv", dv)
            k.emit("scalar", lambda e: e.activation(
                out=rch[:, 1536:2048], in_=rch[:, 0:512], func=AF.Exp))
            ac = k.emit("scalar", lambda e: e.activation(
                out=rch[:, 2048:2560], in_=rch[:, 1024:1536], func=AF.Square))
            k.wait("vector", "ac", ac)
            if tch == 0 and h == 0:
                k.tap("bdelta", rch[:, 1024:1536])
                k.tap("ba", rch[:, 1536:2048])
                k.tap("bdd", rch[:, 2048:2560])
            k.emit("vector", lambda e, h=h: e.tensor_tensor_scan(
                rch[:, 2560:3072], rch[:, 1536:2048], rch[:, 2048:2560],
                cA[:, h:h + 1], ALU.mult, ALU.add))
            k.emit("vector", lambda e, h=h: e.tensor_copy(
                cA[:, h:h + 1], rch[:, 3071:3072]))
            dv = k.emit("vector", lambda e: e.tensor_mul(
                nst[:, 1536:2048], rch[:, 2560:3072], nst[:, 1024:1536]))
            if tch == 0 and h == 0:
                k.tap("bscan", rch[:, 2560:3072])
                k.tap("btt", nst[:, 1536:2048])
            if tch == 0 and h == 7:
                k.tap("bscan7", rch[:, 2560:3072])
            k.wait("tensor", "dv", dv)
            lhsT = (s["selk"][:, h * 16:(h + 1) * 16] if h < 8
                    else s["selq"][:, (h - 8) * 16:(h - 7) * 16])
            MM(pb[4][0:16, 0:512], lhsT, nst[:, 1536:2048], h == 0, h == 15,
               pe=(h == 15))
        if tch == 0:
            k.tap("selgv", s["selg"][:])
        k.wait("vector", "pe", k.sems["pe"])
        dv = k.emit("vector", lambda e: e.tensor_copy(
            nst[0:16, 2048:2560], pb[4][0:16, 0:512]))
        k.wait("tensor", "dv", dv)
        if tch < 2:
            k.tap(f"by{tch}", nst[0:16, 2048:2560])
        dvf = {}
        acf = {}
        pe_gf = {}
        for ft in range(8):
            wsl = ft % 2
            wsl3 = ft % 3
            wq_ = ("sync", "scalar")[wsl3 % 2]
            yb = pb[5] if wsl == 0 else pb[3]
            gb = pb[6] if wsl == 0 else pb[0]
            if ft >= 3:
                k.wait(wq_, "pe", pe_gf[ft - 3])
            else:
                k.wait(wq_, "pe", k.sems["pe"])
            lw = k.dma(wq_, s["wbuf"][:, 4352 + wsl3 * 2048: 4352 + wsl3 * 2048 + 2048]
                       .rearrange("p (n c) -> p n c", n=16),
                       wgv[:, :, ft * 128:(ft + 1) * 128], sem=f"bg{wsl3}")
            k.wait("tensor", f"bg{wsl3}", lw)
            if ft >= 2:
                k.wait("tensor", "dv", dvf[ft - 2])
                k.wait("tensor", "ac", acf[ft - 2])
            MM(yb[:, 0:512], s["selg"][0:16, ft * 128:(ft + 1) * 128],
               nst[0:16, 2048:2560], True, True)
            for dt_ in range(nD):
                pe_gf[ft] = MM(gb[:, 0:512],
                               s["wbuf"][:, 4352 + wsl3 * 2048 + dt_ * 128:
                                         4352 + wsl3 * 2048 + (dt_ + 1) * 128],
                               nchv[:, dt_, :], dt_ == 0, dt_ == nD - 1,
                               pe=(dt_ == nD - 1))
            k.wait("scalar", "pe", k.sems["pe"])
            if ft >= 1:
                k.wait("scalar", "dv", dvf[ft - 1])
            acf[ft] = k.emit("scalar", lambda e, gb=gb, ft=ft: e.activation(
                out=nst[:, 2560:3072], in_=gb[:, 0:512], func=AF.Sigmoid,
                bias=s["bbg"][:, ft:ft + 1]))
            k.wait("vector", "ac", acf[ft])
            k.emit("vector", lambda e, ft=ft, yb=yb, nchv=nchv: e.scalar_tensor_tensor(
                out=s["f32s"][:, 0:512], in0=nchv[:, ft, :],
                scalar=s["bdpc"][:, ft:ft + 1], in1=yb[:, 0:512],
                op0=ALU.mult, op1=ALU.add))
            dvf[ft] = k.emit("vector", lambda e, ft=ft: e.tensor_mul(
                nst[:, 3072 + ft * 512: 3072 + (ft + 1) * 512],
                s["f32s"][:, 0:512], nst[:, 2560:3072]))
            if tch == 0 and ft == 7:
                k.tap("bsig7", nst[:, 2560:3072])
                k.tap("bf327", s["f32s"][:, 0:512])
        if tch == 0:
            k.tap("bfst", nst[:, 3072:7168])
            k.tap("bxg0", s["nch"][:, slot * 8192: slot * 8192 + 8192])
        k.wait("gpsimd", "dv", k.sems["dv"])
        st = k.pdma("gpsimd", [(
            featB[:, 0:8, cols], nst[:, 3072:7168].rearrange("p (n w) -> p n w", n=8),
            featB[:, 8:16, cols], nst[:, 3072:7168].rearrange("p (n w) -> p n w", n=8),
        )], sem="fsh")
        k.wait("vector", "fsh", st)
        k.wait("scalar", "fsh", st)
        pe_tch[tch] = k.sems["pe"]
        dv_tch[tch] = k.sems["dv"]
    k.wait("gpsimd", "fsh", k.sems["fsh"])
    k.flag_write("featB")


def kernel_build(cfg):
    k = K(cfg)
    declare(k)
    load_consts(k)
    ldv = k.sems["cld"]
    for e in ("tensor", "vector", "scalar", "gpsimd"):
        k.wait(e, "cld", ldv)
    k.wait("sync", "rcp", k.sems["rcp"])
    cast_weights(k)
    phase_ln0(k)
    phase_attn_a(k)
    phase_wout(k, "sh_featA", "aw_out", 32, "featA", "n1_f1")
    phase_ffn(k, 0)
    phase_attn_b(k)
    phase_wout(k, "sh_featB", "bw_o", 16, "featB", "n1_f2")
    phase_ffn(k, 1)
    finale(k)
    return k


def finale(k):
    cfg = k.cfg
    s, d = k.sbuf, k.dram
    if "n0" in cfg.taps:
        k.flag_poll("n0_a", "xs")
        k.wait("gpsimd", "xs", k.sems["xs"])
        st = k.dma("gpsimd", d["tap_n0"].ap(), d["sh_n0"].ap(), sem="tput")
        k.wait("gpsimd", "tput", st)
    if "featA" in cfg.taps:
        st = k.dma("gpsimd", d["tap_featA"].ap(), d["sh_featA"].ap(), sem="tput")
        k.wait("gpsimd", "tput", st)
    if "n1" in cfg.taps:
        k.flag_poll("n1_f1", "xs")
        k.wait("gpsimd", "xs", k.sems["xs"])
        st = k.dma("gpsimd", d["tap_n1"].ap(), d["sh_n1"].ap(), sem="tput")
        st = k.dma("gpsimd", d["tap_nrs"].ap(), d["sh_nrs"].ap(), sem="tput")
        k.wait("gpsimd", "tput", st)
    if "n0b" in cfg.taps:
        k.flag_poll("n0_b", "xs")
        k.wait("gpsimd", "xs", k.sems["xs"])
        st = k.dma("gpsimd", d["tap_n0b"].ap(), d["sh_n0"].ap(), sem="tput")
        k.wait("gpsimd", "tput", st)
    if "featB" in cfg.taps:
        k.flag_poll("featB", "xs")
        k.wait("gpsimd", "xs", k.sems["xs"])
        st = k.dma("gpsimd", d["tap_featB"].ap(), d["sh_featB"].ap(), sem="tput")
        k.wait("gpsimd", "tput", st)
    k.wait("gpsimd", "oput", k.sems["oput"])
    k.wait("sync", "oput", k.sems["oput"])


def kernel(**inputs):
    cfg = Cfg()
    k = kernel_build(cfg)
    nc = k.build()
    nonce = int(time.time_ns() // 1000 & 0x0FFFFFFF) | 1
    in_maps = [host_pack(inputs, cfg, core, nonce) for core in range(8)]
    res = run_bass_kernel_spmd(nc, in_maps, core_ids=list(range(8)))
    T, To, D = cfg.T, cfg.To, cfg.D
    out = np.zeros((4, T, D), np.float32)
    for core in range(8):
        s, bidx = core & 1, core // 2
        out[bidx, s * To:(s + 1) * To, :] = np.asarray(res.results[core]["outT"]).T
    return out


# version 05
# speedup vs baseline: 1.0742x; 1.0742x over previous
import time

import numpy as np
import ml_dtypes

import concourse.bass as bass
import concourse.mybir as mybir
from concourse.bass_utils import run_bass_kernel_spmd

F32 = mybir.dt.float32
BF16 = mybir.dt.bfloat16
I32 = mybir.dt.int32
AF = mybir.ActivationFunctionType
ALU = mybir.AluOpType
AX = mybir.AxisListType

BF = ml_dtypes.bfloat16
NEG = -1e30

CONSTS = [1e-5, 1e-6, -0.5, 0.5, 1.0, -1.0, 2.0, 0.1, -0.1,
          float(1.0 / 2048.0), float(128 ** -0.5), float(-(128 ** -0.5)),
          float(1.0 / 128.0), 0.0, -2.4260151319598084, -2.0] + [7.0] * 16


class Cfg:
    def __init__(self, T_full=4096, taps=(), solo=False, detect=True, priv=False):
        self.solo = solo
        self.detect = detect
        self.priv = priv
        self.T = T_full
        self.To = T_full // 2
        self.D = 2048
        self.nD = 16
        self.C = 128
        self.nCh = T_full // 128
        self.TCH = 512
        self.nTCo = self.To // self.TCH
        self.nTCf = self.T // self.TCH
        self.taps = set(taps)



def host_pack(inputs, cfg, core, nonce):
    s = core & 1
    b = core // 2
    To = cfg.To
    f32 = lambda a: np.ascontiguousarray(np.asarray(a, np.float32))
    bf = lambda a: np.ascontiguousarray(np.asarray(a).astype(BF))

    d = {}
    x = np.asarray(inputs["L0_attn_a_x"])
    d["x0T"] = f32(x[b, s * To:(s + 1) * To, :].T)

    wqkv = np.asarray(inputs["L0_attn_a_w_qkv"])
    d["a_wqkv"] = f32(np.concatenate([
        wqkv[:, s * 1024:(s + 1) * 1024],
        wqkv[:, 2048 + s * 1024:2048 + (s + 1) * 1024],
        wqkv[:, 4096 + s * 2048:4096 + (s + 1) * 2048]], axis=1))
    d["a_wz"] = f32(np.asarray(inputs["L0_attn_a_w_z"])[:, s * 2048:(s + 1) * 2048])
    d["a_wba"] = f32(np.concatenate([
        np.asarray(inputs["L0_attn_a_w_b"])[:, s * 16:(s + 1) * 16],
        np.asarray(inputs["L0_attn_a_w_a"])[:, s * 16:(s + 1) * 16]], axis=1))
    convw = np.asarray(inputs["L0_attn_a_conv_w"])
    d["a_convT"] = f32(np.concatenate([
        convw[:, s * 1024:(s + 1) * 1024],
        convw[:, 2048 + s * 1024:2048 + (s + 1) * 1024],
        convw[:, 4096 + s * 2048:4096 + (s + 1) * 2048]], axis=1).T)
    d["a_miscT"] = f32(np.stack([
        np.asarray(inputs["L0_attn_a_dt_bias"])[s * 16:(s + 1) * 16],
        np.asarray(inputs["L0_attn_a_A_log"])[s * 16:(s + 1) * 16]], axis=1))
    d["a_normwT"] = f32(np.asarray(inputs["L0_attn_a_norm_w"])[:, None])
    d["a_wout"] = f32(inputs["L0_attn_a_w_out"])

    d["f_wq"] = f32(np.asarray(inputs["L0_ffn_Wq"]).T[:, s * 1024:(s + 1) * 1024])
    d["f_wk"] = f32(np.asarray(inputs["L0_ffn_Wk"]).T[:, s * 1024:(s + 1) * 1024])
    d["f_wv"] = f32(np.asarray(inputs["L0_ffn_Wv"]).T[:, s * 1024:(s + 1) * 1024])
    d["f_wo"] = f32(inputs["L0_ffn_Wo"])
    d["f_A"] = f32(inputs["L0_ffn_A"])
    d["f_B"] = f32(inputs["L0_ffn_B"])
    d["f_lnwT"] = f32(np.stack([np.asarray(inputs["L0_ffn_ln1_w"]),
                                np.asarray(inputs["L0_ffn_ln2_w"])], axis=1))
    d["f_win"] = f32(np.asarray(inputs["L0_ffn_W_in_base"]).T)
    d["f_wup"] = f32(np.asarray(inputs["L0_ffn_W_up_base"]).T)
    d["f_wdown"] = f32(np.asarray(inputs["L0_ffn_W_down"]).T)

    rot = lambda a: np.concatenate([a[s * 1024:(s + 1) * 1024],
                                    a[(1 - s) * 1024:(2 - s) * 1024]], axis=0)
    wx = rot(np.asarray(inputs["L0_attn_b_Wx"]))
    d["b_wx"] = f32(np.concatenate([wx[:, :256],
                                    wx[:, 256 + s * 16:256 + (s + 1) * 16]], axis=1))
    d["b_wg"] = f32(rot(np.asarray(inputs["L0_attn_b_Wg"]))[:, s * 1024:(s + 1) * 1024])
    d["b_bgT"] = f32(np.asarray(inputs["L0_attn_b_bg"])[s * 1024:(s + 1) * 1024].reshape(8, 128).T)
    d["b_wo"] = f32(inputs["L0_attn_b_Wo"])
    d["b_AT"] = f32(np.asarray(inputs["L0_attn_b_A"])[s * 16:(s + 1) * 16].T)
    dp = np.asarray(inputs["L0_attn_b_Dp"])[s * 16:(s + 1) * 16]
    dpc = np.zeros((128, 8), np.float32)
    for p in range(8):
        dpc[:64, p] = dp[2 * p]
        dpc[64:, p] = dp[2 * p + 1]
    d["b_dpc"] = f32(dpc)

    d["c_ident"] = bf(np.eye(128))
    d["c_ones"] = bf(np.ones((128, 128)))
    ii, jj = np.meshgrid(np.arange(128), np.arange(128), indexing="ij")
    d["c_masks"] = bf(np.stack([
        np.where(jj < ii, 0.0, NEG),
        np.where(jj > ii, 0.0, NEG),
        np.where(jj >= ii, 0.0, NEG),
    ]))
    kk2, cc2 = np.meshgrid(np.arange(128), np.arange(896), indexing="ij")
    d["c_cmask"] = bf(np.where(cc2 - 384 >= kk2, 0.0, NEG))
    sel = np.zeros((16, 16 * 128), np.float32)
    for h in range(16):
        sel[h, h * 128:(h + 1) * 128] = 1.0
    d["c_sel"] = bf(sel)
    selk = np.zeros((128, 8 * 16), np.float32)
    selq = np.zeros((128, 8 * 16), np.float32)
    for r in range(8):
        selk[:, r * 16 + r] = 1.0
        selq[:, r * 16 + 8 + r] = 1.0
    d["c_selk"] = bf(selk)
    d["c_selq"] = bf(selq)
    exp2 = np.zeros((8, 16), np.float32)
    for r in range(8):
        exp2[r, 2 * r] = 1.0
        exp2[r, 2 * r + 1] = 1.0
    d["c_exp2"] = bf(exp2)
    selg = np.zeros((16, 8 * 128), np.float32)
    for ft in range(8):
        selg[2 * ft, ft * 128:ft * 128 + 64] = 1.0
        selg[2 * ft + 1, ft * 128 + 64:(ft + 1) * 128] = 1.0
    d["c_selg"] = bf(selg)
    d["c_scal"] = f32(np.tile(np.array([CONSTS], np.float32), (128, 1)))
    d["par"] = np.ascontiguousarray(
        np.array([[s, nonce & 0x3FFFFFFF, 0, 0]], np.int32))
    return d



ENGS = ("sync", "tensor", "vector", "scalar", "gpsimd")
QSEM = {"sync": "q_ld", "scalar": "q_sc", "gpsimd": "q_gp", "vector": "q_dv"}


class K:
    def __init__(self, cfg):
        self.cfg = cfg
        self.nc = bass.Bass("TRN2", target_bir_lowering=False,
                            detect_race_conditions=cfg.detect)
        self.prog = {e: [] for e in ENGS}
        self.sems = {"prolog": 0}
        self.semh = {}
        self.dram = {}
        self.sbuf = {}
        self.regs = {}
        self.flagseq = {}
        self._nflags = 0
        self.sbuf_bytes = 0

    def dram_t(self, name, shape, dtype, **kw):
        h = self.nc.dram_tensor(name, shape, dtype, **kw)
        self.dram[name] = h
        return h

    def sb(self, name, shape, dtype):
        h = self.nc.alloc_sbuf_tensor(name, list(shape), dtype)
        self.sbuf[name] = h
        self.sbuf_bytes += int(np.prod(shape[1:])) * mybir.dt.size(dtype)
        return h

    def psum(self, name, shape=(128, 512), dtype=F32):
        h = self.nc.alloc_psum_tensor(name, list(shape), dtype)
        self.sbuf[name] = h
        return h

    def sem(self, name):
        if name not in self.sems:
            self.sems[name] = 0
        return name

    def flag_id(self, name):
        if name not in self.flagseq:
            self.flagseq[name] = self._nflags
            self._nflags += 1
        return self.flagseq[name]

    def op(self, eng, fn):
        self.prog[eng].append(fn)

    AUTOSEM = {"vector": "dv", "scalar": "ac", "gpsimd": "gp"}

    def emit(self, eng, call, sem=None, by=1):
        auto = self.AUTOSEM.get(eng)
        if auto is not None:
            self.sem(auto)
            prev = self.sems[auto]
            self.sems[auto] += 1
            def fn(e, call=call, auto=auto, prev=prev):
                if prev:
                    e.wait_ge(self.semh[auto], prev)
                call(e).then_inc(self.semh[auto], 1)
            self.op(eng, fn)
            if sem is not None and sem != auto:
                raise ValueError(f"compute op on {eng} counts on '{auto}'")
            return self.sems[auto]
        if sem is not None:
            self.sem(sem)
            self.sems[sem] += by
            def fn(e, call=call, sem=sem, by=by):
                call(e).then_inc(self.semh[sem], by)
        else:
            def fn(e, call=call):
                call(e)
        self.op(eng, fn)
        return self.sems[sem] if sem else None

    def wait(self, eng, sem, value=None):
        self.sem(sem)
        v = self.sems[sem] if value is None else value
        self.op(eng, lambda e, sem=sem, v=v: e.wait_ge(self.semh[sem], v))

    def dma(self, q, out, in_, sem=None, **kw):
        sem = QSEM[q] if sem is None else sem
        self.sem(sem)
        self.sems[sem] += 16
        def fn(e, out=out, in_=in_, kw=kw, sem=sem):
            e.dma_start(out=out, in_=in_, **kw).then_inc(self.semh[sem], 16)
        self.op(q, fn)
        return self.sems[sem]

    def pdma(self, q, pairs, sem=None):
        sem = QSEM[q] if sem is None else sem
        self.sem(sem)
        self.sems[sem] += 16 * len(pairs)
        tgt = self.sems[sem]
        def fn(e, pairs=pairs, sem=sem):
            pr = self.regs[(e.engine, "pr")]
            with e.If_eq(pr, 0):
                for (o0, i0, _o1, _i1) in pairs:
                    e.dma_start(out=o0, in_=i0).then_inc(self.semh[sem], 16)
            with e.Else():
                for (_o0, _i0, o1, i1) in pairs:
                    e.dma_start(out=o1, in_=i1).then_inc(self.semh[sem], 16)
        self.op(q, fn)
        return tgt

    def flag_write(self, name):
        idx = self.flag_id(name)
        self.sem("q_fl")
        prev = self.sems["q_fl"]
        self.sems["q_fl"] += 16
        fl = self.sbuf["flagout"]
        sh = self.dram["sh_flags"]
        def fn(e, idx=idx, prev=prev):
            nr = self.regs[(e.engine, "nonce")]
            tmp = self.regs[(e.engine, "tmp")]
            pr = self.regs[(e.engine, "pr")]
            e.wait_ge(self.semh["q_fl"], prev)
            e.reg_add(tmp, nr, idx + 1)
            e.store(fl[0:1, 0:1], tmp)
            with e.If_eq(pr, 0):
                e.dma_start(out=sh.ap()[0:1, idx:idx + 1], in_=fl[0:1, 0:1]).then_inc(self.semh["q_fl"], 16)
            with e.Else():
                e.dma_start(out=sh.ap()[1:2, idx:idx + 1], in_=fl[0:1, 0:1]).then_inc(self.semh["q_fl"], 16)
        self.op("gpsimd", fn)

    def flag_poll(self, name, done_sem="xs"):
        idx = self.flag_id(name)
        if self.cfg.solo:
            self.sem(done_sem)
            self.sems[done_sem] += 1
            self.op("gpsimd", lambda e, ds=done_sem: e.sem_inc(self.semh[ds], 1))
            return self.sems[done_sem]
        self.sem(done_sem)
        self.sem("q_pl")
        self.sems[done_sem] += 1
        fl = self.sbuf["flagin"]
        sh = self.dram["sh_flags"]
        def fn(e, idx=idx, done_sem=done_sem):
            pr = self.regs[(e.engine, "pr")]
            nr = self.regs[(e.engine, "nonce")]
            tmp = self.regs[(e.engine, "tmp")]
            fr = self.regs[(e.engine, "fr")]
            diff = self.regs[(e.engine, "diff")]
            cnt = self.regs[(e.engine, "cnt")]
            e.reg_add(tmp, nr, idx + 1)
            e.reg_mov(diff, 1)
            with e.While(diff):
                e.reg_add(cnt, cnt, 16)
                with e.If_eq(pr, 0):
                    e.dma_start(out=fl[0:1, 0:1], in_=sh.ap()[1:2, idx:idx + 1]).then_inc(self.semh["q_pl"], 16)
                with e.Else():
                    e.dma_start(out=fl[0:1, 0:1], in_=sh.ap()[0:1, idx:idx + 1]).then_inc(self.semh["q_pl"], 16)
                e.wait_ge(self.semh["q_pl"], cnt)
                e.reg_load(fr, fl[0:1, 0:1])
                e.reg_alu(diff, fr, tmp, ALU.bitwise_xor)
            e.sem_inc(self.semh[done_sem], 1)
        self.op("gpsimd", fn)
        return self.sems[done_sem]

    def tap(self, name, ap):
        if name not in self.cfg.taps:
            return
        sh = [ap.shape[0], int(np.prod(ap.shape[1:]))]
        t = self.dram_t(f"tap_{name}", sh, ap.dtype, kind="ExternalOutput")
        for e in ("vector", "scalar"):
            self.wait("gpsimd", self.AUTOSEM[e], self.sems.get(self.AUTOSEM[e], 0))
        self.sem("tput")
        st = self.dma("gpsimd", t.ap(), ap, sem="tput")
        self.wait("gpsimd", "tput", st)
        self.wait("vector", "tput", st)
        self.wait("scalar", "tput", st)

    def build(self):
        nc = self.nc
        for n in self.sems:
            self.semh[n] = nc.alloc_semaphore(n)
        blk = nc.Block()
        block = blk.__enter__()

        def prolog(e):
            eng = e.engine
            for rn in ("pr", "nonce", "tmp", "fr", "diff", "cnt"):
                self.regs[(eng, rn)] = e.alloc_register(f"r_{rn}")
            e.wait_ge(self.semh["prolog"], 16)
            e.reg_load(self.regs[(eng, "pr")], self.sbuf["sb_par"][0:1, 0:1])
            e.reg_load(self.regs[(eng, "nonce")], self.sbuf["sb_par"][0:1, 1:2])
            e.reg_mov(self.regs[(eng, "cnt")], 0)

        def mk(engname):
            def run(e):
                if engname == "sync":
                    e.dma_start(out=self.sbuf["sb_par"][:],
                                in_=self.dram["par"].ap()).then_inc(self.semh["prolog"], 16)
                prolog(e)
                for fn in self.prog[engname]:
                    fn(e)
            return run

        block.sync(mk("sync"))
        block.tensor(mk("tensor"))
        block.vector(mk("vector"))
        block.scalar(mk("scalar"))
        block.gpsimd(mk("gpsimd"))
        blk.__exit__(None, None, None)
        return nc


def dview(t, p=128):
    return t.ap().rearrange("(n p) t -> p n t", p=p)



def declare(k):
    cfg = k.cfg
    T, To, D = cfg.T, cfg.To, cfg.D
    dt = k.dram_t
    dt("x0T", [D, To], F32, kind="ExternalInput")
    dt("a_wqkv", [D, 4096], F32, kind="ExternalInput")
    dt("a_wz", [D, 2048], F32, kind="ExternalInput")
    dt("a_wba", [D, 32], F32, kind="ExternalInput")
    dt("a_convT", [4096, 4], F32, kind="ExternalInput")
    dt("a_miscT", [16, 2], F32, kind="ExternalInput")
    dt("a_normwT", [128, 1], F32, kind="ExternalInput")
    dt("a_wout", [4096, D], F32, kind="ExternalInput")
    dt("f_wq", [D, 1024], F32, kind="ExternalInput")
    dt("f_wk", [D, 1024], F32, kind="ExternalInput")
    dt("f_wv", [D, 1024], F32, kind="ExternalInput")
    dt("f_wo", [D, D], F32, kind="ExternalInput")
    dt("f_A", [D, 16], F32, kind="ExternalInput")
    dt("f_B", [16, 8192], F32, kind="ExternalInput")
    dt("f_lnwT", [D, 2], F32, kind="ExternalInput")
    dt("f_win", [D, 8192], F32, kind="ExternalInput")
    dt("f_wup", [D, 8192], F32, kind="ExternalInput")
    dt("f_wdown", [8192, D], F32, kind="ExternalInput")
    dt("b_wx", [D, 272], F32, kind="ExternalInput")
    dt("b_wg", [D, 1024], F32, kind="ExternalInput")
    dt("b_bgT", [128, 8], F32, kind="ExternalInput")
    dt("b_wo", [D, D], F32, kind="ExternalInput")
    dt("b_AT", [128, 16], F32, kind="ExternalInput")
    dt("b_dpc", [128, 8], F32, kind="ExternalInput")
    dt("c_ident", [128, 128], BF16, kind="ExternalInput")
    dt("c_ones", [128, 128], BF16, kind="ExternalInput")
    dt("c_masks", [3, 128, 128], BF16, kind="ExternalInput")
    dt("c_cmask", [128, 896], BF16, kind="ExternalInput")
    dt("c_sel", [16, 2048], BF16, kind="ExternalInput")
    dt("c_selk", [128, 128], BF16, kind="ExternalInput")
    dt("c_selq", [128, 128], BF16, kind="ExternalInput")
    dt("c_exp2", [8, 16], BF16, kind="ExternalInput")
    dt("c_selg", [16, 1024], BF16, kind="ExternalInput")
    dt("c_scal", [128, 32], F32, kind="ExternalInput")
    dt("par", [1, 4], I32, kind="ExternalInput")
    dt("outT", [D, To], F32, kind="ExternalOutput")
    if "n0" in cfg.taps:
        dt("tap_n0", [D, T], BF16, kind="ExternalOutput")
    if "featA" in cfg.taps:
        dt("tap_featA", [4096, T], BF16, kind="ExternalOutput")
    if "n1" in cfg.taps:
        dt("tap_n1", [D, T], BF16, kind="ExternalOutput")
        dt("tap_nrs", [D, T], BF16, kind="ExternalOutput")
    if "n0b" in cfg.taps:
        dt("tap_n0b", [D, T], BF16, kind="ExternalOutput")
    if "featB" in cfg.taps:
        dt("tap_featB", [D, T], BF16, kind="ExternalOutput")
    dt("resid", [D, To], F32)
    dt("aw_qkv", [D, 4096], BF16)
    dt("aw_z", [D, 2048], BF16)
    dt("aw_ba", [D, 32], BF16)
    dt("aw_out", [4096, D], BF16)
    dt("fw_q", [D, 1024], BF16)
    dt("fw_k", [D, 1024], BF16)
    dt("fw_v", [D, 1024], BF16)
    dt("fw_A", [D, 16], BF16)
    dt("fw_woa", [D, 16], BF16)
    dt("fw_B", [16, 8192], BF16)
    dt("fw_in", [D, 8192], BF16)
    dt("fw_up", [D, 8192], BF16)
    dt("fw_down", [8192, D], BF16)
    dt("bw_x", [D, 272], BF16)
    dt("bw_g", [D, 1024], BF16)
    dt("bw_o", [D, D], BF16)
    dt("qT_scr", [1024, T], BF16)
    shkw = {} if cfg.priv else {"addr_space": "Shared"}
    dt("sh_n0", [D, T], BF16, **shkw)
    dt("sh_n1", [D, T], BF16, **shkw)
    dt("sh_nrs", [D, T], BF16, **shkw)
    dt("sh_featA", [4096, T], BF16, **shkw)
    dt("sh_featB", [D, T], BF16, **shkw)
    dt("sh_woa", [2, 16 * T], F32, **shkw)
    dt("sh_flags", [2, 64], I32, **shkw)

    sb = k.sb
    sb("sb_par", (1, 4), I32)
    sb("kconst", (128, 32), F32)
    for i, v in enumerate(CONSTS):
        k.nc.const_aps.register(F32, v, k.sbuf["kconst"][:, i:i + 1])
        k.nc.const_aps.register(BF16, v, k.sbuf["kconst"][:, i:i + 1])
    sb("flagout", (1, 16), I32)
    sb("flagin", (1, 16), I32)
    sb("ident", (128, 128), BF16)
    sb("ones", (128, 128), BF16)
    sb("masks", (128, 3 * 128), BF16)
    sb("cmask", (128, 896), BF16)
    sb("sel", (16, 2048), BF16)
    sb("selk", (128, 128), BF16)
    sb("selq", (128, 128), BF16)
    sb("exp2", (8, 16), BF16)
    sb("selg", (16, 1024), BF16)
    sb("lnw", (128, 2 * 16), F32)
    sb("miscT", (16, 2), F32)
    sb("normw", (128, 1), F32)
    sb("bAT", (128, 16), F32)
    sb("bdpc", (128, 8), F32)
    sb("bbg", (128, 8), F32)
    sb("convw", (128, 32 * 4), F32)
    sb("nch", (128, 16384), BF16)
    sb("wbuf", (128, 12288), BF16)
    sb("bigb", (128, 36864), BF16)
    sb("rch", (128, 8192), F32)
    sb("nst", (128, 8192), BF16)
    sb("rows", (1, 3584), F32)
    sb("rows2", (1, 1536), BF16)
    sb("tails", (128, 128), BF16)
    sb("colsA", (128, 256), F32)
    sb("colsB", (128, 320), BF16)
    sb("f32s", (128, 512), F32)
    assert k.sbuf_bytes <= 212600, k.sbuf_bytes

    for i in range(7):
        k.psum(f"pb{i}")
    k.psum("pb7", (128, 1024), BF16)


def load_consts(k):
    cfg = k.cfg
    d, s = k.dram, k.sbuf
    k.dma("sync", s["kconst"][:], d["c_scal"].ap(), sem="cld")
    k.dma("sync", s["ident"][:], d["c_ident"].ap(), sem="cld")
    k.dma("sync", s["ones"][:], d["c_ones"].ap(), sem="cld")
    k.dma("sync", s["masks"][:].rearrange("p (m c) -> p m c", m=3), d["c_masks"].ap().rearrange("m p c -> p m c"), sem="cld")
    k.dma("sync", s["cmask"][:], d["c_cmask"].ap(), sem="cld")
    k.dma("sync", s["sel"][:], d["c_sel"].ap(), sem="cld")
    k.dma("sync", s["selk"][:], d["c_selk"].ap(), sem="cld")
    k.dma("sync", s["selq"][:], d["c_selq"].ap(), sem="cld")
    k.dma("sync", s["exp2"][:], d["c_exp2"].ap(), sem="cld")
    k.dma("sync", s["selg"][:], d["c_selg"].ap(), sem="cld")
    k.dma("sync", s["lnw"][:].rearrange("p (n c) -> p n c", n=16), d["f_lnwT"].ap().rearrange("(n p) c -> p n c", p=128), sem="cld")
    k.dma("sync", s["miscT"][:], d["a_miscT"].ap(), sem="cld")
    k.dma("sync", s["normw"][:], d["a_normwT"].ap(), sem="cld")
    k.dma("sync", s["bAT"][:], d["b_AT"].ap(), sem="cld")
    k.dma("sync", s["bdpc"][:], d["b_dpc"].ap(), sem="cld")
    k.dma("sync", s["bbg"][:], d["b_bgT"].ap(), sem="cld")
    k.dma("sync", s["convw"][:].rearrange("p (n c) -> p n c", n=32), d["a_convT"].ap().rearrange("(n p) c -> p n c", p=128), sem="cld")
    k.dma("scalar", d["resid"].ap(), d["x0T"].ap(), sem="rcp")


def cast_weights(k):
    cfg = k.cfg
    d, s = k.dram, k.sbuf
    jobs = [
        ("a_wqkv", "aw_qkv", 4096, None),
        ("a_wz", "aw_z", 2048, None),
        ("a_wba", "aw_ba", 32, None),
        ("a_wout", "aw_out", 2048, "normw"),
        ("f_wq", "fw_q", 1024, "ln1"),
        ("f_wk", "fw_k", 1024, "ln1"),
        ("f_wv", "fw_v", 1024, "ln1"),
        ("f_A", "fw_A", 16, "ln2"),
        ("f_B", "fw_B", 8192, None),
        ("f_win", "fw_in", 8192, "ln2"),
        ("f_wup", "fw_up", 8192, "ln2"),
        ("f_wdown", "fw_down", 2048, None),
        ("b_wx", "bw_x", 272, None),
        ("b_wg", "bw_g", 1024, None),
        ("b_wo", "bw_o", 2048, None),
    ]
    CH = 4096
    chunks = []
    for (src, dst, ncols, fold) in jobs:
        srct = d[src]
        nrow = srct.shape[0]
        nrt = (nrow + 127) // 128
        for rt in range(nrt):
            nr = min(128, nrow - rt * 128)
            for c0 in range(0, ncols, CH):
                cw = min(CH, ncols - c0)
                chunks.append((src, dst, fold, rt, nr, c0, cw))
    LQ = ("sync", "scalar")
    SQ = ("gpsimd", "gpsimd")
    stv = {0: 0, 1: 0}
    for i, (src, dst, fold, rt, nr, c0, cw) in enumerate(chunks):
        sl = i % 2
        lo = sl * 4096
        if i >= 2:
            k.wait(LQ[sl], f"wb{sl}", stv[sl])
        ld = k.dma(LQ[sl], s["rch"][0:nr, lo:lo + cw],
                   d[src].ap()[rt * 128:rt * 128 + nr, c0:c0 + cw], sem=f"wl{sl}")
        k.wait("vector", f"wl{sl}", ld)
        if fold == "ln1":
            scal = s["lnw"][:, 2 * rt:2 * rt + 1]
        elif fold == "ln2":
            scal = s["lnw"][:, 2 * rt + 1:2 * rt + 2]
        elif fold == "normw":
            scal = s["normw"][:, 0:1]
        else:
            scal = None
        if scal is None:
            dv = k.emit("vector",
                        lambda e, cw=cw, nr=nr, lo=lo: e.tensor_copy(
                            s["nst"][0:nr, lo:lo + cw], s["rch"][0:nr, lo:lo + cw]),
                        "dv")
        else:
            dv = k.emit("vector",
                        lambda e, cw=cw, scal=scal, nr=nr, lo=lo: e.tensor_scalar_mul(
                            s["nst"][0:nr, lo:lo + cw], s["rch"][0:nr, lo:lo + cw], scal),
                        "dv")
        k.wait(SQ[sl], "dv", dv)
        stv[sl] = k.dma(SQ[sl], d[dst].ap()[rt * 128:rt * 128 + nr, c0:c0 + cw],
                        s["nst"][0:nr, lo:lo + cw], sem=f"wb{sl}")
    for sl in range(2):
        k.wait("sync", f"wb{sl}", stv[sl])
        k.wait("vector", f"wb{sl}", stv[sl])
    pb0 = s["pb0"]
    la = k.dma("sync", s["f32s"][:, 0:256].rearrange("p (n c) -> p n c", n=16),
               d["f_A"].ap().rearrange("(n p) c -> p n c", p=128), sem="wld")
    k.wait("tensor", "wld", la)
    pes_w = {}
    for ct in range(16):
        for dt_ in range(16):
            i = ct * 16 + dt_
            slot = i % 6
            if i >= 6:
                k.wait("sync", "pe", pes_w[i - 6])
            lw = k.dma("sync", s["rch"][:, slot * 128:slot * 128 + 128],
                       d["f_wo"].ap()[dt_ * 128:(dt_ + 1) * 128, ct * 128:(ct + 1) * 128],
                       sem=f"wc{slot}")
            k.wait("tensor", f"wc{slot}", lw)
            pes_w[i] = k.emit("tensor",
                              lambda e, dt_=dt_, slot=slot: e.matmul(
                                  pb0[:, 0:16], s["rch"][:, slot * 128:slot * 128 + 128],
                                  s["f32s"][:, dt_ * 16:(dt_ + 1) * 16],
                                  start=(dt_ == 0), stop=(dt_ == 15)),
                              "pe")
        k.wait("vector", "pe", k.sems["pe"])
        dv = k.emit("vector", lambda e: e.tensor_copy(s["nst"][:, 0:16], pb0[:, 0:16]), "dv")
        k.wait("gpsimd", "dv", dv)
        st = k.dma("gpsimd", d["fw_woa"].ap()[ct * 128:(ct + 1) * 128, 0:16], s["nst"][:, 0:16], sem="wst")
        k.wait("vector", "wst", st)
        k.wait("tensor", "dv", dv)
    k.wait("sync", "wst", k.sems["wst"])



def emit_ln_chunk(k, nw, with_rs):
    cfg = k.cfg
    s = k.sbuf
    nD = cfg.nD
    pb5, pb6, pb7 = s["pb5"], s["pb6"], s["pb2"]
    stg = s["bigb"][:, 36352:36352 + nw]
    
    for t in range(nD):
        k.emit("vector",
               lambda e, t=t: e.tensor_copy(s["nst"][:, t * nw:(t + 1) * nw],
                                            s["rch"][:, t * nw:(t + 1) * nw]),
               "dv")
    dvc = k.sems["dv"]
    k.wait("tensor", "dv", dvc)
    k.wait("scalar", "dv", dvc)
    for t in range(nD):
        ac = k.emit("scalar",
                    lambda e, t=t: e.activation(out=stg, in_=s["nst"][:, t * nw:(t + 1) * nw],
                                                func=AF.Square),
                    "ac")
        k.wait("tensor", "ac", ac)
        k.emit("tensor",
               lambda e, t=t: e.matmul(pb6[0:1, 0:nw], s["ones"][:, 0:1],
                                       s["nst"][:, t * nw:(t + 1) * nw],
                                       start=(t == 0), stop=(t == nD - 1)),
               None)
        pe = k.emit("tensor",
                    lambda e, t=t: e.matmul(pb7[0:1, 0:nw], s["ones"][:, 0:1], stg,
                                            start=(t == 0), stop=(t == nD - 1)),
                    "pe")
        if t < nD - 1:
            k.wait("scalar", "pe", pe)
    k.wait("vector", "pe", k.sems["pe"])
    k.emit("vector", lambda e: e.tensor_scalar_mul(s["rows"][0:1, 0 * 512:0 * 512 + nw], pb6[0:1, 0:nw], 1.0 / cfg.D), None)
    k.emit("vector", lambda e: e.tensor_scalar_mul(s["rows"][0:1, 1 * 512:1 * 512 + nw], pb7[0:1, 0:nw], 1.0 / cfg.D), None)
    k.emit("vector", lambda e: e.scalar_tensor_tensor(
        out=s["rows"][0:1, 2 * 512:2 * 512 + nw], in0=s["rows"][0:1, 0 * 512:0 * 512 + nw], scalar=-1.0, in1=s["rows"][0:1, 0 * 512:0 * 512 + nw],
        op0=ALU.mult, op1=ALU.mult), None)
    dv = k.emit("vector", lambda e: e.tensor_add(s["rows"][0:1, 2 * 512:2 * 512 + nw], s["rows"][0:1, 2 * 512:2 * 512 + nw], s["rows"][0:1, 1 * 512:1 * 512 + nw]), "dv")
    k.wait("scalar", "dv", dv)
    k.emit("scalar", lambda e: e.activation(out=s["rows"][0:1, 3 * 512:3 * 512 + nw], in_=s["rows"][0:1, 2 * 512:2 * 512 + nw], func=AF.Ln, bias=1e-5), None)
    ac = k.emit("scalar", lambda e: e.activation(out=s["rows"][0:1, 3 * 512:3 * 512 + nw], in_=s["rows"][0:1, 3 * 512:3 * 512 + nw], func=AF.Exp, scale=-0.5), "ac")
    k.wait("vector", "ac", ac)
    if with_rs:
        k.emit("vector", lambda e: e.tensor_mul(s["rows"][0:1, 5 * 512:5 * 512 + nw], s["rows"][0:1, 3 * 512:3 * 512 + nw], s["rows"][0:1, 3 * 512:3 * 512 + nw]), None)
        dv = k.emit("vector", lambda e: e.tensor_mul(s["rows"][0:1, 5 * 512:5 * 512 + nw], s["rows"][0:1, 5 * 512:5 * 512 + nw], s["rows"][0:1, 2 * 512:2 * 512 + nw]), "dv")
        k.wait("scalar", "dv", dv)
        k.emit("scalar", lambda e: e.activation(out=s["rows"][0:1, 6 * 512:6 * 512 + nw], in_=s["rows"][0:1, 5 * 512:5 * 512 + nw], func=AF.Ln, bias=1e-6), None)
        ac = k.emit("scalar", lambda e: e.activation(out=s["rows"][0:1, 6 * 512:6 * 512 + nw], in_=s["rows"][0:1, 6 * 512:6 * 512 + nw], func=AF.Exp, scale=-0.5), "ac")
        k.wait("vector", "ac", ac)
        k.emit("vector", lambda e: e.tensor_copy(s["rows2"][0:1, 2 * 512:2 * 512 + nw], s["rows"][0:1, 6 * 512:6 * 512 + nw]), None)
    k.emit("vector", lambda e: e.tensor_copy(s["rows2"][0:1, 0 * 512:0 * 512 + nw], s["rows"][0:1, 3 * 512:3 * 512 + nw]), None)
    dvv = k.emit("vector", lambda e: e.tensor_copy(s["rows2"][0:1, 1 * 512:1 * 512 + nw], s["rows"][0:1, 0 * 512:0 * 512 + nw]), "dv")
    k.wait("tensor", "dv", dvv)
    k.emit("tensor", lambda e: e.matmul(pb6[:, 0:nw], s["ones"][0:1, :], s["rows2"][0:1, 0 * 512:0 * 512 + nw], start=True, stop=True), None)
    pe = k.emit("tensor", lambda e: e.matmul(pb7[:, 0:nw], s["ones"][0:1, :], s["rows2"][0:1, 1 * 512:1 * 512 + nw], start=True, stop=True), "pe")
    k.wait("vector", "pe", pe)
    dv = k.emit("vector", lambda e: e.tensor_copy(s["f32s"][:, 0:nw], pb6[:, 0:nw]), "dv")
    for t in range(nD):
        k.emit("vector",
               lambda e, t=t: e.tensor_sub(pb5[:, 0:nw], s["rch"][:, t * nw:(t + 1) * nw], pb7[:, 0:nw]),
               None)
        k.emit("vector",
               lambda e, t=t: e.tensor_mul(s["nst"][:, t * nw:(t + 1) * nw], pb5[:, 0:nw], s["f32s"][:, 0:nw]),
               "dv")
    return k.sems["dv"]


def emit_nrs_tiles(k, nw, dst_slice_fn):
    s = k.sbuf
    pb4 = s["pb4"]
    k.wait("tensor", "dv", k.sems["dv"])
    pe = k.emit("tensor", lambda e: e.matmul(pb4[:, 0:nw], s["ones"][0:1, :], s["rows2"][0:1, 2 * 512:2 * 512 + nw], start=True, stop=True), "pe")
    k.wait("vector", "pe", pe)
    for t in range(k.cfg.nD):
        k.emit("vector",
               lambda e, t=t: e.tensor_mul(dst_slice_fn(t), s["nst"][:, t * nw:(t + 1) * nw], pb4[:, 0:nw]),
               "dv")
    return k.sems["dv"]



def phase_ln0(k):
    cfg = k.cfg
    s, d = k.sbuf, k.dram
    nw = cfg.TCH
    shn = dview(d["sh_n0"])
    for c in range(cfg.nTCo):
        ld = k.dma("sync", s["rch"][:].rearrange("p (n w) -> p n w", n=16),
                   dview(d["resid"])[:, :, c * nw:(c + 1) * nw], sem="lnld")
        k.wait("vector", "lnld", ld)
        k.wait("scalar", "lnld", ld)
        k.wait("tensor", "lnld", ld)
        dv = emit_ln_chunk(k, nw, with_rs=False)
        k.wait("gpsimd", "dv", dv)
        st = k.pdma("gpsimd", [(
            shn[:, :, c * nw:(c + 1) * nw],
            s["nst"][:].rearrange("p (n w) -> p n w", n=16),
            shn[:, :, cfg.To + c * nw:cfg.To + (c + 1) * nw],
            s["nst"][:].rearrange("p (n w) -> p n w", n=16),
        )], sem="nsh")
        k.wait("vector", "nsh", st)
        k.wait("sync", "nsh", st)
        k.wait("scalar", "nsh", st)
    k.wait("gpsimd", "nsh", k.sems["nsh"])
    k.flag_write("n0_a")



BG_DG = 0
BG_MX = 1040
BG_VT = 25616
BG_KT = 25872
BG_XP = 26896
BG_XT = 27920
BG_UU = 28944
BG_SH = 29968
BG_SL = 32016
BG_EA = 34064
BG_CO = 34576
NS_R = 0
NS_TB = 1152
NS_E = 1536
NS_PT = 2560
NS_KP = 3584
NS_BV = 3840
NS_CR = 4352
RC_BA = 0
RC_T1 = 512
RC_G = 2048
RC_LNB = 2560
RC_BETA = 3072
RC_LNS = 3584
RC_LNS16 = 4096
RC_CH = 4608
RC_GE = 5120
RC_ONES = 5248
RC_SP = 5376


def phase_attn_a(k):
    cfg = k.cfg
    s, d = k.sbuf, k.dram
    nD = cfg.nD
    pb = [s[f"pb{i}"] for i in range(8)]
    big, nst, rch = s["bigb"], s["nst"], s["rch"]

    def MM(out, lhsT, rhs, start, stop, pe=False):
        return k.emit("tensor", lambda e, out=out, lhsT=lhsT, rhs=rhs,
                      start=start, stop=stop:
                      e.matmul(out, lhsT, rhs, start=start, stop=stop),
                      "pe" if pe else None)

    def mxsc(ft):
        return big[:, BG_MX + ft * 512: BG_MX + (ft + 1) * 512]

    k.emit("vector", lambda e: e.memset(big[:, BG_SH:BG_SH + 4096], 0))
    k.emit("gpsimd", lambda e: e.memset(s["tails"][:], 0))
    k.emit("gpsimd", lambda e: e.memset(rch[0:16, RC_ONES:RC_ONES + 128], 1.0))
    k.wait("vector", "gp", k.sems["gp"])
    k.wait("scalar", "gp", k.sems["gp"])
    k.emit("scalar", lambda e: e.activation(out=s["colsA"][0:16, 255:256],
                                            in_=s["miscT"][0:16, 1:2], func=AF.Exp))
    k.wait("vector", "ac", k.sems["ac"])
    k.emit("vector", lambda e: e.tensor_scalar_mul(
        s["colsA"][0:16, 254:255], s["colsA"][0:16, 255:256], -1.0))
    k.wait("gpsimd", "rcp", k.sems["rcp"])

    k.tap("dp0", s["bdpc"][:])
    k.flag_poll("n0_a", "xs")
    k.wait("sync", "xs", k.sems["xs"])

    shn = dview(d["sh_n0"])
    wq = dview(d["aw_qkv"])
    wz = dview(d["aw_z"])
    wba = dview(d["aw_ba"])
    featA = dview(d["sh_featA"])

    pe_sc = {}
    for sc in range(cfg.nTCf):
        slot = sc % 2
        nview = s["nch"][:, slot * 8192: slot * 8192 + 8192].rearrange(
            "p (n w) -> p n w", n=16)
        if sc >= 2:
            k.wait("sync", "pe", pe_sc[sc - 2])
        ln = k.dma("sync", nview, shn[:, :, sc * 512:(sc + 1) * 512], sem=f"na{slot}")
        k.wait("tensor", f"na{slot}", ln)
        k.wait("tensor", "ac", k.sems["ac"])
        k.wait("tensor", "dv", k.sems["dv"])

        acq = {}
        pes_ft = {}
        for ft in range(48):
            wsl = ft % 6
            wq_ = "sync"
            wtile = s["wbuf"][:, wsl * 2048: wsl * 2048 + 2048]
            if ft >= 6:
                k.wait(wq_, "pe", pes_ft[ft - 6])
            elif ft <= 1 and sc > 0:
                k.wait(wq_, "pe", k.sems["pe"])
            src = wq[:, :, ft * 128:(ft + 1) * 128] if ft < 32 else \
                wz[:, :, (ft - 32) * 128:(ft - 31) * 128]
            lw = k.dma(wq_, wtile.rearrange("p (n c) -> p n c", n=16), src,
                       sem=f"wa{wsl}")
            k.wait("tensor", f"wa{wsl}", lw)
            if ft >= 2 and (ft - 2) in acq:
                k.wait("tensor", "ac", acq[ft - 2])
            pbank = pb[ft % 2]
            for dt_ in range(nD):
                pes_ft[ft] = MM(pbank[:, 0:512], wtile[:, dt_ * 128:(dt_ + 1) * 128],
                                nview[:, dt_, :], dt_ == 0, dt_ == nD - 1,
                                pe=(dt_ == nD - 1))
            if ft >= 32:
                k.wait("scalar", "pe", k.sems["pe"])
                acq[ft] = k.emit("scalar", lambda e, ft=ft, pbank=pbank: e.activation(
                    out=mxsc(ft), in_=pbank[:, 0:512], func=AF.Silu))
                continue
            crs = NS_CR + (ft % 2) * 520
            k.wait("scalar", "pe", k.sems["pe"])
            k.emit("scalar", lambda e, crs=crs, ft=ft: e.copy(
                nst[:, crs:crs + 3], s["tails"][:, ft * 4: ft * 4 + 3]))
            k.emit("scalar", lambda e, crs=crs, pbank=pbank: e.copy(
                nst[:, crs + 3:crs + 515], pbank[:, 0:512]))
            ac = k.emit("scalar", lambda e, crs=crs, ft=ft: e.copy(
                s["tails"][:, ft * 4: ft * 4 + 3], nst[:, crs + 512:crs + 515]))
            acc = s["f32s"][:, 0:512] if ft % 2 == 0 else rch[:, 7680:8192]
            k.wait("vector", "ac", ac)
            k.emit("vector", lambda e, crs=crs, ft=ft, acc=acc: e.tensor_scalar_mul(
                acc, nst[:, crs:crs + 512], s["convw"][:, ft * 4: ft * 4 + 1]))
            for kk in range(1, 4):
                k.emit("vector", lambda e, crs=crs, ft=ft, kk=kk, acc=acc:
                       e.scalar_tensor_tensor(
                           out=acc, in0=nst[:, crs + kk: crs + kk + 512],
                           scalar=s["convw"][:, ft * 4 + kk: ft * 4 + kk + 1],
                           in1=acc, op0=ALU.mult, op1=ALU.add))
            dv = k.emit("vector", lambda e: e.engine_nop())
            k.wait("scalar", "dv", dv)
            acq[ft] = k.emit("scalar", lambda e, ft=ft, acc=acc: e.activation(
                out=mxsc(ft), in_=acc, func=AF.Silu))
        k.wait("tensor", "ac", k.sems["ac"])
        if sc == 0:
            k.tap("mx0", big[:, BG_MX:BG_MX + 24576])

        k.wait("sync", "pe", k.sems["pe"])
        lb = k.dma("sync", s["wbuf"][:, 0:512].rearrange("p (n c) -> p n c", n=16),
                   wba[:, :, 0:32], sem="wba")
        k.wait("tensor", "wba", lb)
        k.wait("tensor", "dv", k.sems["dv"])
        for dt_ in range(nD):
            MM(pb[4][0:16, 0:512], s["wbuf"][:, dt_ * 32:dt_ * 32 + 16],
               nview[:, dt_, :], dt_ == 0, dt_ == nD - 1, pe=(dt_ == nD - 1))
        for dt_ in range(nD):
            MM(pb[5][0:16, 0:512], s["wbuf"][:, dt_ * 32 + 16:dt_ * 32 + 32],
               nview[:, dt_, :], dt_ == 0, dt_ == nD - 1, pe=(dt_ == nD - 1))
        k.wait("vector", "pe", k.sems["pe"])
        k.emit("vector", lambda e: e.tensor_copy(rch[0:16, RC_BA:RC_BA + 512],
                                                 pb[4][0:16, 0:512]))
        k.emit("vector", lambda e: e.tensor_copy(rch[32:48, RC_BA:RC_BA + 512],
                                                 pb[5][0:16, 0:512]))

        bsl = rch[0:16, RC_BA:RC_BA + 512]
        asl = rch[32:48, RC_BA:RC_BA + 512]
        T1 = rch[0:16, RC_T1:RC_T1 + 512]
        T2 = rch[0:16, RC_T1 + 512:RC_T1 + 1024]
        T3 = rch[0:16, RC_T1 + 1024:RC_T1 + 1536]

        k.wait("scalar", "dv", k.sems["dv"])
        k.emit("scalar", lambda e: e.activation(out=rch[0:16, RC_BETA:RC_BETA + 512],
                                                in_=bsl, func=AF.Sigmoid))
        k.emit("scalar", lambda e: e.activation(out=T1, in_=bsl, func=AF.Identity,
                                                scale=-1.0))
        ac = k.emit("scalar", lambda e: e.activation(out=T2, in_=T1, func=AF.Abs))
        k.wait("scalar", "ac", ac)
        k.emit("scalar", lambda e: e.activation(out=T2, in_=T2, func=AF.Exp, scale=-1.0))
        ac = k.emit("scalar", lambda e: e.activation(out=T2, in_=T2, func=AF.Ln, bias=1.0))
        k.wait("vector", "ac", ac)
        k.emit("vector", lambda e: e.tensor_scalar_max(T3, T1, 0.0))
        k.emit("vector", lambda e: e.tensor_add(T3, T3, T2))
        dv = k.emit("vector", lambda e: e.tensor_scalar_mul(
            rch[0:16, RC_LNB:RC_LNB + 512], T3, -1.0))
        k.wait("scalar", "dv", dv)
        acb = k.emit("scalar", lambda e: e.activation(out=T1, in_=asl, func=AF.Identity,
                                                      bias=s["miscT"][0:16, 0:1]))
        k.wait("scalar", "ac", acb)
        ac = k.emit("scalar", lambda e: e.activation(out=T2, in_=T1, func=AF.Abs))
        k.wait("scalar", "ac", ac)
        k.emit("scalar", lambda e: e.activation(out=T2, in_=T2, func=AF.Exp, scale=-1.0))
        ac = k.emit("scalar", lambda e: e.activation(out=T2, in_=T2, func=AF.Ln, bias=1.0))
        k.wait("vector", "ac", ac)
        k.emit("vector", lambda e: e.tensor_scalar_max(T3, T1, 0.0))
        k.emit("vector", lambda e: e.tensor_add(T3, T3, T2))
        k.emit("vector", lambda e: e.tensor_scalar_mul(
            rch[0:16, RC_G:RC_G + 512], T3, s["colsA"][0:16, 254:255]))

        for r in range(8):
            ac = k.emit("scalar", lambda e, r=r: e.activation(
                out=nst[:, NS_CR:NS_CR + 512], in_=mxsc(8 + r), func=AF.Square))
            k.wait("tensor", "ac", ac)
            MM(pb[5][0:16, 0:512], s["selk"][:, r * 16:(r + 1) * 16],
               nst[:, NS_CR:NS_CR + 512], r == 0, False, pe=True)
            k.wait("scalar", "pe", k.sems["pe"])
            ac = k.emit("scalar", lambda e, r=r: e.activation(
                out=nst[:, NS_CR:NS_CR + 512], in_=mxsc(r), func=AF.Square))
            k.wait("tensor", "ac", ac)
            MM(pb[5][0:16, 0:512], s["selq"][:, r * 16:(r + 1) * 16],
               nst[:, NS_CR:NS_CR + 512], False, r == 7, pe=True)
            if r < 7:
                k.wait("scalar", "pe", k.sems["pe"])
        k.wait("scalar", "pe", k.sems["pe"])
        k.emit("scalar", lambda e: e.activation(out=rch[0:16, RC_LNS:RC_LNS + 512],
                                                in_=pb[5][0:16, 0:512], func=AF.Ln,
                                                bias=1e-6))
        ac = k.emit("scalar", lambda e: e.activation(
            out=rch[0:16, RC_LNS:RC_LNS + 512],
            in_=rch[0:16, RC_LNS:RC_LNS + 512], func=AF.Identity, scale=-0.5))
        k.wait("vector", "ac", ac)
        dv = k.emit("vector", lambda e: e.tensor_copy(
            nst[0:8, NS_TB:NS_TB + 512], rch[0:8, RC_LNS:RC_LNS + 512]))
        k.wait("tensor", "dv", dv)
        MM(pb[6][0:16, 0:512], s["exp2"][0:8, 0:16], nst[0:8, NS_TB:NS_TB + 512],
           True, True, pe=True)
        k.wait("vector", "pe", k.sems["pe"])
        k.emit("vector", lambda e: e.tensor_copy(
            rch[0:16, RC_LNS16:RC_LNS16 + 512], pb[6][0:16, 0:512]))

        if sc == 0:
            k.tap("rows0", rch[:, 0:8192])
        for c in range(4):
            emit_dn_chunk(k, sc, c)
            ch = sc * 4 + c
            k.wait("gpsimd", "dv", k.sems["dv"])
            st = k.pdma("gpsimd", [(
                featA[:, 0:16, ch * 128:(ch + 1) * 128],
                big[:, BG_CO:BG_CO + 2048].rearrange("p (n t) -> p n t", n=16),
                featA[:, 16:32, ch * 128:(ch + 1) * 128],
                big[:, BG_CO:BG_CO + 2048].rearrange("p (n t) -> p n t", n=16),
            )], sem="fsh")
            k.wait("vector", "fsh", st)
        pe_sc[sc] = k.sems["pe"]

    k.wait("gpsimd", "fsh", k.sems["fsh"])
    k.flag_write("featA")


def emit_dn_chunk(k, sc, c):
    cfg = k.cfg
    s = k.sbuf
    pb = [s[f"pb{i}"] for i in range(8)]
    big, nst, rch = s["bigb"], s["nst"], s["rch"]
    cA, cB = s["colsA"], s["colsB"]
    LNC = float(np.log(128.0 ** -0.5))
    c0 = c * 128

    def MM(out, lhsT, rhs, start, stop, pe=False):
        return k.emit("tensor", lambda e, out=out, lhsT=lhsT, rhs=rhs,
                      start=start, stop=stop:
                      e.matmul(out, lhsT, rhs, start=start, stop=stop),
                      "pe" if pe else None)

    def mx(ft, w=128):
        return big[:, BG_MX + ft * 512 + c0: BG_MX + ft * 512 + c0 + w]

    G = rch[0:16, RC_CH:RC_CH + 128]
    GLS = rch[0:16, RC_CH + 128:RC_CH + 256]
    NGS = rch[0:16, RC_CH + 256:RC_CH + 384]
    GRT = rch[0:16, RC_CH + 384:RC_CH + 512]
    LNS16c = rch[0:16, RC_LNS16 + c0:RC_LNS16 + c0 + 128]
    SP1 = rch[0:16, RC_SP:RC_SP + 128]
    SP2 = rch[0:16, RC_SP + 128:RC_SP + 256]

    k.emit("vector", lambda e: e.tensor_tensor_scan(
        out=G, data0=rch[0:16, RC_ONES:RC_ONES + 128],
        data1=rch[0:16, RC_G + c0:RC_G + c0 + 128], initial=0.0,
        op0=ALU.mult, op1=ALU.add))
    k.emit("vector", lambda e: e.tensor_add(
        GLS, G, rch[0:16, RC_LNB + c0:RC_LNB + c0 + 128]))
    k.emit("vector", lambda e: e.tensor_add(GLS, GLS, LNS16c))
    k.emit("vector", lambda e: e.scalar_tensor_tensor(
        out=NGS, in0=G, scalar=-1.0, in1=LNS16c, op0=ALU.mult, op1=ALU.add))
    k.emit("vector", lambda e: e.scalar_tensor_tensor(
        out=GRT, in0=G, scalar=-1.0, in1=LNS16c, op0=ALU.mult, op1=ALU.add))
    k.emit("vector", lambda e: e.tensor_scalar_add(GRT, GRT, G[0:16, 127:128]))
    k.emit("vector", lambda e: e.tensor_scalar_mul(
        rch[0:16, RC_GE:RC_GE + 128], rch[0:16, RC_ONES:RC_ONES + 128],
        G[0:16, 127:128]))

    def split3(src, p0, p1, p2):
        (b0, o0), (b1, o1), (b2, o2) = p0, p1, p2
        stage = nst[0:16, NS_CR:NS_CR + 128]
        k.emit("vector", lambda e: e.tensor_copy(stage, src))
        k.emit("vector", lambda e: e.tensor_sub(SP1, src, stage))
        k.emit("vector", lambda e, b0=b0, o0=o0: e.tensor_copy(
            nst[b0:b0 + 16, o0:o0 + 128], stage))
        k.emit("vector", lambda e: e.tensor_copy(stage, SP1))
        k.emit("vector", lambda e: e.tensor_sub(SP2, SP1, stage))
        k.emit("vector", lambda e, b1=b1, o1=o1: e.tensor_copy(
            nst[b1:b1 + 16, o1:o1 + 128], stage))
        return k.emit("vector", lambda e, b2=b2, o2=o2: e.tensor_copy(
            nst[b2:b2 + 16, o2:o2 + 128], SP2))

    split3(G, (0, NS_R), (0, NS_R + 128), (0, NS_R + 256))
    split3(GLS, (0, NS_R + 384), (0, NS_R + 512), (0, NS_R + 640))
    split3(NGS, (0, NS_R + 768), (0, NS_R + 896), (0, NS_R + 1024))
    split3(G, (0, NS_TB), (32, NS_TB), (64, NS_TB))
    split3(GRT, (0, NS_TB + 128), (32, NS_TB + 128), (64, NS_TB + 128))
    k.emit("vector", lambda e: e.tensor_copy(
        nst[0:16, NS_TB + 256:NS_TB + 384], rch[0:16, RC_LNS + c0:RC_LNS + c0 + 128]))
    k.emit("vector", lambda e: e.tensor_copy(
        nst[32:48, NS_TB + 256:NS_TB + 384], rch[0:16, RC_BETA + c0:RC_BETA + c0 + 128]))
    dv = k.emit("vector", lambda e: e.tensor_copy(
        nst[64:80, NS_TB + 256:NS_TB + 384], rch[0:16, RC_GE:RC_GE + 128]))

    k.wait("tensor", "dv", dv)
    for bi in range(3):
        k.emit("tensor", lambda e, bi=bi: e.transpose(
            pb[7][:, bi * 128:bi * 128 + 96],
            nst[0:96, NS_TB + bi * 128:NS_TB + bi * 128 + 128],
            s["ident"][0:96, 0:96]), "pe" if bi == 2 else None)
    k.wait("vector", "pe", k.sems["pe"])
    k.emit("vector", lambda e: e.tensor_copy(cB[:, 0:96], pb[7][:, 0:96]))
    k.emit("vector", lambda e: e.tensor_copy(cB[:, 96:192], pb[7][:, 128:224]))
    dv = k.emit("vector", lambda e: e.tensor_copy(cB[:, 192:288], pb[7][:, 256:352]))
    k.wait("tensor", "dv", dv)

    k.emit("vector", lambda e: e.tensor_copy(cA[:, 208:224], cB[:, 224:240]))
    k.emit("vector", lambda e: e.tensor_add(cA[:, 0:16], cB[:, 0:16], cB[:, 32:48]))
    k.emit("vector", lambda e: e.tensor_add(cA[:, 0:16], cA[:, 0:16], cB[:, 64:80]))
    k.emit("vector", lambda e: e.tensor_add(cA[:, 16:32], cB[:, 96:112], cB[:, 128:144]))
    k.emit("vector", lambda e: e.tensor_add(cA[:, 16:32], cA[:, 16:32], cB[:, 160:176]))
    lnskr = cB[:, 192:200].unsqueeze(2).broadcast_to([128, 8, 2])
    lnsqr = cB[:, 200:208].unsqueeze(2).broadcast_to([128, 8, 2])
    A16 = lambda o: cA[:, o:o + 16].rearrange("p (q r) -> p q r", q=8)
    k.emit("vector", lambda e: e.tensor_add(A16(32), A16(0), lnsqr))
    dv = k.emit("vector", lambda e: e.engine_nop())
    k.wait("scalar", "dv", dv)
    k.emit("scalar", lambda e: e.activation(out=cA[:, 48:64], in_=cA[:, 32:48],
                                            func=AF.Exp, bias=LNC))
    k.emit("scalar", lambda e: e.activation(
        out=cA[:, 64:80].rearrange("p (q r) -> p q r", q=8), in_=lnsqr,
        func=AF.Exp, bias=LNC))
    k.emit("scalar", lambda e: e.activation(out=cA[:, 80:96], in_=cA[:, 16:32],
                                            func=AF.Exp))
    k.emit("vector", lambda e: e.tensor_add(A16(96), A16(0), lnskr))
    dv = k.emit("vector", lambda e: e.engine_nop())
    k.wait("scalar", "dv", dv)
    ac = k.emit("scalar", lambda e: e.activation(out=cA[:, 112:128], in_=cA[:, 96:112],
                                                 func=AF.Exp))
    k.wait("vector", "ac", ac)
    k.emit("vector", lambda e: e.scalar_tensor_tensor(
        out=cA[:, 128:144], in0=cA[:, 112:128], scalar=-1.0, in1=cB[:, 224:240],
        op0=ALU.mult, op1=ALU.mult))
    acg = k.emit("scalar", lambda e: e.activation(out=cA[:, 144:160],
                                                  in_=cB[:, 256:272], func=AF.Exp))

    k.wait("tensor", "ac", acg)
    k.wait("tensor", "dv", k.sems["dv"])
    for kq in range(8):
        bank = pb[0] if kq < 4 else pb[5]
        lane = (kq % 4) * 128
        MM(bank[:, lane:lane + 128], mx(8 + kq), mx(8 + kq), True, True, pe=(kq == 7))
    for kq in range(8):
        bank = pb[1] if kq < 4 else pb[6]
        lane = (kq % 4) * 128
        MM(bank[:, lane:lane + 128], mx(8 + kq), mx(kq), True, True, pe=(kq == 7))
    for kq in range(8):
        k.emit("tensor", lambda e, kq=kq: e.transpose(
            pb[7][:, (kq % 4) * 128:(kq % 4) * 128 + 128], mx(8 + kq), s["ident"][:]),
            "pe" if kq % 4 == 3 else None)
        if kq % 4 == 3:
            k.wait("vector", "pe", k.sems["pe"])
            g0 = (kq // 4) * 4
            dv = k.emit("vector", lambda e, g0=g0: e.tensor_copy(
                big[:, BG_KT + g0 * 128:BG_KT + g0 * 128 + 512], pb[7][:, 0:512]))
            k.wait("tensor", "dv", dv)

    RT = lambda off: nst[0:16, off:off + 128]

    for g in range(2):
        hb = g * 8
        akk = pb[0] if g == 0 else pb[5]
        aqk = pb[1] if g == 0 else pb[6]
        k.wait("vector", "pe", k.sems["pe"])
        dv = k.emit("vector", lambda e, akk=akk: e.tensor_scalar_mul(
            big[:, BG_EA:BG_EA + 512], akk[:, 0:512], -1.0))
        k.wait("tensor", "dv", dv)

        for orient in range(2):
            for hh in range(8):
                h = hb + hh
                bank = pb[2] if hh < 4 else pb[3]
                lane = (hh % 4) * 128
                out = bank[:, lane:lane + 128]
                selh = s["sel"][:, h * 128:(h + 1) * 128]
                ra = NS_R + 384 if orient == 0 else NS_R + 768
                rb = NS_R + 768 if orient == 0 else NS_R + 384
                for t3 in range(3):
                    MM(out, RT(ra + t3 * 128), selh, t3 == 0, False)
                for t3 in range(3):
                    MM(out, selh, RT(rb + t3 * 128), False, False)
                mki = 0 if orient == 0 else 128
                MM(out, s["ident"][:], s["masks"][:, mki:mki + 128], False, True,
                   pe=(hh == 7))
            k.wait("scalar", "pe", k.sems["pe"])
            k.emit("scalar", lambda e: e.activation(out=nst[:, NS_E:NS_E + 512],
                                                    in_=pb[2][:, 0:512], func=AF.Exp))
            ac = k.emit("scalar", lambda e: e.activation(
                out=nst[:, NS_E + 512:NS_E + 1024], in_=pb[3][:, 0:512], func=AF.Exp))
            k.wait("vector", "ac", ac)
            dst = BG_XP if orient == 0 else BG_XT
            ear = big[:, BG_EA:BG_EA + 512].rearrange(
                "p (q t) -> p q t", q=4).unsqueeze(2).broadcast_to([128, 4, 2, 128])
            dv = k.emit("vector", lambda e, dst=dst, ear=ear: e.tensor_mul(
                big[:, dst:dst + 1024].rearrange("p (q r t) -> p q r t", q=4, r=2),
                nst[:, NS_E:NS_E + 1024].rearrange("p (q r t) -> p q r t", q=4, r=2),
                ear))
            k.wait("tensor", "dv", dv)

        for hh in range(8):
            h = hb + hh
            bank = pb[2] if hh < 4 else pb[3]
            lane = (hh % 4) * 128
            out = bank[:, lane:lane + 128]
            selh = s["sel"][:, h * 128:(h + 1) * 128]
            for t3 in range(3):
                MM(out, selh, RT(NS_R + t3 * 128), t3 == 0, False)
            for t3 in range(3):
                MM(out, RT(NS_R + 768 + t3 * 128), selh, False, False)
            MM(out, s["ident"][:], s["masks"][:, 256:384], False, True, pe=(hh == 7))
        k.wait("scalar", "pe", k.sems["pe"])
        k.emit("scalar", lambda e: e.activation(out=nst[:, NS_E:NS_E + 512],
                                                in_=pb[2][:, 0:512], func=AF.Exp))
        ac = k.emit("scalar", lambda e: e.activation(
            out=nst[:, NS_E + 512:NS_E + 1024], in_=pb[3][:, 0:512], func=AF.Exp))
        k.wait("vector", "ac", ac)
        aqr = aqk[:, 0:512].rearrange("p (q t) -> p q t", q=4).unsqueeze(2) \
            .broadcast_to([128, 4, 2, 128])
        dv = k.emit("vector", lambda e, aqr=aqr: e.tensor_mul(
            nst[:, NS_PT:NS_PT + 1024].rearrange("p (q r t) -> p q r t", q=4, r=2),
            nst[:, NS_E:NS_E + 1024].rearrange("p (q r t) -> p q r t", q=4, r=2),
            aqr))
        k.wait("tensor", "dv", dv)

        for qh in range(2):
            for i4 in range(4):
                hh = qh * 4 + i4
                h = hb + hh
                k.emit("tensor", lambda e, i4=i4, h=h: e.transpose(
                    pb[7][:, i4 * 128:i4 * 128 + 128], mx(16 + h),
                    s["ident"][:]), "pe" if i4 == 3 else None)
            k.wait("vector", "pe", k.sems["pe"])
            for i4 in range(4):
                h = hb + qh * 4 + i4
                k.emit("vector", lambda e, i4=i4, h=h: e.tensor_scalar_mul(
                    nst[:, NS_BV + i4 * 128:NS_BV + (i4 + 1) * 128],
                    pb[7][:, i4 * 128:i4 * 128 + 128], cA[:, 208 + h:209 + h]))
            dv = k.emit("vector", lambda e: e.engine_nop())
            k.wait("tensor", "dv", dv)
            for i4 in range(4):
                hh = qh * 4 + i4
                h = hb + hh
                kqt = mx(8 + g * 4 + hh // 2)
                MM(pb[4][:, i4 * 128:i4 * 128 + 128], kqt,
                   big[:, BG_SH + h * 128:BG_SH + (h + 1) * 128], True, False)
                MM(pb[4][:, i4 * 128:i4 * 128 + 128], kqt,
                   big[:, BG_SL + h * 128:BG_SL + (h + 1) * 128], False, True,
                   pe=(i4 == 3))
            k.wait("vector", "pe", k.sems["pe"])
            for i4 in range(4):
                hh = qh * 4 + i4
                h = hb + hh
                k.emit("vector", lambda e, i4=i4, hh=hh, h=h: e.scalar_tensor_tensor(
                    out=big[:, BG_UU + hh * 128:BG_UU + (hh + 1) * 128],
                    in0=pb[4][:, i4 * 128:i4 * 128 + 128],
                    scalar=cA[:, 128 + h:129 + h],
                    in1=nst[:, NS_BV + i4 * 128:NS_BV + (i4 + 1) * 128],
                    op0=ALU.mult, op1=ALU.add))
            dv = k.emit("vector", lambda e: e.engine_nop())
            k.wait("tensor", "dv", dv)

        for j in range(7):
            for hh in range(8):
                bank = pb[2] if hh < 4 else pb[3]
                lane = (hh % 4) * 128
                MM(bank[:, lane:lane + 128],
                   big[:, BG_XT + hh * 128:BG_XT + (hh + 1) * 128],
                   big[:, BG_UU + hh * 128:BG_UU + (hh + 1) * 128], True, False)
                MM(bank[:, lane:lane + 128], s["ident"][:],
                   big[:, BG_UU + hh * 128:BG_UU + (hh + 1) * 128], False, True,
                   pe=(hh == 7))
            k.wait("vector", "pe", k.sems["pe"])
            k.emit("vector", lambda e: e.tensor_copy(
                big[:, BG_UU:BG_UU + 512], pb[2][:, 0:512]))
            dv = k.emit("vector", lambda e: e.tensor_copy(
                big[:, BG_UU + 512:BG_UU + 1024], pb[3][:, 0:512]))
            k.wait("tensor", "dv", dv)
            if j < 6:
                for hh in range(8):
                    bank = pb[4] if hh < 4 else pb[0]
                    lane = (hh % 4) * 128
                    MM(bank[:, lane:lane + 128],
                       big[:, BG_XT + hh * 128:BG_XT + (hh + 1) * 128],
                       big[:, BG_XP + hh * 128:BG_XP + (hh + 1) * 128], True, True,
                       pe=(hh == 7))
                for hh in range(8):
                    bank = pb[2] if hh < 4 else pb[3]
                    lane = (hh % 4) * 128
                    MM(bank[:, lane:lane + 128],
                       big[:, BG_XP + hh * 128:BG_XP + (hh + 1) * 128],
                       big[:, BG_XT + hh * 128:BG_XT + (hh + 1) * 128], True, True,
                       pe=(hh == 7))
                k.wait("vector", "pe", k.sems["pe"])
                k.emit("vector", lambda e: e.tensor_copy(
                    big[:, BG_XP:BG_XP + 512], pb[4][:, 0:512]))
                k.emit("vector", lambda e: e.tensor_copy(
                    big[:, BG_XP + 512:BG_XP + 1024], pb[0][:, 0:512]))
                k.emit("vector", lambda e: e.tensor_copy(
                    big[:, BG_XT:BG_XT + 512], pb[2][:, 0:512]))
                dv = k.emit("vector", lambda e: e.tensor_copy(
                    big[:, BG_XT + 512:BG_XT + 1024], pb[3][:, 0:512]))
                k.wait("tensor", "dv", dv)

        for hh in range(8):
            h = hb + hh
            lane = (hh % 4) * 128
            vsl = BG_VT + (hh % 2) * 128
            MM(pb[2][:, lane:lane + 128], nst[:, NS_PT + hh * 128:NS_PT + (hh + 1) * 128],
               big[:, BG_UU + hh * 128:BG_UU + (hh + 1) * 128], True, True)
            MM(pb[3][:, lane:lane + 128], mx(g * 4 + hh // 2),
               big[:, BG_SH + h * 128:BG_SH + (h + 1) * 128], True, False)
            MM(pb[3][:, lane:lane + 128], mx(g * 4 + hh // 2),
               big[:, BG_SL + h * 128:BG_SL + (h + 1) * 128], False, True, pe=True)
            k.wait("vector", "pe", k.sems["pe"])
            k.emit("vector", lambda e, lane=lane, h=h: e.tensor_scalar_mul(
                s["f32s"][:, lane:lane + 128], pb[3][:, lane:lane + 128],
                cA[:, 48 + h:49 + h]))
            dv = k.emit("vector", lambda e, lane=lane, h=h, vsl=vsl: e.scalar_tensor_tensor(
                out=big[:, vsl:vsl + 128], in0=pb[2][:, lane:lane + 128],
                scalar=cA[:, 64 + h:65 + h], in1=s["f32s"][:, lane:lane + 128],
                op0=ALU.mult, op1=ALU.add))
            k.wait("scalar", "dv", dv)
            ac = k.emit("scalar", lambda e, vsl=vsl, h=h, lane=lane: e.activation(
                out=s["f32s"][:, lane:lane + 128], in_=big[:, vsl:vsl + 128],
                func=AF.Square, accum_out=cA[:, 160 + h:161 + h]))
            k.wait("scalar", "ac", ac)
            k.emit("scalar", lambda e, h=h: e.activation(
                out=cA[:, 176 + h:177 + h], in_=cA[:, 160 + h:161 + h], func=AF.Ln,
                scale=0.0078125, bias=1e-6))
            ac = k.emit("scalar", lambda e, h=h: e.activation(
                out=cA[:, 176 + h:177 + h], in_=cA[:, 176 + h:177 + h], func=AF.Exp,
                scale=-0.5))
            k.wait("vector", "ac", ac)
            osl = BG_VT + ((hh % 2) ^ 1) * 128
            dv = k.emit("vector", lambda e, vsl=vsl, osl=osl, h=h: e.tensor_scalar_mul(
                big[:, osl:osl + 128], big[:, vsl:vsl + 128], cA[:, 176 + h:177 + h]))
            k.wait("tensor", "dv", dv)
            k.emit("tensor", lambda e, osl=osl, lane=lane: e.transpose(
                pb[7][:, lane:lane + 128], big[:, osl:osl + 128], s["ident"][:]),
                "pe")
            k.wait("vector", "pe", k.sems["pe"])
            dv = k.emit("vector", lambda e, lane=lane, h=h: e.tensor_mul(
                big[:, BG_CO + h * 128:BG_CO + (h + 1) * 128],
                pb[7][:, lane:lane + 128], mx(32 + h)))
            k.emit("vector", lambda e, h=h: e.tensor_scalar_mul(
                nst[:, NS_KP:NS_KP + 128],
                big[:, BG_KT + (h // 2) * 128:BG_KT + (h // 2) * 128 + 128],
                cA[:, 80 + h:81 + h]))
            dv = k.emit("vector", lambda e, h=h: e.tensor_scalar_mul(
                nst[:, NS_KP + 128:NS_KP + 256], s["ident"][:], cA[:, 144 + h:145 + h]))
            k.wait("tensor", "dv", dv)
            MM(pb[4][:, lane:lane + 128], nst[:, NS_KP:NS_KP + 128],
               big[:, BG_UU + hh * 128:BG_UU + (hh + 1) * 128], True, False)
            MM(pb[4][:, lane:lane + 128], nst[:, NS_KP + 128:NS_KP + 256],
               big[:, BG_SH + h * 128:BG_SH + (h + 1) * 128], False, False)
            MM(pb[4][:, lane:lane + 128], nst[:, NS_KP + 128:NS_KP + 256],
               big[:, BG_SL + h * 128:BG_SL + (h + 1) * 128], False, True, pe=True)
            k.wait("vector", "pe", k.sems["pe"])
            k.emit("vector", lambda e, lane=lane, h=h: e.tensor_copy(
                big[:, BG_SH + h * 128:BG_SH + (h + 1) * 128], pb[4][:, lane:lane + 128]))
            dv = k.emit("vector", lambda e, lane=lane, h=h: e.tensor_sub(
                big[:, BG_SL + h * 128:BG_SL + (h + 1) * 128],
                pb[4][:, lane:lane + 128],
                big[:, BG_SH + h * 128:BG_SH + (h + 1) * 128]))
            k.wait("tensor", "dv", dv)


def phase_wout(k, featname, wname, nvt, pollflag, outflag):
    cfg = k.cfg
    s, d = k.sbuf, k.dram
    pb = [s[f"pb{i}"] for i in range(8)]
    big = s["bigb"]
    nw = cfg.TCH
    feat = dview(d[featname])
    wo = d[wname].ap().rearrange("(v p) d -> p v d", p=128)
    shn = dview(d["sh_n1"])
    shr = dview(d["sh_nrs"])

    def MM(out, lhsT, rhs, start, stop, pe=False):
        return k.emit("tensor", lambda e, out=out, lhsT=lhsT, rhs=rhs,
                      start=start, stop=stop:
                      e.matmul(out, lhsT, rhs, start=start, stop=stop),
                      "pe" if pe else None)

    k.flag_poll(pollflag, "xs")
    k.wait("sync", "xs", k.sems["xs"])

    for co in range(cfg.nTCo):
        k.wait("sync", "rwb", k.sems.get("rwb", 0))
        lr = k.dma("sync", s["rch"][:].rearrange("p (n w) -> p n w", n=16),
                   dview(d["resid"])[:, :, co * nw:(co + 1) * nw], sem="lnld")
        fa = k.pdma("sync", [(
            big[:, 0:nvt * 512].rearrange("p (n w) -> p n w", n=nvt),
            feat[:, :, co * nw:(co + 1) * nw],
            big[:, 0:nvt * 512].rearrange("p (n w) -> p n w", n=nvt),
            feat[:, :, cfg.To + co * nw:cfg.To + (co + 1) * nw],
        )], sem="fld")
        k.wait("tensor", "fld", fa)
        k.wait("vector", "lnld", lr)
        pes_dt = {}
        for dt_ in range(cfg.nD):
            wsl = dt_ % 3
            wq_ = ("sync", "scalar")[wsl % 2]
            if dt_ >= 3:
                k.wait(wq_, "pe", pes_dt[dt_ - 3])
            elif co > 0 and dt_ <= 1:
                k.wait(wq_, "pe", k.sems["pe"])
            lw = k.dma(wq_,
                       s["wbuf"][:, wsl * 4096:wsl * 4096 + nvt * 128].rearrange(
                           "p (v c) -> p v c", v=nvt),
                       wo[:, :, dt_ * 128:(dt_ + 1) * 128], sem=f"wo{wsl}")
            k.wait("tensor", f"wo{wsl}", lw)
            pbank = pb[dt_ % 2]
            if dt_ >= 2:
                k.wait("tensor", "dv", k.sems["dv"])
            for vt in range(nvt):
                pes_dt[dt_] = MM(
                    pbank[:, 0:512],
                    s["wbuf"][:, wsl * 4096 + vt * 128:wsl * 4096 + (vt + 1) * 128],
                    big[:, vt * 512:(vt + 1) * 512], vt == 0, vt == nvt - 1,
                    pe=(vt == nvt - 1))
            k.wait("vector", "pe", k.sems["pe"])
            k.emit("vector", lambda e, dt_=dt_, pbank=pbank: e.tensor_add(
                s["rch"][:, dt_ * nw:(dt_ + 1) * nw],
                s["rch"][:, dt_ * nw:(dt_ + 1) * nw], pbank[:, 0:512]))
        dv = k.emit("vector", lambda e: e.engine_nop())
        k.wait("scalar", "dv", dv)
        k.wait("tensor", "dv", dv)
        rw = k.dma("scalar", dview(d["resid"])[:, :, co * nw:(co + 1) * nw],
                   s["rch"][:].rearrange("p (n w) -> p n w", n=16), sem="rwb")
        dv = emit_ln_chunk(k, nw, with_rs=True)
        k.wait("gpsimd", "dv", dv)
        st = k.pdma("gpsimd", [(
            shn[:, :, co * nw:(co + 1) * nw],
            s["nst"][:].rearrange("p (n w) -> p n w", n=16),
            shn[:, :, cfg.To + co * nw:cfg.To + (co + 1) * nw],
            s["nst"][:].rearrange("p (n w) -> p n w", n=16),
        )], sem="nsh")
        dv = emit_nrs_tiles(k, nw, lambda t: big[:, 16384 + t * nw:16384 + (t + 1) * nw])
        k.wait("gpsimd", "dv", dv)
        st = k.pdma("gpsimd", [(
            shr[:, :, co * nw:(co + 1) * nw],
            big[:, 16384:24576].rearrange("p (n w) -> p n w", n=16),
            shr[:, :, cfg.To + co * nw:cfg.To + (co + 1) * nw],
            big[:, 16384:24576].rearrange("p (n w) -> p n w", n=16),
        )], sem="nsh")
        k.wait("vector", "nsh", st)
        k.wait("sync", "nsh", st)
    k.wait("sync", "rwb", k.sems["rwb"])
    k.wait("gpsimd", "nsh", k.sems["nsh"])
    k.flag_write(outflag)




def phase_ffn(k, ci):
    cfg = k.cfg
    s, d = k.sbuf, k.dram
    pb = [s[f"pb{i}"] for i in range(8)]
    big, nst, rch = s["bigb"], s["nst"], s["rch"]
    T, To, nD = cfg.T, cfg.To, cfg.nD
    nQC = T // 512
    KR, VR = 0, 2 * T
    QS, PS, OS, QI = 4 * T, 4 * T + 1024, 4 * T + 2048, 4 * T + 2560
    RSQ = float(128 ** -0.5)

    def MM(out, lhsT, rhs, start, stop, pe=False):
        return k.emit("tensor", lambda e, out=out, lhsT=lhsT, rhs=rhs,
                      start=start, stop=stop:
                      e.matmul(out, lhsT, rhs, start=start, stop=stop),
                      "pe" if pe else None)

    woav = big[:, 28672:36864].bitcast(F32)
    shr = dview(d["sh_nrs"])
    shn1 = dview(d["sh_n1"])
    wqv, wkv, wvv = dview(d["fw_q"]), dview(d["fw_k"]), dview(d["fw_v"])
    qsc = dview(d["qT_scr"])
    woa_rows = d["fw_woa"].ap()

    if ci == 0:
        k.tap("dp1", s["bdpc"][:])
    k.flag_poll(f"n1_f{ci + 1}", "xs")
    k.wait("sync", "xs", k.sems["xs"])
    k.emit("vector", lambda e: e.memset(woav[0:16, 0:T], 0.0))

    for r in range(4):
        k.wait("sync", "pe", k.sems["pe"])
        lw = k.dma("sync", s["wbuf"][:, 0:4096].rearrange("p (n c) -> p n c", n=16),
                   wqv[:, :, r * 256:(r + 1) * 256], sem="wa0")
        lw = k.dma("sync", s["wbuf"][:, 4096:8192].rearrange("p (n c) -> p n c", n=16),
                   wkv[:, :, r * 256:(r + 1) * 256], sem="wa0")
        lw = k.dma("sync", s["wbuf"][:, 8192:12288].rearrange("p (n c) -> p n c", n=16),
                   wvv[:, :, r * 256:(r + 1) * 256], sem="wa0")
        cbv = s["colsB"][:, 0:32].rearrange("p (n c) -> p n c", n=2)
        wrv = woa_rows.rearrange("(n p) c -> p n c", p=128)
        fa = k.pdma("sync", [(
            cbv, wrv[:, 2 * r:2 * r + 2, :],
            cbv, wrv[:, 8 + 2 * r:8 + 2 * r + 2, :])], sem="fld")
        k.wait("tensor", "wa0", lw)
        k.wait("tensor", "fld", fa)
        k.emit("vector", lambda e: e.memset(s["rows"][0:1, 16:20], 0.0))

        pe_tc = {}
        for tc in range(cfg.nTCf):
            slot = tc % 2
            nview = s["nch"][:, slot * 8192: slot * 8192 + 8192].rearrange(
                "p (n w) -> p n w", n=16)
            if tc >= 2:
                k.wait("sync", "pe", pe_tc[tc - 2])
            ln = k.dma("sync", nview, shr[:, :, tc * 512:(tc + 1) * 512],
                       sem=f"na{slot}")
            k.wait("tensor", f"na{slot}", ln)
            k.wait("tensor", "dv", k.sems["dv"])
            k.wait("tensor", "ac", k.sems["ac"])
            for h2 in range(2):
                for dt_ in range(nD):
                    MM(pb[h2][:, 0:512],
                       s["wbuf"][:, 4096 + dt_ * 256 + h2 * 128:
                                 4096 + dt_ * 256 + (h2 + 1) * 128],
                       nview[:, dt_, :], dt_ == 0, dt_ == nD - 1, pe=(dt_ == nD - 1))
                k.wait("vector", "pe", k.sems["pe"])
                k.emit("vector", lambda e, h2=h2, tc=tc: e.tensor_copy(
                    big[:, KR + h2 * T + tc * 512: KR + h2 * T + (tc + 1) * 512],
                    pb[h2][:, 0:512]))
                for dt_ in range(nD):
                    MM(pb[2 + h2][:, 0:512],
                       s["wbuf"][:, dt_ * 256 + h2 * 128: dt_ * 256 + (h2 + 1) * 128],
                       nview[:, dt_, :], dt_ == 0, dt_ == nD - 1, pe=(dt_ == nD - 1))
                k.wait("vector", "pe", k.sems["pe"])
                k.wait("vector", f"qs{h2}", k.sems.get(f"qs{h2}", 0))
                dv = k.emit("vector", lambda e, h2=h2: e.tensor_copy(
                    big[:, QS + h2 * 512: QS + (h2 + 1) * 512], pb[2 + h2][:, 0:512]))
                k.wait("scalar", "dv", dv)
                ac = k.emit("scalar", lambda e, h2=h2: e.activation(
                    out=nst[:, 0:512], in_=big[:, QS + h2 * 512: QS + (h2 + 1) * 512],
                    func=AF.Square))
                k.wait("tensor", "ac", ac)
                pe = MM(pb[6][0:1, 0:512], s["ones"][:, 0:1], nst[:, 0:512],
                        True, True, pe=True)
                k.wait("vector", "pe", pe)
                k.emit("vector", lambda e: e.reduce_max(
                    s["rows"][0:1, 24:25], pb[6][0:1, 0:512], AX.X))
                k.emit("vector", lambda e, h2=h2: e.tensor_max(
                    s["rows"][0:1, 16 + h2:17 + h2], s["rows"][0:1, 16 + h2:17 + h2],
                    s["rows"][0:1, 24:25]))
                k.wait("gpsimd", "dv", dv)
                k.dma("gpsimd", qsc[:, 2 * r + h2, tc * 512:(tc + 1) * 512],
                      big[:, QS + h2 * 512: QS + (h2 + 1) * 512], sem=f"qs{h2}")
            for vp in range(2):
                bank = pb[4 + vp]
                for vt in (2 * vp, 2 * vp + 1):
                    off = (vt % 2) * 256
                    for dt_ in range(nD):
                        MM(bank[:, off:off + 256],
                           nview[:, dt_, vt * 128:(vt + 1) * 128],
                           s["wbuf"][:, 8192 + dt_ * 256: 8192 + (dt_ + 1) * 256],
                           dt_ == 0, dt_ == nD - 1,
                           pe=(dt_ == nD - 1 and vt == 2 * vp + 1))
                k.wait("vector", "pe", k.sems["pe"])
                for vt in (2 * vp, 2 * vp + 1):
                    off = (vt % 2) * 256
                    for h2 in range(2):
                        k.emit("vector",
                               lambda e, h2=h2, vt=vt, tc=tc, bank=bank, off=off:
                               e.tensor_copy(
                                   big[:, VR + h2 * T + (tc * 4 + vt) * 128:
                                       VR + h2 * T + (tc * 4 + vt + 1) * 128],
                                   bank[:, off + h2 * 128: off + (h2 + 1) * 128]))
            pe_tc[tc] = k.sems["pe"]

        for h2 in range(2):
            for tc in range(cfg.nTCf):
                k.wait("scalar", "dv", k.sems["dv"])
                ac = k.emit("scalar", lambda e, h2=h2, tc=tc: e.activation(
                    out=nst[:, 0:512],
                    in_=big[:, KR + h2 * T + tc * 512: KR + h2 * T + (tc + 1) * 512],
                    func=AF.Square))
                k.wait("tensor", "ac", ac)
                pe = MM(pb[6][0:1, 0:512], s["ones"][:, 0:1], nst[:, 0:512],
                        True, True, pe=True)
                k.wait("vector", "pe", pe)
                k.emit("vector", lambda e: e.reduce_max(
                    s["rows"][0:1, 24:25], pb[6][0:1, 0:512], AX.X))
                k.emit("vector", lambda e, h2=h2: e.tensor_max(
                    s["rows"][0:1, 18 + h2:19 + h2], s["rows"][0:1, 18 + h2:19 + h2],
                    s["rows"][0:1, 24:25]))
        for h2 in range(2):
            dv = k.emit("vector", lambda e, h2=h2: e.tensor_mul(
                s["rows"][0:1, 24:25], s["rows"][0:1, 16 + h2:17 + h2],
                s["rows"][0:1, 18 + h2:19 + h2]))
            k.wait("scalar", "dv", dv)
            k.emit("scalar", lambda e: e.activation(
                out=s["rows"][0:1, 24:25], in_=s["rows"][0:1, 24:25], func=AF.Ln))
            ac = k.emit("scalar", lambda e: e.activation(
                out=s["rows"][0:1, 24:25], in_=s["rows"][0:1, 24:25], func=AF.Exp,
                scale=0.5))
            k.wait("vector", "ac", ac)
            dv = k.emit("vector", lambda e: e.tensor_copy(
                s["rows2"][0:1, 0:1], s["rows"][0:1, 24:25]))
            k.wait("tensor", "dv", dv)
            pe = MM(pb[6][:, 0:1], s["ones"][0:1, :], s["rows2"][0:1, 0:1],
                    True, True, pe=True)
            k.wait("vector", "pe", pe)
            k.emit("vector", lambda e, h2=h2: e.tensor_scalar_mul(
                s["colsA"][:, h2:h2 + 1], pb[6][:, 0:1], -RSQ))

        k.wait("sync", "qs0", k.sems.get("qs0", 0))
        k.wait("sync", "qs1", k.sems.get("qs1", 0))
        for qc in range(nQC):
            nkt = 4 * qc + 4
            for h2 in range(2):
                k.wait("sync", "pe", k.sems["pe"])
                lq = k.dma("sync", big[:, QI + h2 * 512: QI + (h2 + 1) * 512],
                           qsc[:, 2 * r + h2, qc * 512:(qc + 1) * 512], sem=f"qi{h2}")
                k.wait("tensor", f"qi{h2}", lq)
                k.wait("tensor", "dv", k.sems["dv"])
                acq = {}
                pes = {}
                for kt in range(nkt):
                    bank = pb[kt % 2]
                    if kt >= 2:
                        k.wait("tensor", "ac", acq[kt - 2])
                    r_off = kt - 4 * qc
                    if r_off >= 0:
                        MM(bank[:, 0:512], big[:, KR + h2 * T + kt * 128:
                                               KR + h2 * T + (kt + 1) * 128],
                           big[:, QI + h2 * 512: QI + (h2 + 1) * 512], True, False)
                        pes[kt] = MM(bank[:, 0:512], s["ident"][:],
                                     s["cmask"][:, 384 - r_off * 128:896 - r_off * 128],
                                     False, True, pe=True)
                    else:
                        pes[kt] = MM(bank[:, 0:512], big[:, KR + h2 * T + kt * 128:
                                                         KR + h2 * T + (kt + 1) * 128],
                                     big[:, QI + h2 * 512: QI + (h2 + 1) * 512],
                                     True, True, pe=True)
                    if kt >= 1:
                        k.wait("tensor", "ac", acq[kt - 1])
                        MM(pb[3][0:1, 0:512], s["ones"][:, 0:1],
                           big[:, PS + ((kt - 1) % 2) * 512: PS + ((kt - 1) % 2 + 1) * 512],
                           kt - 1 == 0, False)
                        MM(pb[2][:, 0:512], big[:, VR + h2 * T + (kt - 1) * 128:
                                                VR + h2 * T + kt * 128],
                           big[:, PS + ((kt - 1) % 2) * 512: PS + ((kt - 1) % 2 + 1) * 512],
                           kt - 1 == 0, False)
                    k.wait("scalar", "pe", pes[kt])
                    acq[kt] = k.emit("scalar", lambda e, kt=kt, bank=bank, h2=h2:
                                     e.activation(
                                         out=big[:, PS + (kt % 2) * 512:
                                                 PS + (kt % 2 + 1) * 512],
                                         in_=bank[:, 0:512], func=AF.Exp,
                                         bias=s["colsA"][:, h2:h2 + 1], scale=RSQ))
                kt = nkt - 1
                k.wait("tensor", "ac", acq[kt])
                MM(pb[3][0:1, 0:512], s["ones"][:, 0:1],
                   big[:, PS + (kt % 2) * 512: PS + (kt % 2 + 1) * 512],
                   kt == 0, True)
                pvs = MM(pb[2][:, 0:512], big[:, VR + h2 * T + kt * 128:
                                              VR + h2 * T + (kt + 1) * 128],
                         big[:, PS + (kt % 2) * 512: PS + (kt % 2 + 1) * 512],
                         kt == 0, True, pe=True)
                k.wait("vector", "pe", pvs)
                dv = k.emit("vector", lambda e: e.reciprocal(
                    s["rows"][0:1, 512:1024], pb[3][0:1, 0:512]))
                k.emit("vector", lambda e: e.tensor_copy(
                    s["rows2"][0:1, 0:512], s["rows"][0:1, 512:1024]))
                dv = k.emit("vector", lambda e: e.engine_nop())
                k.wait("tensor", "dv", dv)
                pe = MM(pb[5][:, 0:512], s["ones"][0:1, :], s["rows2"][0:1, 0:512],
                        True, True, pe=True)
                k.wait("vector", "pe", pe)
                k.emit("vector", lambda e: e.tensor_copy(s["f32s"][:, 0:512],
                                                         pb[5][:, 0:512]))
                dv = k.emit("vector", lambda e: e.tensor_mul(
                    big[:, OS:OS + 512], pb[2][:, 0:512], s["f32s"][:, 0:512]))
                k.wait("tensor", "dv", dv)
                was = MM(pb[4][0:16, 0:512], s["colsB"][:, h2 * 16:(h2 + 1) * 16],
                         big[:, OS:OS + 512], h2 == 0, h2 == 1, pe=(h2 == 1))
                if h2 == 1:
                    k.wait("vector", "pe", was)
                    k.emit("vector", lambda e, qc=qc: e.tensor_add(
                        woav[0:16, qc * 512:(qc + 1) * 512],
                        woav[0:16, qc * 512:(qc + 1) * 512], pb[4][0:16, 0:512]))

    k.wait("gpsimd", "dv", k.sems["dv"])
    shw = d["sh_woa"].ap()
    st = k.pdma("gpsimd", [(
        shw[0:1, :].rearrange("o (r t) -> (o r) t", r=16), woav[0:16, 0:T],
        shw[1:2, :].rearrange("o (r t) -> (o r) t", r=16), woav[0:16, 0:T])],
        sem="fsh")
    k.wait("gpsimd", "fsh", st)
    k.flag_write(f"woa{ci}")
    k.flag_poll(f"woa{ci}", "xs")
    k.wait("gpsimd", "xs", k.sems["xs"])
    w0 = shw[0:1, :].rearrange("o (r t) -> (o r) t", r=16)
    w1 = shw[1:2, :].rearrange("o (r t) -> (o r) t", r=16)
    k.wait("gpsimd", "rwb", k.sems.get("rwb", 0))
    ld = k.pdma("gpsimd", [
        (rch[0:16, 0:To], w0[:, 0:To], rch[0:16, 0:To], w0[:, To:T]),
        (rch[0:16, 8192 - To:8192], w1[:, 0:To], rch[0:16, 8192 - To:8192], w1[:, To:T]),
    ], sem="gld")
    k.wait("vector", "gld", ld)
    dv = k.emit("vector", lambda e: e.tensor_add(
        rch[0:16, 0:To], rch[0:16, 0:To], rch[0:16, 8192 - To:8192]))
    k.wait("scalar", "dv", dv)
    ac = k.emit("scalar", lambda e: e.activation(
        out=rch[0:16, 8192 - To:8192], in_=rch[0:16, 0:To], func=AF.Tanh))
    k.wait("vector", "ac", ac)
    dvm = k.emit("vector", lambda e: e.tensor_scalar_mul(
        s["sel"][0:16, 0:To], rch[0:16, 8192 - To:8192], 0.1))
    if ci == 0:
        k.tap("mod", s["sel"][0:16, 0:To])

    k.wait("sync", "pe", k.sems["pe"])
    la = k.dma("sync", s["colsB"][:, 32:288].rearrange("p (n c) -> p n c", n=16),
               dview(d["fw_A"]), sem="wa0")
    wiv = dview(d["fw_in"])
    wuv = dview(d["fw_up"])
    wdv = d["fw_down"].ap().rearrange("(ft p) c -> p ft c", p=128)
    resv = dview(d["resid"])
    shn0 = dview(d["sh_n0"])
    for co in range(cfg.nTCo):
        nchv = s["nch"][:, 0:8192].rearrange("p (n w) -> p n w", n=16)
        if co == 0:
            k.wait("sync", "dv", dvm)
        k.wait("sync", "nsh", k.sems.get("nsh", 0))
        k.wait("sync", "rwb", k.sems.get("rwb", 0))
        k.wait("sync", "oput", k.sems.get("oput", 0))
        k.wait("sync", "pe", k.sems["pe"])
        k.wait("scalar", "pe", k.sems["pe"])
        ln = k.pdma("sync", [(
            nchv, shr[:, :, co * 512:(co + 1) * 512],
            nchv, shr[:, :, To + co * 512:To + (co + 1) * 512])], sem="na0")
        lr = k.dma("sync", rch[:].rearrange("p (n w) -> p n w", n=16),
                   resv[:, :, co * 512:(co + 1) * 512], sem="lnld")
        lb = k.dma("sync", nst[0:16, 0:8192], d["fw_B"].ap(), sem="wba")
        k.wait("tensor", "na0", ln)
        k.wait("vector", "lnld", lr)
        k.wait("tensor", "wa0", k.sems["wa0"])
        k.wait("tensor", "dv", k.sems["dv"])
        k.wait("tensor", "ac", k.sems["ac"])
        for dt_ in range(nD):
            MM(pb[0][0:16, 0:512], s["colsB"][:, 32 + dt_ * 16: 32 + (dt_ + 1) * 16],
               nchv[:, dt_, :], dt_ == 0, dt_ == nD - 1, pe=(dt_ == nD - 1))
        k.wait("vector", "pe", k.sems["pe"])
        dvh = k.emit("vector", lambda e, co=co: e.tensor_mul(
            s["wbuf"][0:16, 9216:9728], pb[0][0:16, 0:512],
            s["sel"][0:16, co * 512:(co + 1) * 512]))
        k.wait("tensor", "dv", dvh)
        k.wait("tensor", "wba", lb)
        ld0 = k.dma("sync", s["nch"][:, 8192:16384].rearrange("p (n c) -> p n c", n=64),
                    wdv[:, :, 0:128], sem="ws0")
        ac_silu = {}
        dv_gu = {}
        pe_up = {}
        for f in range(64):
            sl = f % 2
            wq_ = "sync"
            if f >= 2:
                k.wait(wq_, "pe", pe_up[f - 2])
            lw = k.dma(wq_, s["wbuf"][:, sl * 4096: sl * 4096 + 2048]
                       .rearrange("p (n c) -> p n c", n=16),
                       wiv[:, :, f * 128:(f + 1) * 128], sem=f"wm{sl}")
            lw = k.dma(wq_, s["wbuf"][:, sl * 4096 + 2048: sl * 4096 + 4096]
                       .rearrange("p (n c) -> p n c", n=16),
                       wuv[:, :, f * 128:(f + 1) * 128], sem=f"wm{sl}")
            k.wait("tensor", f"wm{sl}", lw)
            gbank = pb[1 + sl]
            if f >= 2:
                k.wait("tensor", "ac", ac_silu[f - 2])
            for dt_ in range(nD):
                MM(gbank[:, 0:512], s["wbuf"][:, sl * 4096 + dt_ * 128:
                                              sl * 4096 + (dt_ + 1) * 128],
                   nchv[:, dt_, :], dt_ == 0, False)
            peg = MM(gbank[:, 0:512], nst[0:16, f * 128:(f + 1) * 128],
                     s["wbuf"][0:16, 9216:9728], False, True, pe=True)
            k.wait("scalar", "pe", peg)
            if f >= 2:
                k.wait("scalar", "dv", dv_gu[f - 2])
            ac_silu[f] = k.emit("scalar", lambda e, sl=sl, gbank=gbank: e.activation(
                out=s["wbuf"][:, 8192 + sl * 512: 8192 + (sl + 1) * 512],
                in_=gbank[:, 0:512], func=AF.Silu))
            if f >= 1:
                k.wait("tensor", "dv", dv_gu[f - 1])
            for dt_ in range(nD):
                MM(pb[3][:, 0:512], s["wbuf"][:, sl * 4096 + 2048 + dt_ * 128:
                                              sl * 4096 + 2048 + (dt_ + 1) * 128],
                   nchv[:, dt_, :], dt_ == 0, dt_ == nD - 1, pe=(dt_ == nD - 1))
            pe_up[f] = k.sems["pe"]
            k.wait("vector", "pe", pe_up[f])
            k.wait("vector", "ac", ac_silu[f])
            dv_gu[f] = k.emit("vector", lambda e, f=f, sl=sl: e.tensor_mul(
                big[:, f * 512:(f + 1) * 512], pb[3][:, 0:512],
                s["wbuf"][:, 8192 + sl * 512: 8192 + (sl + 1) * 512]))
        k.wait("gpsimd", "pe", pe_up[63])
        lnr = k.pdma("gpsimd", [(
            nst[:].rearrange("p (n w) -> p n w", n=16),
            shn1[:, :, co * 512:(co + 1) * 512],
            nst[:].rearrange("p (n w) -> p n w", n=16),
            shn1[:, :, To + co * 512:To + (co + 1) * 512])], sem="gld")
        k.wait("tensor", "dv", dv_gu[63])
        pe_dn = {}
        for dt_ in range(nD):
            ssl = dt_ % 2
            swq = ("sync", "scalar")[ssl]
            if dt_ >= 1:
                if dt_ == 1:
                    k.wait(swq, "pe", pe_up[63])
                else:
                    k.wait(swq, "pe", pe_dn[dt_ - 2])
                stv = (s["nch"][:, 8192:16384] if ssl == 0 else s["wbuf"][:, 0:8192])
                lw = k.dma(swq, stv.rearrange("p (n c) -> p n c", n=64),
                           wdv[:, :, dt_ * 128:(dt_ + 1) * 128], sem=f"ws{ssl}")
            stripe = s["nch"][:, 8192:16384] if ssl == 0 else s["wbuf"][:, 0:8192]
            k.wait("tensor", f"ws{ssl}", k.sems[f"ws{ssl}"])
            bank = pb[4 + ssl]
            if dt_ >= 2:
                k.wait("tensor", "dv", k.sems["dv"])
            for ft in range(64):
                MM(bank[:, 0:512], stripe[:, ft * 128:(ft + 1) * 128],
                   big[:, ft * 512:(ft + 1) * 512], ft == 0, ft == 63, pe=(ft == 63))
            pe_dn[dt_] = k.sems["pe"]
            k.wait("vector", "pe", pe_dn[dt_])
            k.emit("vector", lambda e, dt_=dt_, bank=bank: e.tensor_add(
                rch[:, dt_ * 512:(dt_ + 1) * 512], rch[:, dt_ * 512:(dt_ + 1) * 512],
                bank[:, 0:512]))
        k.wait("vector", "gld", lnr)
        for dt_ in range(nD):
            MMdv = k.emit("vector", lambda e, dt_=dt_: e.tensor_add(
                rch[:, dt_ * 512:(dt_ + 1) * 512], rch[:, dt_ * 512:(dt_ + 1) * 512],
                nst[:, dt_ * 512:(dt_ + 1) * 512]))
        dv = k.emit("vector", lambda e: e.engine_nop())
        k.wait("scalar", "dv", dv)
        k.wait("tensor", "dv", dv)
        if ci == 0:
            rw = k.dma("scalar", resv[:, :, co * 512:(co + 1) * 512],
                       rch[:].rearrange("p (n w) -> p n w", n=16), sem="rwb")
            dv = emit_ln_chunk(k, 512, with_rs=False)
            k.wait("gpsimd", "dv", dv)
            st = k.pdma("gpsimd", [(
                shn0[:, :, co * 512:(co + 1) * 512],
                nst[:].rearrange("p (n w) -> p n w", n=16),
                shn0[:, :, To + co * 512:To + (co + 1) * 512],
                nst[:].rearrange("p (n w) -> p n w", n=16))], sem="nsh")
            k.wait("vector", "nsh", st)
            k.wait("sync", "nsh", st)
            k.wait("scalar", "nsh", st)
        else:
            k.dma("scalar", d["outT"].ap().rearrange("(n p) t -> p n t", p=128)
                  [:, :, co * 512:(co + 1) * 512],
                  rch[:].rearrange("p (n w) -> p n w", n=16), sem="oput")
    if ci == 0:
        k.wait("gpsimd", "nsh", k.sems["nsh"])
        k.flag_write("n0_b")



def phase_attn_b(k):
    cfg = k.cfg
    s, d = k.sbuf, k.dram
    pb = [s[f"pb{i}"] for i in range(8)]
    nst, rch = s["nst"], s["rch"]
    T, nD = cfg.T, cfg.nD
    cA = s["colsA"]

    def MM(out, lhsT, rhs, start, stop, pe=False):
        return k.emit("tensor", lambda e, out=out, lhsT=lhsT, rhs=rhs,
                      start=start, stop=stop:
                      e.matmul(out, lhsT, rhs, start=start, stop=stop),
                      "pe" if pe else None)

    k.tap("dp2", s["bdpc"][:])
    k.flag_poll("n0_b", "xs")
    k.wait("sync", "xs", k.sems["xs"])
    k.wait("sync", "pe", k.sems["pe"])
    k.wait("sync", "dv", k.sems["dv"])
    lx = k.dma("sync", s["wbuf"][:, 0:4352].rearrange("p (n c) -> p n c", n=16),
               dview(d["bw_x"]), sem="wa0")
    k.wait("tensor", "wa0", lx)
    k.wait("vector", "rwb", k.sems.get("rwb", 0))
    k.emit("vector", lambda e: e.memset(cA[:, 0:16], 0.0))
    k.emit("vector", lambda e: e.memset(s["rows"][0:1, 3072:3200], 1.0))
    k.wait("tensor", "dv", k.sems["dv"])
    pe = MM(pb[5][0:16, 0:16], s["ident"][0:16, 0:16], s["ident"][0:16, 0:16],
            True, True, pe=True)
    k.wait("vector", "pe", pe)
    k.emit("vector", lambda e: e.tensor_copy(cA[0:16, 16:32], pb[5][0:16, 0:16]))
    shn = dview(d["sh_n0"])
    wgv = dview(d["bw_g"])
    featB = dview(d["sh_featB"])

    pe_tch = {}
    dv_tch = {}
    for tch in range(cfg.nTCf):
        slot = tch % 2
        nchv = s["nch"][:, slot * 8192: slot * 8192 + 8192].rearrange(
            "p (n w) -> p n w", n=16)
        if tch >= 2:
            k.wait("sync", "pe", pe_tch[tch - 2])
            k.wait("sync", "dv", dv_tch[tch - 2])
        cols = slice(tch * 512, (tch + 1) * 512)
        ln = k.pdma("sync", [
            (nchv[:, 0:8, :], shn[:, 0:8, cols], nchv[:, 0:8, :], shn[:, 8:16, cols]),
            (nchv[:, 8:16, :], shn[:, 8:16, cols], nchv[:, 8:16, :], shn[:, 0:8, cols]),
        ], sem=f"na{slot}")
        k.wait("tensor", f"na{slot}", ln)
        k.wait("tensor", "dv", k.sems["dv"])
        k.wait("tensor", "ac", k.sems["ac"])
        for dt_ in range(nD):
            MM(pb[0][:, 0:512], s["wbuf"][:, dt_ * 272: dt_ * 272 + 128],
               nchv[:, dt_, :], dt_ == 0, dt_ == nD - 1, pe=(dt_ == nD - 1))
        for dt_ in range(nD):
            MM(pb[1][:, 0:512], s["wbuf"][:, dt_ * 272 + 128: dt_ * 272 + 256],
               nchv[:, dt_, :], dt_ == 0, dt_ == nD - 1, pe=(dt_ == nD - 1))
        for dt_ in range(nD):
            MM(pb[2][0:16, 0:512], s["wbuf"][:, dt_ * 272 + 256: dt_ * 272 + 272],
               nchv[:, dt_, :], dt_ == 0, dt_ == nD - 1, pe=(dt_ == nD - 1))
        k.wait("vector", "pe", k.sems["pe"])
        k.emit("vector", lambda e: e.tensor_copy(rch[:, 3072:3584], pb[0][:, 0:512]))
        k.emit("vector", lambda e: e.tensor_copy(nst[:, 1024:1536], pb[1][:, 0:512]))
        dv = k.emit("vector", lambda e: e.tensor_copy(rch[0:16, 3584:4096],
                                                      pb[2][0:16, 0:512]))
        k.wait("tensor", "dv", dv)
        if tch == 0:
            k.tap("bproj", rch[:, 3072:4096])
        for h in range(16):
            k.wait("vector", "pe", k.sems["pe"])
            pe = MM(pb[3][0:1, 0:512], cA[0:16, 16 + h:17 + h],
                    rch[0:16, 3584:4096], True, True, pe=True)
            k.wait("vector", "pe", pe)
            dv0 = k.emit("vector", lambda e: e.tensor_copy(
                s["rows"][0:1, 0:512], pb[3][0:1, 0:512]))
            k.wait("tensor", "dv", dv0)
            pe = MM(pb[3][:, 0:512], s["rows"][0:1, 3072:3200],
                    s["rows"][0:1, 0:512], True, True, pe=True)
            k.wait("vector", "pe", pe)
            dv = k.emit("vector", lambda e: e.tensor_add(
                rch[:, 0:512], pb[3][:, 0:512], rch[:, 3072:3584]))
            k.wait("scalar", "dv", dv)
            k.emit("scalar", lambda e: e.activation(
                out=rch[:, 512:1024], in_=rch[:, 0:512], func=AF.Abs))
            k.emit("scalar", lambda e: e.activation(
                out=rch[:, 512:1024], in_=rch[:, 512:1024], func=AF.Exp, scale=-1.0))
            ac = k.emit("scalar", lambda e: e.activation(
                out=rch[:, 512:1024], in_=rch[:, 512:1024], func=AF.Ln, bias=1.0))
            k.wait("vector", "ac", ac)
            k.emit("vector", lambda e: e.tensor_scalar_max(
                rch[:, 1024:1536], rch[:, 0:512], 0.0))
            k.emit("vector", lambda e: e.tensor_add(
                rch[:, 1024:1536], rch[:, 1024:1536], rch[:, 512:1024]))
            dv = k.emit("vector", lambda e, h=h: e.tensor_scalar_mul(
                rch[:, 0:512], rch[:, 1024:1536], s["bAT"][:, h:h + 1]))
            k.wait("scalar", "dv", dv)
            k.emit("scalar", lambda e: e.activation(
                out=rch[:, 1536:2048], in_=rch[:, 0:512], func=AF.Exp))
            ac = k.emit("scalar", lambda e: e.activation(
                out=rch[:, 2048:2560], in_=rch[:, 1024:1536], func=AF.Square))
            k.wait("vector", "ac", ac)
            if tch == 0 and h == 0:
                k.tap("bdelta", rch[:, 1024:1536])
                k.tap("ba", rch[:, 1536:2048])
                k.tap("bdd", rch[:, 2048:2560])
            k.emit("vector", lambda e, h=h: e.tensor_tensor_scan(
                rch[:, 2560:3072], rch[:, 1536:2048], rch[:, 2048:2560],
                cA[:, h:h + 1], ALU.mult, ALU.add))
            k.emit("vector", lambda e, h=h: e.tensor_copy(
                cA[:, h:h + 1], rch[:, 3071:3072]))
            dv = k.emit("vector", lambda e: e.tensor_mul(
                nst[:, 1536:2048], rch[:, 2560:3072], nst[:, 1024:1536]))
            if tch == 0 and h == 0:
                k.tap("bscan", rch[:, 2560:3072])
                k.tap("btt", nst[:, 1536:2048])
            if tch == 0 and h == 7:
                k.tap("bscan7", rch[:, 2560:3072])
            k.wait("tensor", "dv", dv)
            lhsT = (s["selk"][:, h * 16:(h + 1) * 16] if h < 8
                    else s["selq"][:, (h - 8) * 16:(h - 7) * 16])
            MM(pb[4][0:16, 0:512], lhsT, nst[:, 1536:2048], h == 0, h == 15,
               pe=(h == 15))
        if tch == 0:
            k.tap("selgv", s["selg"][:])
        k.wait("vector", "pe", k.sems["pe"])
        dv = k.emit("vector", lambda e: e.tensor_copy(
            nst[0:16, 2048:2560], pb[4][0:16, 0:512]))
        k.wait("tensor", "dv", dv)
        if tch < 2:
            k.tap(f"by{tch}", nst[0:16, 2048:2560])
        dvf = {}
        acf = {}
        pe_gf = {}
        for ft in range(8):
            wsl = ft % 2
            wsl3 = ft % 3
            wq_ = "sync"
            yb = pb[5] if wsl == 0 else pb[3]
            gb = pb[6] if wsl == 0 else pb[0]
            if ft >= 3:
                k.wait(wq_, "pe", pe_gf[ft - 3])
            else:
                k.wait(wq_, "pe", k.sems["pe"])
            lw = k.dma(wq_, s["wbuf"][:, 4352 + wsl3 * 2048: 4352 + wsl3 * 2048 + 2048]
                       .rearrange("p (n c) -> p n c", n=16),
                       wgv[:, :, ft * 128:(ft + 1) * 128], sem=f"bg{wsl3}")
            k.wait("tensor", f"bg{wsl3}", lw)
            if ft >= 2:
                k.wait("tensor", "dv", dvf[ft - 2])
                k.wait("tensor", "ac", acf[ft - 2])
            MM(yb[:, 0:512], s["selg"][0:16, ft * 128:(ft + 1) * 128],
               nst[0:16, 2048:2560], True, True)
            for dt_ in range(nD):
                pe_gf[ft] = MM(gb[:, 0:512],
                               s["wbuf"][:, 4352 + wsl3 * 2048 + dt_ * 128:
                                         4352 + wsl3 * 2048 + (dt_ + 1) * 128],
                               nchv[:, dt_, :], dt_ == 0, dt_ == nD - 1,
                               pe=(dt_ == nD - 1))
            k.wait("scalar", "pe", k.sems["pe"])
            if ft >= 1:
                k.wait("scalar", "dv", dvf[ft - 1])
            acf[ft] = k.emit("scalar", lambda e, gb=gb, ft=ft: e.activation(
                out=nst[:, 2560:3072], in_=gb[:, 0:512], func=AF.Sigmoid,
                bias=s["bbg"][:, ft:ft + 1]))
            k.wait("vector", "ac", acf[ft])
            k.emit("vector", lambda e, ft=ft, yb=yb, nchv=nchv: e.scalar_tensor_tensor(
                out=s["f32s"][:, 0:512], in0=nchv[:, ft, :],
                scalar=s["bdpc"][:, ft:ft + 1], in1=yb[:, 0:512],
                op0=ALU.mult, op1=ALU.add))
            dvf[ft] = k.emit("vector", lambda e, ft=ft: e.tensor_mul(
                nst[:, 3072 + ft * 512: 3072 + (ft + 1) * 512],
                s["f32s"][:, 0:512], nst[:, 2560:3072]))
            if tch == 0 and ft == 7:
                k.tap("bsig7", nst[:, 2560:3072])
                k.tap("bf327", s["f32s"][:, 0:512])
        if tch == 0:
            k.tap("bfst", nst[:, 3072:7168])
            k.tap("bxg0", s["nch"][:, slot * 8192: slot * 8192 + 8192])
        k.wait("gpsimd", "dv", k.sems["dv"])
        st = k.pdma("gpsimd", [(
            featB[:, 0:8, cols], nst[:, 3072:7168].rearrange("p (n w) -> p n w", n=8),
            featB[:, 8:16, cols], nst[:, 3072:7168].rearrange("p (n w) -> p n w", n=8),
        )], sem="fsh")
        k.wait("vector", "fsh", st)
        k.wait("scalar", "fsh", st)
        pe_tch[tch] = k.sems["pe"]
        dv_tch[tch] = k.sems["dv"]
    k.wait("gpsimd", "fsh", k.sems["fsh"])
    k.flag_write("featB")


def kernel_build(cfg):
    k = K(cfg)
    declare(k)
    load_consts(k)
    ldv = k.sems["cld"]
    for e in ("tensor", "vector", "scalar", "gpsimd"):
        k.wait(e, "cld", ldv)
    k.wait("sync", "rcp", k.sems["rcp"])
    cast_weights(k)
    phase_ln0(k)
    phase_attn_a(k)
    phase_wout(k, "sh_featA", "aw_out", 32, "featA", "n1_f1")
    phase_ffn(k, 0)
    phase_attn_b(k)
    phase_wout(k, "sh_featB", "bw_o", 16, "featB", "n1_f2")
    phase_ffn(k, 1)
    finale(k)
    return k


def finale(k):
    cfg = k.cfg
    s, d = k.sbuf, k.dram
    if "n0" in cfg.taps:
        k.flag_poll("n0_a", "xs")
        k.wait("gpsimd", "xs", k.sems["xs"])
        st = k.dma("gpsimd", d["tap_n0"].ap(), d["sh_n0"].ap(), sem="tput")
        k.wait("gpsimd", "tput", st)
    if "featA" in cfg.taps:
        st = k.dma("gpsimd", d["tap_featA"].ap(), d["sh_featA"].ap(), sem="tput")
        k.wait("gpsimd", "tput", st)
    if "n1" in cfg.taps:
        k.flag_poll("n1_f1", "xs")
        k.wait("gpsimd", "xs", k.sems["xs"])
        st = k.dma("gpsimd", d["tap_n1"].ap(), d["sh_n1"].ap(), sem="tput")
        st = k.dma("gpsimd", d["tap_nrs"].ap(), d["sh_nrs"].ap(), sem="tput")
        k.wait("gpsimd", "tput", st)
    if "n0b" in cfg.taps:
        k.flag_poll("n0_b", "xs")
        k.wait("gpsimd", "xs", k.sems["xs"])
        st = k.dma("gpsimd", d["tap_n0b"].ap(), d["sh_n0"].ap(), sem="tput")
        k.wait("gpsimd", "tput", st)
    if "featB" in cfg.taps:
        k.flag_poll("featB", "xs")
        k.wait("gpsimd", "xs", k.sems["xs"])
        st = k.dma("gpsimd", d["tap_featB"].ap(), d["sh_featB"].ap(), sem="tput")
        k.wait("gpsimd", "tput", st)
    k.wait("gpsimd", "oput", k.sems["oput"])
    k.wait("sync", "oput", k.sems["oput"])


def kernel(**inputs):
    cfg = Cfg()
    k = kernel_build(cfg)
    nc = k.build()
    nonce = int(time.time_ns() // 1000 & 0x0FFFFFFF) | 1
    in_maps = [host_pack(inputs, cfg, core, nonce) for core in range(8)]
    res = run_bass_kernel_spmd(nc, in_maps, core_ids=list(range(8)))
    T, To, D = cfg.T, cfg.To, cfg.D
    out = np.zeros((4, T, D), np.float32)
    for core in range(8):
        s, bidx = core & 1, core // 2
        out[bidx, s * To:(s + 1) * To, :] = np.asarray(res.results[core]["outT"]).T
    return out


# version 06 (final)
# speedup vs baseline: 1.1227x; 1.0452x over previous
import time

import numpy as np
import ml_dtypes

import concourse.bass as bass
import concourse.mybir as mybir
from concourse.bass_utils import run_bass_kernel_spmd

F32 = mybir.dt.float32
BF16 = mybir.dt.bfloat16
I32 = mybir.dt.int32
AF = mybir.ActivationFunctionType
ALU = mybir.AluOpType
AX = mybir.AxisListType

BF = ml_dtypes.bfloat16
NEG = -1e30

CONSTS = [1e-5, 1e-6, -0.5, 0.5, 1.0, -1.0, 2.0, 0.1, -0.1,
          float(1.0 / 2048.0), float(128 ** -0.5), float(-(128 ** -0.5)),
          float(1.0 / 128.0), 0.0, -2.4260151319598084, -2.0] + [7.0] * 16


class Cfg:
    def __init__(self, T_full=4096, taps=(), solo=False, detect=True, priv=False):
        self.solo = solo
        self.detect = detect
        self.priv = priv
        self.T = T_full
        self.To = T_full // 2
        self.D = 2048
        self.nD = 16
        self.C = 128
        self.nCh = T_full // 128
        self.TCH = 512
        self.nTCo = self.To // self.TCH
        self.nTCf = self.T // self.TCH
        self.taps = set(taps)



def host_pack(inputs, cfg, core, nonce):
    s = core & 1
    b = core // 2
    To = cfg.To
    f32 = lambda a: np.ascontiguousarray(np.asarray(a, np.float32))
    bf = lambda a: np.ascontiguousarray(np.asarray(a).astype(BF))

    d = {}
    x = np.asarray(inputs["L0_attn_a_x"])
    d["x0T"] = f32(x[b, s * To:(s + 1) * To, :].T)

    wqkv = np.asarray(inputs["L0_attn_a_w_qkv"])
    d["a_wqkv"] = f32(np.concatenate([
        wqkv[:, s * 1024:(s + 1) * 1024],
        wqkv[:, 2048 + s * 1024:2048 + (s + 1) * 1024],
        wqkv[:, 4096 + s * 2048:4096 + (s + 1) * 2048]], axis=1))
    d["a_wz"] = f32(np.asarray(inputs["L0_attn_a_w_z"])[:, s * 2048:(s + 1) * 2048])
    d["a_wba"] = f32(np.concatenate([
        np.asarray(inputs["L0_attn_a_w_b"])[:, s * 16:(s + 1) * 16],
        np.asarray(inputs["L0_attn_a_w_a"])[:, s * 16:(s + 1) * 16]], axis=1))
    convw = np.asarray(inputs["L0_attn_a_conv_w"])
    d["a_convT"] = f32(np.concatenate([
        convw[:, s * 1024:(s + 1) * 1024],
        convw[:, 2048 + s * 1024:2048 + (s + 1) * 1024],
        convw[:, 4096 + s * 2048:4096 + (s + 1) * 2048]], axis=1).T)
    d["a_miscT"] = f32(np.stack([
        np.asarray(inputs["L0_attn_a_dt_bias"])[s * 16:(s + 1) * 16],
        np.asarray(inputs["L0_attn_a_A_log"])[s * 16:(s + 1) * 16]], axis=1))
    d["a_normwT"] = f32(np.asarray(inputs["L0_attn_a_norm_w"])[:, None])
    d["a_wout"] = f32(inputs["L0_attn_a_w_out"])

    d["f_wq"] = f32(np.asarray(inputs["L0_ffn_Wq"]).T[:, s * 1024:(s + 1) * 1024])
    d["f_wk"] = f32(np.asarray(inputs["L0_ffn_Wk"]).T[:, s * 1024:(s + 1) * 1024])
    d["f_wv"] = f32(np.asarray(inputs["L0_ffn_Wv"]).T[:, s * 1024:(s + 1) * 1024])
    d["f_wo"] = f32(inputs["L0_ffn_Wo"])
    d["f_A"] = f32(inputs["L0_ffn_A"])
    d["f_B"] = f32(inputs["L0_ffn_B"])
    d["f_lnwT"] = f32(np.stack([np.asarray(inputs["L0_ffn_ln1_w"]),
                                np.asarray(inputs["L0_ffn_ln2_w"])], axis=1))
    d["f_win"] = f32(np.asarray(inputs["L0_ffn_W_in_base"]).T)
    d["f_wup"] = f32(np.asarray(inputs["L0_ffn_W_up_base"]).T)
    d["f_wdown"] = f32(np.asarray(inputs["L0_ffn_W_down"]).T)

    rot = lambda a: np.concatenate([a[s * 1024:(s + 1) * 1024],
                                    a[(1 - s) * 1024:(2 - s) * 1024]], axis=0)
    wx = rot(np.asarray(inputs["L0_attn_b_Wx"]))
    d["b_wx"] = f32(np.concatenate([wx[:, :256],
                                    wx[:, 256 + s * 16:256 + (s + 1) * 16]], axis=1))
    d["b_wg"] = f32(rot(np.asarray(inputs["L0_attn_b_Wg"]))[:, s * 1024:(s + 1) * 1024])
    d["b_bgT"] = f32(np.asarray(inputs["L0_attn_b_bg"])[s * 1024:(s + 1) * 1024].reshape(8, 128).T)
    d["b_wo"] = f32(inputs["L0_attn_b_Wo"])
    d["b_AT"] = f32(np.asarray(inputs["L0_attn_b_A"])[s * 16:(s + 1) * 16].T)
    dp = np.asarray(inputs["L0_attn_b_Dp"])[s * 16:(s + 1) * 16]
    dpc = np.zeros((128, 8), np.float32)
    for p in range(8):
        dpc[:64, p] = dp[2 * p]
        dpc[64:, p] = dp[2 * p + 1]
    d["b_dpc"] = f32(dpc)

    d["c_ident"] = bf(np.eye(128))
    d["c_ones"] = bf(np.ones((128, 128)))
    ii, jj = np.meshgrid(np.arange(128), np.arange(128), indexing="ij")
    d["c_masks"] = bf(np.stack([
        np.where(jj < ii, 0.0, NEG),
        np.where(jj > ii, 0.0, NEG),
        np.where(jj >= ii, 0.0, NEG),
    ]))
    kk2, cc2 = np.meshgrid(np.arange(128), np.arange(896), indexing="ij")
    d["c_cmask"] = bf(np.where(cc2 - 384 >= kk2, 0.0, NEG))
    sel = np.zeros((16, 16 * 128), np.float32)
    for h in range(16):
        sel[h, h * 128:(h + 1) * 128] = 1.0
    d["c_sel"] = bf(sel)
    selk = np.zeros((128, 8 * 16), np.float32)
    selq = np.zeros((128, 8 * 16), np.float32)
    for r in range(8):
        selk[:, r * 16 + r] = 1.0
        selq[:, r * 16 + 8 + r] = 1.0
    d["c_selk"] = bf(selk)
    d["c_selq"] = bf(selq)
    exp2 = np.zeros((8, 16), np.float32)
    for r in range(8):
        exp2[r, 2 * r] = 1.0
        exp2[r, 2 * r + 1] = 1.0
    d["c_exp2"] = bf(exp2)
    selg = np.zeros((16, 8 * 128), np.float32)
    for ft in range(8):
        selg[2 * ft, ft * 128:ft * 128 + 64] = 1.0
        selg[2 * ft + 1, ft * 128 + 64:(ft + 1) * 128] = 1.0
    d["c_selg"] = bf(selg)
    d["c_scal"] = f32(np.tile(np.array([CONSTS], np.float32), (128, 1)))
    d["par"] = np.ascontiguousarray(
        np.array([[s, nonce & 0x3FFFFFFF, 0, 0]], np.int32))
    return d



ENGS = ("sync", "tensor", "vector", "scalar", "gpsimd")
QSEM = {"sync": "q_ld", "scalar": "q_sc", "gpsimd": "q_gp", "vector": "q_dv"}


class K:
    def __init__(self, cfg):
        self.cfg = cfg
        self.nc = bass.Bass("TRN2", target_bir_lowering=False,
                            detect_race_conditions=cfg.detect)
        self.prog = {e: [] for e in ENGS}
        self.sems = {"prolog": 0}
        self.semh = {}
        self.dram = {}
        self.sbuf = {}
        self.regs = {}
        self.flagseq = {}
        self._nflags = 0
        self.sbuf_bytes = 0

    def dram_t(self, name, shape, dtype, **kw):
        h = self.nc.dram_tensor(name, shape, dtype, **kw)
        self.dram[name] = h
        return h

    def sb(self, name, shape, dtype):
        h = self.nc.alloc_sbuf_tensor(name, list(shape), dtype)
        self.sbuf[name] = h
        self.sbuf_bytes += int(np.prod(shape[1:])) * mybir.dt.size(dtype)
        return h

    def psum(self, name, shape=(128, 512), dtype=F32):
        h = self.nc.alloc_psum_tensor(name, list(shape), dtype)
        self.sbuf[name] = h
        return h

    def sem(self, name):
        if name not in self.sems:
            self.sems[name] = 0
        return name

    def flag_id(self, name):
        if name not in self.flagseq:
            self.flagseq[name] = self._nflags
            self._nflags += 1
        return self.flagseq[name]

    def op(self, eng, fn):
        self.prog[eng].append(fn)

    AUTOSEM = {"vector": "dv", "scalar": "ac", "gpsimd": "gp"}

    def emit(self, eng, call, sem=None, by=1):
        auto = self.AUTOSEM.get(eng)
        if auto is not None:
            self.sem(auto)
            prev = self.sems[auto]
            self.sems[auto] += 1
            def fn(e, call=call, auto=auto, prev=prev):
                if prev:
                    e.wait_ge(self.semh[auto], prev)
                call(e).then_inc(self.semh[auto], 1)
            self.op(eng, fn)
            if sem is not None and sem != auto:
                raise ValueError(f"compute op on {eng} counts on '{auto}'")
            return self.sems[auto]
        if sem is not None:
            self.sem(sem)
            self.sems[sem] += by
            def fn(e, call=call, sem=sem, by=by):
                call(e).then_inc(self.semh[sem], by)
        else:
            def fn(e, call=call):
                call(e)
        self.op(eng, fn)
        return self.sems[sem] if sem else None

    def wait(self, eng, sem, value=None):
        self.sem(sem)
        v = self.sems[sem] if value is None else value
        self.op(eng, lambda e, sem=sem, v=v: e.wait_ge(self.semh[sem], v))

    def dma(self, q, out, in_, sem=None, **kw):
        sem = QSEM[q] if sem is None else sem
        self.sem(sem)
        self.sems[sem] += 16
        def fn(e, out=out, in_=in_, kw=kw, sem=sem):
            e.dma_start(out=out, in_=in_, **kw).then_inc(self.semh[sem], 16)
        self.op(q, fn)
        return self.sems[sem]

    def pdma(self, q, pairs, sem=None):
        sem = QSEM[q] if sem is None else sem
        self.sem(sem)
        self.sems[sem] += 16 * len(pairs)
        tgt = self.sems[sem]
        def fn(e, pairs=pairs, sem=sem):
            pr = self.regs[(e.engine, "pr")]
            with e.If_eq(pr, 0):
                for (o0, i0, _o1, _i1) in pairs:
                    e.dma_start(out=o0, in_=i0).then_inc(self.semh[sem], 16)
            with e.Else():
                for (_o0, _i0, o1, i1) in pairs:
                    e.dma_start(out=o1, in_=i1).then_inc(self.semh[sem], 16)
        self.op(q, fn)
        return tgt

    def flag_write(self, name):
        idx = self.flag_id(name)
        self.sem("q_fl")
        prev = self.sems["q_fl"]
        self.sems["q_fl"] += 16
        fl = self.sbuf["flagout"]
        sh = self.dram["sh_flags"]
        def fn(e, idx=idx, prev=prev):
            nr = self.regs[(e.engine, "nonce")]
            tmp = self.regs[(e.engine, "tmp")]
            pr = self.regs[(e.engine, "pr")]
            e.wait_ge(self.semh["q_fl"], prev)
            e.reg_add(tmp, nr, idx + 1)
            e.store(fl[0:1, 0:1], tmp)
            with e.If_eq(pr, 0):
                e.dma_start(out=sh.ap()[0:1, idx:idx + 1], in_=fl[0:1, 0:1]).then_inc(self.semh["q_fl"], 16)
            with e.Else():
                e.dma_start(out=sh.ap()[1:2, idx:idx + 1], in_=fl[0:1, 0:1]).then_inc(self.semh["q_fl"], 16)
        self.op("gpsimd", fn)

    def flag_poll(self, name, done_sem="xs"):
        idx = self.flag_id(name)
        if self.cfg.solo:
            self.sem(done_sem)
            self.sems[done_sem] += 1
            self.op("gpsimd", lambda e, ds=done_sem: e.sem_inc(self.semh[ds], 1))
            return self.sems[done_sem]
        self.sem(done_sem)
        self.sem("q_pl")
        self.sems[done_sem] += 1
        fl = self.sbuf["flagin"]
        sh = self.dram["sh_flags"]
        def fn(e, idx=idx, done_sem=done_sem):
            pr = self.regs[(e.engine, "pr")]
            nr = self.regs[(e.engine, "nonce")]
            tmp = self.regs[(e.engine, "tmp")]
            fr = self.regs[(e.engine, "fr")]
            diff = self.regs[(e.engine, "diff")]
            cnt = self.regs[(e.engine, "cnt")]
            e.reg_add(tmp, nr, idx + 1)
            e.reg_mov(diff, 1)
            with e.While(diff):
                e.reg_add(cnt, cnt, 16)
                with e.If_eq(pr, 0):
                    e.dma_start(out=fl[0:1, 0:1], in_=sh.ap()[1:2, idx:idx + 1]).then_inc(self.semh["q_pl"], 16)
                with e.Else():
                    e.dma_start(out=fl[0:1, 0:1], in_=sh.ap()[0:1, idx:idx + 1]).then_inc(self.semh["q_pl"], 16)
                e.wait_ge(self.semh["q_pl"], cnt)
                e.reg_load(fr, fl[0:1, 0:1])
                e.reg_alu(diff, fr, tmp, ALU.bitwise_xor)
            e.sem_inc(self.semh[done_sem], 1)
        self.op("gpsimd", fn)
        return self.sems[done_sem]

    def tap(self, name, ap):
        if name not in self.cfg.taps:
            return
        sh = [ap.shape[0], int(np.prod(ap.shape[1:]))]
        t = self.dram_t(f"tap_{name}", sh, ap.dtype, kind="ExternalOutput")
        for e in ("vector", "scalar"):
            self.wait("gpsimd", self.AUTOSEM[e], self.sems.get(self.AUTOSEM[e], 0))
        self.sem("tput")
        st = self.dma("gpsimd", t.ap(), ap, sem="tput")
        self.wait("gpsimd", "tput", st)
        self.wait("vector", "tput", st)
        self.wait("scalar", "tput", st)

    def build(self):
        nc = self.nc
        for n in self.sems:
            self.semh[n] = nc.alloc_semaphore(n)
        blk = nc.Block()
        block = blk.__enter__()

        def prolog(e):
            eng = e.engine
            for rn in ("pr", "nonce", "tmp", "fr", "diff", "cnt"):
                self.regs[(eng, rn)] = e.alloc_register(f"r_{rn}")
            e.wait_ge(self.semh["prolog"], 16)
            e.reg_load(self.regs[(eng, "pr")], self.sbuf["sb_par"][0:1, 0:1])
            e.reg_load(self.regs[(eng, "nonce")], self.sbuf["sb_par"][0:1, 1:2])
            e.reg_mov(self.regs[(eng, "cnt")], 0)

        def mk(engname):
            def run(e):
                if engname == "sync":
                    e.dma_start(out=self.sbuf["sb_par"][:],
                                in_=self.dram["par"].ap()).then_inc(self.semh["prolog"], 16)
                prolog(e)
                for fn in self.prog[engname]:
                    fn(e)
            return run

        block.sync(mk("sync"))
        block.tensor(mk("tensor"))
        block.vector(mk("vector"))
        block.scalar(mk("scalar"))
        block.gpsimd(mk("gpsimd"))
        blk.__exit__(None, None, None)
        return nc


def dview(t, p=128):
    return t.ap().rearrange("(n p) t -> p n t", p=p)



def declare(k):
    cfg = k.cfg
    T, To, D = cfg.T, cfg.To, cfg.D
    dt = k.dram_t
    dt("x0T", [D, To], F32, kind="ExternalInput")
    dt("a_wqkv", [D, 4096], F32, kind="ExternalInput")
    dt("a_wz", [D, 2048], F32, kind="ExternalInput")
    dt("a_wba", [D, 32], F32, kind="ExternalInput")
    dt("a_convT", [4096, 4], F32, kind="ExternalInput")
    dt("a_miscT", [16, 2], F32, kind="ExternalInput")
    dt("a_normwT", [128, 1], F32, kind="ExternalInput")
    dt("a_wout", [4096, D], F32, kind="ExternalInput")
    dt("f_wq", [D, 1024], F32, kind="ExternalInput")
    dt("f_wk", [D, 1024], F32, kind="ExternalInput")
    dt("f_wv", [D, 1024], F32, kind="ExternalInput")
    dt("f_wo", [D, D], F32, kind="ExternalInput")
    dt("f_A", [D, 16], F32, kind="ExternalInput")
    dt("f_B", [16, 8192], F32, kind="ExternalInput")
    dt("f_lnwT", [D, 2], F32, kind="ExternalInput")
    dt("f_win", [D, 8192], F32, kind="ExternalInput")
    dt("f_wup", [D, 8192], F32, kind="ExternalInput")
    dt("f_wdown", [8192, D], F32, kind="ExternalInput")
    dt("b_wx", [D, 272], F32, kind="ExternalInput")
    dt("b_wg", [D, 1024], F32, kind="ExternalInput")
    dt("b_bgT", [128, 8], F32, kind="ExternalInput")
    dt("b_wo", [D, D], F32, kind="ExternalInput")
    dt("b_AT", [128, 16], F32, kind="ExternalInput")
    dt("b_dpc", [128, 8], F32, kind="ExternalInput")
    dt("c_ident", [128, 128], BF16, kind="ExternalInput")
    dt("c_ones", [128, 128], BF16, kind="ExternalInput")
    dt("c_masks", [3, 128, 128], BF16, kind="ExternalInput")
    dt("c_cmask", [128, 896], BF16, kind="ExternalInput")
    dt("c_sel", [16, 2048], BF16, kind="ExternalInput")
    dt("c_selk", [128, 128], BF16, kind="ExternalInput")
    dt("c_selq", [128, 128], BF16, kind="ExternalInput")
    dt("c_exp2", [8, 16], BF16, kind="ExternalInput")
    dt("c_selg", [16, 1024], BF16, kind="ExternalInput")
    dt("c_scal", [128, 32], F32, kind="ExternalInput")
    dt("par", [1, 4], I32, kind="ExternalInput")
    dt("outT", [D, To], F32, kind="ExternalOutput")
    if "n0" in cfg.taps:
        dt("tap_n0", [D, T], BF16, kind="ExternalOutput")
    if "featA" in cfg.taps:
        dt("tap_featA", [4096, T], BF16, kind="ExternalOutput")
    if "n1" in cfg.taps:
        dt("tap_n1", [D, T], BF16, kind="ExternalOutput")
        dt("tap_nrs", [D, T], BF16, kind="ExternalOutput")
    if "n0b" in cfg.taps:
        dt("tap_n0b", [D, T], BF16, kind="ExternalOutput")
    if "featB" in cfg.taps:
        dt("tap_featB", [D, T], BF16, kind="ExternalOutput")
    dt("resid", [D, To], F32)
    dt("aw_qkv", [D, 4096], BF16)
    dt("aw_z", [D, 2048], BF16)
    dt("aw_ba", [D, 32], BF16)
    dt("aw_out", [4096, D], BF16)
    dt("fw_q", [D, 1024], BF16)
    dt("fw_k", [D, 1024], BF16)
    dt("fw_v", [D, 1024], BF16)
    dt("fw_A", [D, 16], BF16)
    dt("fw_woa", [D, 16], BF16)
    dt("fw_B", [16, 8192], BF16)
    dt("fw_in", [D, 8192], BF16)
    dt("fw_up", [D, 8192], BF16)
    dt("fw_down", [8192, D], BF16)
    dt("bw_x", [D, 272], BF16)
    dt("bw_g", [D, 1024], BF16)
    dt("bw_o", [D, D], BF16)
    dt("qT_scr", [1024, T], BF16)
    shkw = {} if cfg.priv else {"addr_space": "Shared"}
    dt("sh_n0", [D, T], BF16, **shkw)
    dt("sh_n1", [D, T], BF16, **shkw)
    dt("sh_nrs", [D, T], BF16, **shkw)
    dt("sh_featA", [4096, T], BF16, **shkw)
    dt("sh_featB", [D, T], BF16, **shkw)
    dt("sh_woa", [2, 16 * T], F32, **shkw)
    dt("sh_flags", [2, 64], I32, **shkw)

    sb = k.sb
    sb("sb_par", (1, 4), I32)
    sb("kconst", (128, 32), F32)
    for i, v in enumerate(CONSTS):
        k.nc.const_aps.register(F32, v, k.sbuf["kconst"][:, i:i + 1])
        k.nc.const_aps.register(BF16, v, k.sbuf["kconst"][:, i:i + 1])
    sb("flagout", (1, 16), I32)
    sb("flagin", (1, 16), I32)
    sb("ident", (128, 128), BF16)
    sb("ones", (128, 128), BF16)
    sb("masks", (128, 3 * 128), BF16)
    sb("cmask", (128, 896), BF16)
    sb("sel", (16, 2048), BF16)
    sb("selk", (128, 128), BF16)
    sb("selq", (128, 128), BF16)
    sb("exp2", (8, 16), BF16)
    sb("selg", (16, 1024), BF16)
    sb("lnw", (128, 2 * 16), F32)
    sb("miscT", (16, 2), F32)
    sb("normw", (128, 1), F32)
    sb("bAT", (128, 16), F32)
    sb("bdpc", (128, 8), F32)
    sb("bbg", (128, 8), F32)
    sb("convw", (128, 32 * 4), F32)
    sb("nch", (128, 16384), BF16)
    sb("wbuf", (128, 12288), BF16)
    sb("bigb", (128, 36864), BF16)
    sb("rch", (128, 8192), F32)
    sb("nst", (128, 8192), BF16)
    sb("rows", (1, 3584), F32)
    sb("rows2", (1, 1536), BF16)
    sb("tails", (128, 128), BF16)
    sb("colsA", (128, 256), F32)
    sb("colsB", (128, 320), BF16)
    sb("f32s", (128, 512), F32)
    assert k.sbuf_bytes <= 212600, k.sbuf_bytes

    for i in range(7):
        k.psum(f"pb{i}")
    k.psum("pb7", (128, 1024), BF16)


def load_consts(k):
    cfg = k.cfg
    d, s = k.dram, k.sbuf
    k.dma("sync", s["kconst"][:], d["c_scal"].ap(), sem="cld")
    k.dma("sync", s["ident"][:], d["c_ident"].ap(), sem="cld")
    k.dma("sync", s["ones"][:], d["c_ones"].ap(), sem="cld")
    k.dma("sync", s["masks"][:].rearrange("p (m c) -> p m c", m=3), d["c_masks"].ap().rearrange("m p c -> p m c"), sem="cld")
    k.dma("sync", s["cmask"][:], d["c_cmask"].ap(), sem="cld")
    k.dma("sync", s["sel"][:], d["c_sel"].ap(), sem="cld")
    k.dma("sync", s["selk"][:], d["c_selk"].ap(), sem="cld")
    k.dma("sync", s["selq"][:], d["c_selq"].ap(), sem="cld")
    k.dma("sync", s["exp2"][:], d["c_exp2"].ap(), sem="cld")
    k.dma("sync", s["selg"][:], d["c_selg"].ap(), sem="cld")
    k.dma("sync", s["lnw"][:].rearrange("p (n c) -> p n c", n=16), d["f_lnwT"].ap().rearrange("(n p) c -> p n c", p=128), sem="cld")
    k.dma("sync", s["miscT"][:], d["a_miscT"].ap(), sem="cld")
    k.dma("sync", s["normw"][:], d["a_normwT"].ap(), sem="cld")
    k.dma("sync", s["bAT"][:], d["b_AT"].ap(), sem="cld")
    k.dma("sync", s["bdpc"][:], d["b_dpc"].ap(), sem="cld")
    k.dma("sync", s["bbg"][:], d["b_bgT"].ap(), sem="cld")
    k.dma("sync", s["convw"][:].rearrange("p (n c) -> p n c", n=32), d["a_convT"].ap().rearrange("(n p) c -> p n c", p=128), sem="cld")
    k.dma("scalar", d["resid"].ap(), d["x0T"].ap(), sem="rcp")


def cast_weights(k):
    cfg = k.cfg
    d, s = k.dram, k.sbuf
    jobs = [
        ("a_wqkv", "aw_qkv", 4096, None),
        ("a_wz", "aw_z", 2048, None),
        ("a_wba", "aw_ba", 32, None),
        ("a_wout", "aw_out", 2048, "normw"),
        ("f_wq", "fw_q", 1024, "ln1"),
        ("f_wk", "fw_k", 1024, "ln1"),
        ("f_wv", "fw_v", 1024, "ln1"),
        ("f_A", "fw_A", 16, "ln2"),
        ("f_B", "fw_B", 8192, None),
        ("f_win", "fw_in", 8192, "ln2"),
        ("f_wup", "fw_up", 8192, "ln2"),
        ("f_wdown", "fw_down", 2048, None),
        ("b_wx", "bw_x", 272, None),
        ("b_wg", "bw_g", 1024, None),
        ("b_wo", "bw_o", 2048, None),
    ]
    CH = 4096
    chunks = []
    for (src, dst, ncols, fold) in jobs:
        srct = d[src]
        nrow = srct.shape[0]
        nrt = (nrow + 127) // 128
        for rt in range(nrt):
            nr = min(128, nrow - rt * 128)
            for c0 in range(0, ncols, CH):
                cw = min(CH, ncols - c0)
                chunks.append((src, dst, fold, rt, nr, c0, cw))
    LQ = ("sync", "scalar")
    SQ = ("gpsimd", "gpsimd")
    stv = {0: 0, 1: 0}
    for i, (src, dst, fold, rt, nr, c0, cw) in enumerate(chunks):
        sl = i % 2
        lo = sl * 4096
        if i >= 2:
            k.wait(LQ[sl], f"wb{sl}", stv[sl])
        ld = k.dma(LQ[sl], s["rch"][0:nr, lo:lo + cw],
                   d[src].ap()[rt * 128:rt * 128 + nr, c0:c0 + cw], sem=f"wl{sl}")
        k.wait("vector", f"wl{sl}", ld)
        if fold == "ln1":
            scal = s["lnw"][:, 2 * rt:2 * rt + 1]
        elif fold == "ln2":
            scal = s["lnw"][:, 2 * rt + 1:2 * rt + 2]
        elif fold == "normw":
            scal = s["normw"][:, 0:1]
        else:
            scal = None
        if scal is None:
            dv = k.emit("vector",
                        lambda e, cw=cw, nr=nr, lo=lo: e.tensor_copy(
                            s["nst"][0:nr, lo:lo + cw], s["rch"][0:nr, lo:lo + cw]),
                        "dv")
        else:
            dv = k.emit("vector",
                        lambda e, cw=cw, scal=scal, nr=nr, lo=lo: e.tensor_scalar_mul(
                            s["nst"][0:nr, lo:lo + cw], s["rch"][0:nr, lo:lo + cw], scal),
                        "dv")
        k.wait(SQ[sl], "dv", dv)
        stv[sl] = k.dma(SQ[sl], d[dst].ap()[rt * 128:rt * 128 + nr, c0:c0 + cw],
                        s["nst"][0:nr, lo:lo + cw], sem=f"wb{sl}")
    for sl in range(2):
        k.wait("sync", f"wb{sl}", stv[sl])
        k.wait("vector", f"wb{sl}", stv[sl])
    pb0 = s["pb0"]
    la = k.dma("sync", s["f32s"][:, 0:256].rearrange("p (n c) -> p n c", n=16),
               d["f_A"].ap().rearrange("(n p) c -> p n c", p=128), sem="wld")
    k.wait("tensor", "wld", la)
    pes_w = {}
    for ct in range(16):
        for dt_ in range(16):
            i = ct * 16 + dt_
            slot = i % 6
            if i >= 6:
                k.wait("sync", "pe", pes_w[i - 6])
            lw = k.dma("sync", s["rch"][:, slot * 128:slot * 128 + 128],
                       d["f_wo"].ap()[dt_ * 128:(dt_ + 1) * 128, ct * 128:(ct + 1) * 128],
                       sem=f"wc{slot}")
            k.wait("tensor", f"wc{slot}", lw)
            pes_w[i] = k.emit("tensor",
                              lambda e, dt_=dt_, slot=slot: e.matmul(
                                  pb0[:, 0:16], s["rch"][:, slot * 128:slot * 128 + 128],
                                  s["f32s"][:, dt_ * 16:(dt_ + 1) * 16],
                                  start=(dt_ == 0), stop=(dt_ == 15)),
                              "pe")
        k.wait("vector", "pe", k.sems["pe"])
        dv = k.emit("vector", lambda e: e.tensor_copy(s["nst"][:, 0:16], pb0[:, 0:16]), "dv")
        k.wait("gpsimd", "dv", dv)
        st = k.dma("gpsimd", d["fw_woa"].ap()[ct * 128:(ct + 1) * 128, 0:16], s["nst"][:, 0:16], sem="wst")
        k.wait("vector", "wst", st)
        k.wait("tensor", "dv", dv)
    k.wait("sync", "wst", k.sems["wst"])



def emit_ln_chunk(k, nw, with_rs):
    cfg = k.cfg
    s = k.sbuf
    nD = cfg.nD
    pb5, pb6, pb7 = s["pb5"], s["pb6"], s["pb2"]
    stg = s["bigb"][:, 36352:36352 + nw]
    
    for t in range(nD):
        k.emit("vector",
               lambda e, t=t: e.tensor_copy(s["nst"][:, t * nw:(t + 1) * nw],
                                            s["rch"][:, t * nw:(t + 1) * nw]),
               "dv")
    dvc = k.sems["dv"]
    k.wait("tensor", "dv", dvc)
    k.wait("scalar", "dv", dvc)
    for t in range(nD):
        ac = k.emit("scalar",
                    lambda e, t=t: e.activation(out=stg, in_=s["nst"][:, t * nw:(t + 1) * nw],
                                                func=AF.Square),
                    "ac")
        k.wait("tensor", "ac", ac)
        k.emit("tensor",
               lambda e, t=t: e.matmul(pb6[0:1, 0:nw], s["ones"][:, 0:1],
                                       s["nst"][:, t * nw:(t + 1) * nw],
                                       start=(t == 0), stop=(t == nD - 1)),
               None)
        pe = k.emit("tensor",
                    lambda e, t=t: e.matmul(pb7[0:1, 0:nw], s["ones"][:, 0:1], stg,
                                            start=(t == 0), stop=(t == nD - 1)),
                    "pe")
        if t < nD - 1:
            k.wait("scalar", "pe", pe)
    k.wait("vector", "pe", k.sems["pe"])
    k.emit("vector", lambda e: e.tensor_scalar_mul(s["rows"][0:1, 0 * 512:0 * 512 + nw], pb6[0:1, 0:nw], 1.0 / cfg.D), None)
    k.emit("vector", lambda e: e.tensor_scalar_mul(s["rows"][0:1, 1 * 512:1 * 512 + nw], pb7[0:1, 0:nw], 1.0 / cfg.D), None)
    k.emit("vector", lambda e: e.scalar_tensor_tensor(
        out=s["rows"][0:1, 2 * 512:2 * 512 + nw], in0=s["rows"][0:1, 0 * 512:0 * 512 + nw], scalar=-1.0, in1=s["rows"][0:1, 0 * 512:0 * 512 + nw],
        op0=ALU.mult, op1=ALU.mult), None)
    dv = k.emit("vector", lambda e: e.tensor_add(s["rows"][0:1, 2 * 512:2 * 512 + nw], s["rows"][0:1, 2 * 512:2 * 512 + nw], s["rows"][0:1, 1 * 512:1 * 512 + nw]), "dv")
    k.wait("scalar", "dv", dv)
    k.emit("scalar", lambda e: e.activation(out=s["rows"][0:1, 3 * 512:3 * 512 + nw], in_=s["rows"][0:1, 2 * 512:2 * 512 + nw], func=AF.Ln, bias=1e-5), None)
    ac = k.emit("scalar", lambda e: e.activation(out=s["rows"][0:1, 3 * 512:3 * 512 + nw], in_=s["rows"][0:1, 3 * 512:3 * 512 + nw], func=AF.Exp, scale=-0.5), "ac")
    k.wait("vector", "ac", ac)
    if with_rs:
        k.emit("vector", lambda e: e.tensor_mul(s["rows"][0:1, 5 * 512:5 * 512 + nw], s["rows"][0:1, 3 * 512:3 * 512 + nw], s["rows"][0:1, 3 * 512:3 * 512 + nw]), None)
        dv = k.emit("vector", lambda e: e.tensor_mul(s["rows"][0:1, 5 * 512:5 * 512 + nw], s["rows"][0:1, 5 * 512:5 * 512 + nw], s["rows"][0:1, 2 * 512:2 * 512 + nw]), "dv")
        k.wait("scalar", "dv", dv)
        k.emit("scalar", lambda e: e.activation(out=s["rows"][0:1, 6 * 512:6 * 512 + nw], in_=s["rows"][0:1, 5 * 512:5 * 512 + nw], func=AF.Ln, bias=1e-6), None)
        ac = k.emit("scalar", lambda e: e.activation(out=s["rows"][0:1, 6 * 512:6 * 512 + nw], in_=s["rows"][0:1, 6 * 512:6 * 512 + nw], func=AF.Exp, scale=-0.5), "ac")
        k.wait("vector", "ac", ac)
        k.emit("vector", lambda e: e.tensor_copy(s["rows2"][0:1, 2 * 512:2 * 512 + nw], s["rows"][0:1, 6 * 512:6 * 512 + nw]), None)
    k.emit("vector", lambda e: e.tensor_copy(s["rows2"][0:1, 0 * 512:0 * 512 + nw], s["rows"][0:1, 3 * 512:3 * 512 + nw]), None)
    dvv = k.emit("vector", lambda e: e.tensor_copy(s["rows2"][0:1, 1 * 512:1 * 512 + nw], s["rows"][0:1, 0 * 512:0 * 512 + nw]), "dv")
    k.wait("tensor", "dv", dvv)
    k.emit("tensor", lambda e: e.matmul(pb6[:, 0:nw], s["ones"][0:1, :], s["rows2"][0:1, 0 * 512:0 * 512 + nw], start=True, stop=True), None)
    pe = k.emit("tensor", lambda e: e.matmul(pb7[:, 0:nw], s["ones"][0:1, :], s["rows2"][0:1, 1 * 512:1 * 512 + nw], start=True, stop=True), "pe")
    k.wait("vector", "pe", pe)
    dv = k.emit("vector", lambda e: e.tensor_copy(s["f32s"][:, 0:nw], pb6[:, 0:nw]), "dv")
    for t in range(nD):
        k.emit("vector",
               lambda e, t=t: e.tensor_sub(pb5[:, 0:nw], s["rch"][:, t * nw:(t + 1) * nw], pb7[:, 0:nw]),
               None)
        k.emit("vector",
               lambda e, t=t: e.tensor_mul(s["nst"][:, t * nw:(t + 1) * nw], pb5[:, 0:nw], s["f32s"][:, 0:nw]),
               "dv")
    return k.sems["dv"]


def emit_nrs_tiles(k, nw, dst_slice_fn):
    s = k.sbuf
    pb4 = s["pb4"]
    k.wait("tensor", "dv", k.sems["dv"])
    pe = k.emit("tensor", lambda e: e.matmul(pb4[:, 0:nw], s["ones"][0:1, :], s["rows2"][0:1, 2 * 512:2 * 512 + nw], start=True, stop=True), "pe")
    k.wait("vector", "pe", pe)
    for t in range(k.cfg.nD):
        k.emit("vector",
               lambda e, t=t: e.tensor_mul(dst_slice_fn(t), s["nst"][:, t * nw:(t + 1) * nw], pb4[:, 0:nw]),
               "dv")
    return k.sems["dv"]



def phase_ln0(k):
    cfg = k.cfg
    s, d = k.sbuf, k.dram
    nw = cfg.TCH
    shn = dview(d["sh_n0"])
    for c in range(cfg.nTCo):
        ld = k.dma("sync", s["rch"][:].rearrange("p (n w) -> p n w", n=16),
                   dview(d["resid"])[:, :, c * nw:(c + 1) * nw], sem="lnld")
        k.wait("vector", "lnld", ld)
        k.wait("scalar", "lnld", ld)
        k.wait("tensor", "lnld", ld)
        dv = emit_ln_chunk(k, nw, with_rs=False)
        k.wait("gpsimd", "dv", dv)
        st = k.pdma("gpsimd", [(
            shn[:, :, c * nw:(c + 1) * nw],
            s["nst"][:].rearrange("p (n w) -> p n w", n=16),
            shn[:, :, cfg.To + c * nw:cfg.To + (c + 1) * nw],
            s["nst"][:].rearrange("p (n w) -> p n w", n=16),
        )], sem="nsh")
        k.wait("vector", "nsh", st)
        k.wait("sync", "nsh", st)
        k.wait("scalar", "nsh", st)
    k.wait("gpsimd", "nsh", k.sems["nsh"])
    k.flag_write("n0_a")



BG_DG = 0
BG_MX = 1040
BG_VT = 25616
BG_KT = 25872
BG_XP = 26896
BG_XT = 27920
BG_UU = 28944
BG_SH = 29968
BG_SL = 32016
BG_EA = 34064
BG_CO = 34576
NS_R = 0
NS_TB = 1152
NS_E = 1536
NS_PT = 2560
NS_KP = 3584
NS_BV = 3840
NS_CR = 4352
RC_BA = 0
RC_T1 = 512
RC_G = 2048
RC_LNB = 2560
RC_BETA = 3072
RC_LNS = 3584
RC_LNS16 = 4096
RC_CH = 4608
RC_GE = 5120
RC_ONES = 5248
RC_SP = 5376


def phase_attn_a(k):
    cfg = k.cfg
    s, d = k.sbuf, k.dram
    nD = cfg.nD
    pb = [s[f"pb{i}"] for i in range(8)]
    big, nst, rch = s["bigb"], s["nst"], s["rch"]

    def MM(out, lhsT, rhs, start, stop, pe=False):
        return k.emit("tensor", lambda e, out=out, lhsT=lhsT, rhs=rhs,
                      start=start, stop=stop:
                      e.matmul(out, lhsT, rhs, start=start, stop=stop),
                      "pe" if pe else None)

    def mxsc(ft):
        return big[:, BG_MX + ft * 512: BG_MX + (ft + 1) * 512]

    k.emit("vector", lambda e: e.memset(big[:, BG_SH:BG_SH + 4096], 0))
    k.emit("gpsimd", lambda e: e.memset(s["tails"][:], 0))
    k.emit("gpsimd", lambda e: e.memset(rch[0:16, RC_ONES:RC_ONES + 128], 1.0))
    k.wait("vector", "gp", k.sems["gp"])
    k.wait("scalar", "gp", k.sems["gp"])
    k.emit("scalar", lambda e: e.activation(out=s["colsA"][0:16, 255:256],
                                            in_=s["miscT"][0:16, 1:2], func=AF.Exp))
    k.wait("vector", "ac", k.sems["ac"])
    k.emit("vector", lambda e: e.tensor_scalar_mul(
        s["colsA"][0:16, 254:255], s["colsA"][0:16, 255:256], -1.0))
    k.wait("gpsimd", "rcp", k.sems["rcp"])

    k.tap("dp0", s["bdpc"][:])
    k.flag_poll("n0_a", "xs")
    k.wait("sync", "xs", k.sems["xs"])

    shn = dview(d["sh_n0"])
    wq = dview(d["aw_qkv"])
    wz = dview(d["aw_z"])
    wba = dview(d["aw_ba"])
    featA = dview(d["sh_featA"])

    pe_sc = {}
    for sc in range(cfg.nTCf):
        slot = sc % 2
        nview = s["nch"][:, slot * 8192: slot * 8192 + 8192].rearrange(
            "p (n w) -> p n w", n=16)
        if sc >= 2:
            k.wait("sync", "pe", pe_sc[sc - 2])
        ln = k.dma("sync", nview, shn[:, :, sc * 512:(sc + 1) * 512], sem=f"na{slot}")
        k.wait("tensor", f"na{slot}", ln)
        k.wait("tensor", "ac", k.sems["ac"])
        k.wait("tensor", "dv", k.sems["dv"])

        acq = {}
        pes_ft = {}
        for ft in range(48):
            wsl = ft % 6
            wq_ = "sync"
            wtile = s["wbuf"][:, wsl * 2048: wsl * 2048 + 2048]
            if ft >= 6:
                k.wait(wq_, "pe", pes_ft[ft - 6])
            elif ft <= 1 and sc > 0:
                k.wait(wq_, "pe", k.sems["pe"])
            src = wq[:, :, ft * 128:(ft + 1) * 128] if ft < 32 else \
                wz[:, :, (ft - 32) * 128:(ft - 31) * 128]
            lw = k.dma(wq_, wtile.rearrange("p (n c) -> p n c", n=16), src,
                       sem=f"wa{wsl}")
            k.wait("tensor", f"wa{wsl}", lw)
            if ft >= 2 and (ft - 2) in acq:
                k.wait("tensor", "ac", acq[ft - 2])
            pbank = pb[ft % 2]
            for dt_ in range(nD):
                pes_ft[ft] = MM(pbank[:, 0:512], wtile[:, dt_ * 128:(dt_ + 1) * 128],
                                nview[:, dt_, :], dt_ == 0, dt_ == nD - 1,
                                pe=(dt_ == nD - 1))
            if ft >= 32:
                k.wait("scalar", "pe", k.sems["pe"])
                acq[ft] = k.emit("scalar", lambda e, ft=ft, pbank=pbank: e.activation(
                    out=mxsc(ft), in_=pbank[:, 0:512], func=AF.Silu))
                continue
            crs = NS_CR + (ft % 2) * 520
            k.wait("scalar", "pe", k.sems["pe"])
            k.emit("scalar", lambda e, crs=crs, ft=ft: e.copy(
                nst[:, crs:crs + 3], s["tails"][:, ft * 4: ft * 4 + 3]))
            k.emit("scalar", lambda e, crs=crs, pbank=pbank: e.copy(
                nst[:, crs + 3:crs + 515], pbank[:, 0:512]))
            ac = k.emit("scalar", lambda e, crs=crs, ft=ft: e.copy(
                s["tails"][:, ft * 4: ft * 4 + 3], nst[:, crs + 512:crs + 515]))
            acc = s["f32s"][:, 0:512] if ft % 2 == 0 else rch[:, 7680:8192]
            k.wait("vector", "ac", ac)
            k.emit("vector", lambda e, crs=crs, ft=ft, acc=acc: e.tensor_scalar_mul(
                acc, nst[:, crs:crs + 512], s["convw"][:, ft * 4: ft * 4 + 1]))
            for kk in range(1, 4):
                k.emit("vector", lambda e, crs=crs, ft=ft, kk=kk, acc=acc:
                       e.scalar_tensor_tensor(
                           out=acc, in0=nst[:, crs + kk: crs + kk + 512],
                           scalar=s["convw"][:, ft * 4 + kk: ft * 4 + kk + 1],
                           in1=acc, op0=ALU.mult, op1=ALU.add))
            dv = k.emit("vector", lambda e: e.engine_nop())
            k.wait("scalar", "dv", dv)
            acq[ft] = k.emit("scalar", lambda e, ft=ft, acc=acc: e.activation(
                out=mxsc(ft), in_=acc, func=AF.Silu))
        k.wait("tensor", "ac", k.sems["ac"])
        if sc == 0:
            k.tap("mx0", big[:, BG_MX:BG_MX + 24576])

        k.wait("sync", "pe", k.sems["pe"])
        lb = k.dma("sync", s["wbuf"][:, 0:512].rearrange("p (n c) -> p n c", n=16),
                   wba[:, :, 0:32], sem="wba")
        k.wait("tensor", "wba", lb)
        k.wait("tensor", "dv", k.sems["dv"])
        for dt_ in range(nD):
            MM(pb[4][0:16, 0:512], s["wbuf"][:, dt_ * 32:dt_ * 32 + 16],
               nview[:, dt_, :], dt_ == 0, dt_ == nD - 1, pe=(dt_ == nD - 1))
        for dt_ in range(nD):
            MM(pb[5][0:16, 0:512], s["wbuf"][:, dt_ * 32 + 16:dt_ * 32 + 32],
               nview[:, dt_, :], dt_ == 0, dt_ == nD - 1, pe=(dt_ == nD - 1))
        k.wait("vector", "pe", k.sems["pe"])
        k.emit("vector", lambda e: e.tensor_copy(rch[0:16, RC_BA:RC_BA + 512],
                                                 pb[4][0:16, 0:512]))
        k.emit("vector", lambda e: e.tensor_copy(rch[32:48, RC_BA:RC_BA + 512],
                                                 pb[5][0:16, 0:512]))

        bsl = rch[0:16, RC_BA:RC_BA + 512]
        asl = rch[32:48, RC_BA:RC_BA + 512]
        T1 = rch[0:16, RC_T1:RC_T1 + 512]
        T2 = rch[0:16, RC_T1 + 512:RC_T1 + 1024]
        T3 = rch[0:16, RC_T1 + 1024:RC_T1 + 1536]

        k.wait("scalar", "dv", k.sems["dv"])
        k.emit("scalar", lambda e: e.activation(out=rch[0:16, RC_BETA:RC_BETA + 512],
                                                in_=bsl, func=AF.Sigmoid))
        k.emit("scalar", lambda e: e.activation(out=T1, in_=bsl, func=AF.Identity,
                                                scale=-1.0))
        ac = k.emit("scalar", lambda e: e.activation(out=T2, in_=T1, func=AF.Abs))
        k.wait("scalar", "ac", ac)
        k.emit("scalar", lambda e: e.activation(out=T2, in_=T2, func=AF.Exp, scale=-1.0))
        ac = k.emit("scalar", lambda e: e.activation(out=T2, in_=T2, func=AF.Ln, bias=1.0))
        k.wait("vector", "ac", ac)
        k.emit("vector", lambda e: e.tensor_scalar_max(T3, T1, 0.0))
        k.emit("vector", lambda e: e.tensor_add(T3, T3, T2))
        dv = k.emit("vector", lambda e: e.tensor_scalar_mul(
            rch[0:16, RC_LNB:RC_LNB + 512], T3, -1.0))
        k.wait("scalar", "dv", dv)
        acb = k.emit("scalar", lambda e: e.activation(out=T1, in_=asl, func=AF.Identity,
                                                      bias=s["miscT"][0:16, 0:1]))
        k.wait("scalar", "ac", acb)
        ac = k.emit("scalar", lambda e: e.activation(out=T2, in_=T1, func=AF.Abs))
        k.wait("scalar", "ac", ac)
        k.emit("scalar", lambda e: e.activation(out=T2, in_=T2, func=AF.Exp, scale=-1.0))
        ac = k.emit("scalar", lambda e: e.activation(out=T2, in_=T2, func=AF.Ln, bias=1.0))
        k.wait("vector", "ac", ac)
        k.emit("vector", lambda e: e.tensor_scalar_max(T3, T1, 0.0))
        k.emit("vector", lambda e: e.tensor_add(T3, T3, T2))
        k.emit("vector", lambda e: e.tensor_scalar_mul(
            rch[0:16, RC_G:RC_G + 512], T3, s["colsA"][0:16, 254:255]))

        for r in range(8):
            ac = k.emit("scalar", lambda e, r=r: e.activation(
                out=nst[:, NS_CR:NS_CR + 512], in_=mxsc(8 + r), func=AF.Square))
            k.wait("tensor", "ac", ac)
            MM(pb[5][0:16, 0:512], s["selk"][:, r * 16:(r + 1) * 16],
               nst[:, NS_CR:NS_CR + 512], r == 0, False, pe=True)
            k.wait("scalar", "pe", k.sems["pe"])
            ac = k.emit("scalar", lambda e, r=r: e.activation(
                out=nst[:, NS_CR:NS_CR + 512], in_=mxsc(r), func=AF.Square))
            k.wait("tensor", "ac", ac)
            MM(pb[5][0:16, 0:512], s["selq"][:, r * 16:(r + 1) * 16],
               nst[:, NS_CR:NS_CR + 512], False, r == 7, pe=True)
            if r < 7:
                k.wait("scalar", "pe", k.sems["pe"])
        k.wait("scalar", "pe", k.sems["pe"])
        k.emit("scalar", lambda e: e.activation(out=rch[0:16, RC_LNS:RC_LNS + 512],
                                                in_=pb[5][0:16, 0:512], func=AF.Ln,
                                                bias=1e-6))
        ac = k.emit("scalar", lambda e: e.activation(
            out=rch[0:16, RC_LNS:RC_LNS + 512],
            in_=rch[0:16, RC_LNS:RC_LNS + 512], func=AF.Identity, scale=-0.5))
        k.wait("vector", "ac", ac)
        dv = k.emit("vector", lambda e: e.tensor_copy(
            nst[0:8, NS_TB:NS_TB + 512], rch[0:8, RC_LNS:RC_LNS + 512]))
        k.wait("tensor", "dv", dv)
        MM(pb[6][0:16, 0:512], s["exp2"][0:8, 0:16], nst[0:8, NS_TB:NS_TB + 512],
           True, True, pe=True)
        k.wait("vector", "pe", k.sems["pe"])
        k.emit("vector", lambda e: e.tensor_copy(
            rch[0:16, RC_LNS16:RC_LNS16 + 512], pb[6][0:16, 0:512]))

        if sc == 0:
            k.tap("rows0", rch[:, 0:8192])
        for c in range(4):
            emit_dn_chunk(k, sc, c)
            ch = sc * 4 + c
            k.wait("gpsimd", "dv", k.sems["dv"])
            st = k.pdma("gpsimd", [(
                featA[:, 0:16, ch * 128:(ch + 1) * 128],
                big[:, BG_CO:BG_CO + 2048].rearrange("p (n t) -> p n t", n=16),
                featA[:, 16:32, ch * 128:(ch + 1) * 128],
                big[:, BG_CO:BG_CO + 2048].rearrange("p (n t) -> p n t", n=16),
            )], sem="fsh")
            k.wait("vector", "fsh", st)
        pe_sc[sc] = k.sems["pe"]

    k.wait("gpsimd", "fsh", k.sems["fsh"])
    k.flag_write("featA")


def emit_dn_chunk(k, sc, c):
    cfg = k.cfg
    s = k.sbuf
    pb = [s[f"pb{i}"] for i in range(8)]
    big, nst, rch = s["bigb"], s["nst"], s["rch"]
    cA, cB = s["colsA"], s["colsB"]
    LNC = float(np.log(128.0 ** -0.5))
    c0 = c * 128

    def MM(out, lhsT, rhs, start, stop, pe=False):
        return k.emit("tensor", lambda e, out=out, lhsT=lhsT, rhs=rhs,
                      start=start, stop=stop:
                      e.matmul(out, lhsT, rhs, start=start, stop=stop),
                      "pe" if pe else None)

    def mx(ft, w=128):
        return big[:, BG_MX + ft * 512 + c0: BG_MX + ft * 512 + c0 + w]

    G = rch[0:16, RC_CH:RC_CH + 128]
    GLS = rch[0:16, RC_CH + 128:RC_CH + 256]
    NGS = rch[0:16, RC_CH + 256:RC_CH + 384]
    GRT = rch[0:16, RC_CH + 384:RC_CH + 512]
    LNS16c = rch[0:16, RC_LNS16 + c0:RC_LNS16 + c0 + 128]
    SP1 = rch[0:16, RC_SP:RC_SP + 128]
    SP2 = rch[0:16, RC_SP + 128:RC_SP + 256]

    k.emit("vector", lambda e: e.tensor_tensor_scan(
        out=G, data0=rch[0:16, RC_ONES:RC_ONES + 128],
        data1=rch[0:16, RC_G + c0:RC_G + c0 + 128], initial=0.0,
        op0=ALU.mult, op1=ALU.add))
    k.emit("vector", lambda e: e.tensor_add(
        GLS, G, rch[0:16, RC_LNB + c0:RC_LNB + c0 + 128]))
    k.emit("vector", lambda e: e.tensor_add(GLS, GLS, LNS16c))
    k.emit("vector", lambda e: e.scalar_tensor_tensor(
        out=NGS, in0=G, scalar=-1.0, in1=LNS16c, op0=ALU.mult, op1=ALU.add))
    k.emit("vector", lambda e: e.scalar_tensor_tensor(
        out=GRT, in0=G, scalar=-1.0, in1=LNS16c, op0=ALU.mult, op1=ALU.add))
    k.emit("vector", lambda e: e.tensor_scalar_add(GRT, GRT, G[0:16, 127:128]))
    k.emit("vector", lambda e: e.tensor_scalar_mul(
        rch[0:16, RC_GE:RC_GE + 128], rch[0:16, RC_ONES:RC_ONES + 128],
        G[0:16, 127:128]))

    def split3(src, p0, p1, p2):
        (b0, o0), (b1, o1), (b2, o2) = p0, p1, p2
        stage = nst[0:16, NS_CR:NS_CR + 128]
        k.emit("vector", lambda e: e.tensor_copy(stage, src))
        k.emit("vector", lambda e: e.tensor_sub(SP1, src, stage))
        k.emit("vector", lambda e, b0=b0, o0=o0: e.tensor_copy(
            nst[b0:b0 + 16, o0:o0 + 128], stage))
        k.emit("vector", lambda e: e.tensor_copy(stage, SP1))
        k.emit("vector", lambda e: e.tensor_sub(SP2, SP1, stage))
        k.emit("vector", lambda e, b1=b1, o1=o1: e.tensor_copy(
            nst[b1:b1 + 16, o1:o1 + 128], stage))
        return k.emit("vector", lambda e, b2=b2, o2=o2: e.tensor_copy(
            nst[b2:b2 + 16, o2:o2 + 128], SP2))

    split3(G, (0, NS_R), (0, NS_R + 128), (0, NS_R + 256))
    split3(GLS, (0, NS_R + 384), (0, NS_R + 512), (0, NS_R + 640))
    split3(NGS, (0, NS_R + 768), (0, NS_R + 896), (0, NS_R + 1024))
    split3(G, (0, NS_TB), (32, NS_TB), (64, NS_TB))
    split3(GRT, (0, NS_TB + 128), (32, NS_TB + 128), (64, NS_TB + 128))
    k.emit("vector", lambda e: e.tensor_copy(
        nst[0:16, NS_TB + 256:NS_TB + 384], rch[0:16, RC_LNS + c0:RC_LNS + c0 + 128]))
    k.emit("vector", lambda e: e.tensor_copy(
        nst[32:48, NS_TB + 256:NS_TB + 384], rch[0:16, RC_BETA + c0:RC_BETA + c0 + 128]))
    dv = k.emit("vector", lambda e: e.tensor_copy(
        nst[64:80, NS_TB + 256:NS_TB + 384], rch[0:16, RC_GE:RC_GE + 128]))

    k.wait("tensor", "dv", dv)
    for bi in range(3):
        k.emit("tensor", lambda e, bi=bi: e.transpose(
            pb[7][:, bi * 128:bi * 128 + 96],
            nst[0:96, NS_TB + bi * 128:NS_TB + bi * 128 + 128],
            s["ident"][0:96, 0:96]), "pe" if bi == 2 else None)
    k.wait("vector", "pe", k.sems["pe"])
    k.emit("vector", lambda e: e.tensor_copy(cB[:, 0:96], pb[7][:, 0:96]))
    k.emit("vector", lambda e: e.tensor_copy(cB[:, 96:192], pb[7][:, 128:224]))
    dv = k.emit("vector", lambda e: e.tensor_copy(cB[:, 192:288], pb[7][:, 256:352]))
    k.wait("tensor", "dv", dv)

    k.emit("vector", lambda e: e.tensor_copy(cA[:, 208:224], cB[:, 224:240]))
    k.emit("vector", lambda e: e.tensor_add(cA[:, 0:16], cB[:, 0:16], cB[:, 32:48]))
    k.emit("vector", lambda e: e.tensor_add(cA[:, 0:16], cA[:, 0:16], cB[:, 64:80]))
    k.emit("vector", lambda e: e.tensor_add(cA[:, 16:32], cB[:, 96:112], cB[:, 128:144]))
    k.emit("vector", lambda e: e.tensor_add(cA[:, 16:32], cA[:, 16:32], cB[:, 160:176]))
    lnskr = cB[:, 192:200].unsqueeze(2).broadcast_to([128, 8, 2])
    lnsqr = cB[:, 200:208].unsqueeze(2).broadcast_to([128, 8, 2])
    A16 = lambda o: cA[:, o:o + 16].rearrange("p (q r) -> p q r", q=8)
    k.emit("vector", lambda e: e.tensor_add(A16(32), A16(0), lnsqr))
    dv = k.emit("vector", lambda e: e.engine_nop())
    k.wait("scalar", "dv", dv)
    k.emit("scalar", lambda e: e.activation(out=cA[:, 48:64], in_=cA[:, 32:48],
                                            func=AF.Exp, bias=LNC))
    k.emit("scalar", lambda e: e.activation(
        out=cA[:, 64:80].rearrange("p (q r) -> p q r", q=8), in_=lnsqr,
        func=AF.Exp, bias=LNC))
    k.emit("scalar", lambda e: e.activation(out=cA[:, 80:96], in_=cA[:, 16:32],
                                            func=AF.Exp))
    k.emit("vector", lambda e: e.tensor_add(A16(96), A16(0), lnskr))
    dv = k.emit("vector", lambda e: e.engine_nop())
    k.wait("scalar", "dv", dv)
    ac = k.emit("scalar", lambda e: e.activation(out=cA[:, 112:128], in_=cA[:, 96:112],
                                                 func=AF.Exp))
    k.wait("vector", "ac", ac)
    k.emit("vector", lambda e: e.scalar_tensor_tensor(
        out=cA[:, 128:144], in0=cA[:, 112:128], scalar=-1.0, in1=cB[:, 224:240],
        op0=ALU.mult, op1=ALU.mult))
    acg = k.emit("scalar", lambda e: e.activation(out=cA[:, 144:160],
                                                  in_=cB[:, 256:272], func=AF.Exp))

    k.wait("tensor", "ac", acg)
    k.wait("tensor", "dv", k.sems["dv"])
    for kq in range(8):
        bank = pb[0] if kq < 4 else pb[5]
        lane = (kq % 4) * 128
        MM(bank[:, lane:lane + 128], mx(8 + kq), mx(8 + kq), True, True, pe=(kq == 7))
    for kq in range(8):
        bank = pb[1] if kq < 4 else pb[6]
        lane = (kq % 4) * 128
        MM(bank[:, lane:lane + 128], mx(8 + kq), mx(kq), True, True, pe=(kq == 7))
    for kq in range(8):
        k.emit("tensor", lambda e, kq=kq: e.transpose(
            pb[7][:, (kq % 4) * 128:(kq % 4) * 128 + 128], mx(8 + kq), s["ident"][:]),
            "pe" if kq % 4 == 3 else None)
        if kq % 4 == 3:
            k.wait("vector", "pe", k.sems["pe"])
            g0 = (kq // 4) * 4
            dv = k.emit("vector", lambda e, g0=g0: e.tensor_copy(
                big[:, BG_KT + g0 * 128:BG_KT + g0 * 128 + 512], pb[7][:, 0:512]))
            k.wait("tensor", "dv", dv)

    RT = lambda off: nst[0:16, off:off + 128]

    for g in range(2):
        hb = g * 8
        akk = pb[0] if g == 0 else pb[5]
        aqk = pb[1] if g == 0 else pb[6]
        k.wait("vector", "pe", k.sems["pe"])
        dv = k.emit("vector", lambda e, akk=akk: e.tensor_scalar_mul(
            big[:, BG_EA:BG_EA + 512], akk[:, 0:512], -1.0))
        k.wait("tensor", "dv", dv)

        for orient in range(2):
            for hh in range(8):
                h = hb + hh
                bank = pb[2] if hh < 4 else pb[3]
                lane = (hh % 4) * 128
                out = bank[:, lane:lane + 128]
                selh = s["sel"][:, h * 128:(h + 1) * 128]
                ra = NS_R + 384 if orient == 0 else NS_R + 768
                rb = NS_R + 768 if orient == 0 else NS_R + 384
                for t3 in range(3):
                    MM(out, RT(ra + t3 * 128), selh, t3 == 0, False)
                for t3 in range(3):
                    MM(out, selh, RT(rb + t3 * 128), False, False)
                mki = 0 if orient == 0 else 128
                MM(out, s["ident"][:], s["masks"][:, mki:mki + 128], False, True,
                   pe=(hh == 7))
            k.wait("scalar", "pe", k.sems["pe"])
            k.emit("scalar", lambda e: e.activation(out=nst[:, NS_E:NS_E + 512],
                                                    in_=pb[2][:, 0:512], func=AF.Exp))
            ac = k.emit("scalar", lambda e: e.activation(
                out=nst[:, NS_E + 512:NS_E + 1024], in_=pb[3][:, 0:512], func=AF.Exp))
            k.wait("vector", "ac", ac)
            dst = BG_XP if orient == 0 else BG_XT
            ear = big[:, BG_EA:BG_EA + 512].rearrange(
                "p (q t) -> p q t", q=4).unsqueeze(2).broadcast_to([128, 4, 2, 128])
            dv = k.emit("vector", lambda e, dst=dst, ear=ear: e.tensor_mul(
                big[:, dst:dst + 1024].rearrange("p (q r t) -> p q r t", q=4, r=2),
                nst[:, NS_E:NS_E + 1024].rearrange("p (q r t) -> p q r t", q=4, r=2),
                ear))
            k.wait("tensor", "dv", dv)

        for hh in range(8):
            h = hb + hh
            bank = pb[2] if hh < 4 else pb[3]
            lane = (hh % 4) * 128
            out = bank[:, lane:lane + 128]
            selh = s["sel"][:, h * 128:(h + 1) * 128]
            for t3 in range(3):
                MM(out, selh, RT(NS_R + t3 * 128), t3 == 0, False)
            for t3 in range(3):
                MM(out, RT(NS_R + 768 + t3 * 128), selh, False, False)
            MM(out, s["ident"][:], s["masks"][:, 256:384], False, True, pe=(hh == 7))
        k.wait("scalar", "pe", k.sems["pe"])
        k.emit("scalar", lambda e: e.activation(out=nst[:, NS_E:NS_E + 512],
                                                in_=pb[2][:, 0:512], func=AF.Exp))
        ac = k.emit("scalar", lambda e: e.activation(
            out=nst[:, NS_E + 512:NS_E + 1024], in_=pb[3][:, 0:512], func=AF.Exp))
        k.wait("vector", "ac", ac)
        aqr = aqk[:, 0:512].rearrange("p (q t) -> p q t", q=4).unsqueeze(2) \
            .broadcast_to([128, 4, 2, 128])
        dv = k.emit("vector", lambda e, aqr=aqr: e.tensor_mul(
            nst[:, NS_PT:NS_PT + 1024].rearrange("p (q r t) -> p q r t", q=4, r=2),
            nst[:, NS_E:NS_E + 1024].rearrange("p (q r t) -> p q r t", q=4, r=2),
            aqr))
        k.wait("tensor", "dv", dv)

        for qh in range(2):
            for i4 in range(4):
                hh = qh * 4 + i4
                h = hb + hh
                k.emit("tensor", lambda e, i4=i4, h=h: e.transpose(
                    pb[7][:, i4 * 128:i4 * 128 + 128], mx(16 + h),
                    s["ident"][:]), "pe" if i4 == 3 else None)
            k.wait("vector", "pe", k.sems["pe"])
            for i4 in range(4):
                h = hb + qh * 4 + i4
                k.emit("vector", lambda e, i4=i4, h=h: e.tensor_scalar_mul(
                    nst[:, NS_BV + i4 * 128:NS_BV + (i4 + 1) * 128],
                    pb[7][:, i4 * 128:i4 * 128 + 128], cA[:, 208 + h:209 + h]))
            dv = k.emit("vector", lambda e: e.engine_nop())
            k.wait("tensor", "dv", dv)
            for i4 in range(4):
                hh = qh * 4 + i4
                h = hb + hh
                kqt = mx(8 + g * 4 + hh // 2)
                MM(pb[4][:, i4 * 128:i4 * 128 + 128], kqt,
                   big[:, BG_SH + h * 128:BG_SH + (h + 1) * 128], True, False)
                MM(pb[4][:, i4 * 128:i4 * 128 + 128], kqt,
                   big[:, BG_SL + h * 128:BG_SL + (h + 1) * 128], False, True,
                   pe=(i4 == 3))
            k.wait("vector", "pe", k.sems["pe"])
            for i4 in range(4):
                hh = qh * 4 + i4
                h = hb + hh
                k.emit("vector", lambda e, i4=i4, hh=hh, h=h: e.scalar_tensor_tensor(
                    out=big[:, BG_UU + hh * 128:BG_UU + (hh + 1) * 128],
                    in0=pb[4][:, i4 * 128:i4 * 128 + 128],
                    scalar=cA[:, 128 + h:129 + h],
                    in1=nst[:, NS_BV + i4 * 128:NS_BV + (i4 + 1) * 128],
                    op0=ALU.mult, op1=ALU.add))
            dv = k.emit("vector", lambda e: e.engine_nop())
            k.wait("tensor", "dv", dv)

        for j in range(7):
            for hh in range(8):
                bank = pb[2] if hh < 4 else pb[3]
                lane = (hh % 4) * 128
                MM(bank[:, lane:lane + 128],
                   big[:, BG_XT + hh * 128:BG_XT + (hh + 1) * 128],
                   big[:, BG_UU + hh * 128:BG_UU + (hh + 1) * 128], True, False)
                MM(bank[:, lane:lane + 128], s["ident"][:],
                   big[:, BG_UU + hh * 128:BG_UU + (hh + 1) * 128], False, True,
                   pe=(hh == 7))
            k.wait("vector", "pe", k.sems["pe"])
            k.emit("vector", lambda e: e.tensor_copy(
                big[:, BG_UU:BG_UU + 512], pb[2][:, 0:512]))
            dv = k.emit("vector", lambda e: e.tensor_copy(
                big[:, BG_UU + 512:BG_UU + 1024], pb[3][:, 0:512]))
            k.wait("tensor", "dv", dv)
            if j < 6:
                for hh in range(8):
                    bank = pb[4] if hh < 4 else pb[0]
                    lane = (hh % 4) * 128
                    MM(bank[:, lane:lane + 128],
                       big[:, BG_XT + hh * 128:BG_XT + (hh + 1) * 128],
                       big[:, BG_XP + hh * 128:BG_XP + (hh + 1) * 128], True, True,
                       pe=(hh == 7))
                for hh in range(8):
                    bank = pb[2] if hh < 4 else pb[3]
                    lane = (hh % 4) * 128
                    MM(bank[:, lane:lane + 128],
                       big[:, BG_XP + hh * 128:BG_XP + (hh + 1) * 128],
                       big[:, BG_XT + hh * 128:BG_XT + (hh + 1) * 128], True, True,
                       pe=(hh == 7))
                k.wait("vector", "pe", k.sems["pe"])
                k.emit("vector", lambda e: e.tensor_copy(
                    big[:, BG_XP:BG_XP + 512], pb[4][:, 0:512]))
                k.emit("vector", lambda e: e.tensor_copy(
                    big[:, BG_XP + 512:BG_XP + 1024], pb[0][:, 0:512]))
                k.emit("vector", lambda e: e.tensor_copy(
                    big[:, BG_XT:BG_XT + 512], pb[2][:, 0:512]))
                dv = k.emit("vector", lambda e: e.tensor_copy(
                    big[:, BG_XT + 512:BG_XT + 1024], pb[3][:, 0:512]))
                k.wait("tensor", "dv", dv)

        for hh in range(8):
            h = hb + hh
            lane = (hh % 4) * 128
            vsl = BG_VT + (hh % 2) * 128
            MM(pb[2][:, lane:lane + 128], nst[:, NS_PT + hh * 128:NS_PT + (hh + 1) * 128],
               big[:, BG_UU + hh * 128:BG_UU + (hh + 1) * 128], True, True)
            MM(pb[3][:, lane:lane + 128], mx(g * 4 + hh // 2),
               big[:, BG_SH + h * 128:BG_SH + (h + 1) * 128], True, False)
            MM(pb[3][:, lane:lane + 128], mx(g * 4 + hh // 2),
               big[:, BG_SL + h * 128:BG_SL + (h + 1) * 128], False, True, pe=True)
            k.wait("vector", "pe", k.sems["pe"])
            k.emit("vector", lambda e, lane=lane, h=h: e.tensor_scalar_mul(
                s["f32s"][:, lane:lane + 128], pb[3][:, lane:lane + 128],
                cA[:, 48 + h:49 + h]))
            dv = k.emit("vector", lambda e, lane=lane, h=h, vsl=vsl: e.scalar_tensor_tensor(
                out=big[:, vsl:vsl + 128], in0=pb[2][:, lane:lane + 128],
                scalar=cA[:, 64 + h:65 + h], in1=s["f32s"][:, lane:lane + 128],
                op0=ALU.mult, op1=ALU.add))
            k.wait("scalar", "dv", dv)
            ac = k.emit("scalar", lambda e, vsl=vsl, h=h, lane=lane: e.activation(
                out=s["f32s"][:, lane:lane + 128], in_=big[:, vsl:vsl + 128],
                func=AF.Square, accum_out=cA[:, 160 + h:161 + h]))
            k.wait("scalar", "ac", ac)
            k.emit("scalar", lambda e, h=h: e.activation(
                out=cA[:, 176 + h:177 + h], in_=cA[:, 160 + h:161 + h], func=AF.Ln,
                scale=0.0078125, bias=1e-6))
            ac = k.emit("scalar", lambda e, h=h: e.activation(
                out=cA[:, 176 + h:177 + h], in_=cA[:, 176 + h:177 + h], func=AF.Exp,
                scale=-0.5))
            k.wait("vector", "ac", ac)
            osl = BG_VT + ((hh % 2) ^ 1) * 128
            dv = k.emit("vector", lambda e, vsl=vsl, osl=osl, h=h: e.tensor_scalar_mul(
                big[:, osl:osl + 128], big[:, vsl:vsl + 128], cA[:, 176 + h:177 + h]))
            k.wait("tensor", "dv", dv)
            k.emit("tensor", lambda e, osl=osl, lane=lane: e.transpose(
                pb[7][:, lane:lane + 128], big[:, osl:osl + 128], s["ident"][:]),
                "pe")
            k.wait("vector", "pe", k.sems["pe"])
            dv = k.emit("vector", lambda e, lane=lane, h=h: e.tensor_mul(
                big[:, BG_CO + h * 128:BG_CO + (h + 1) * 128],
                pb[7][:, lane:lane + 128], mx(32 + h)))
            k.emit("vector", lambda e, h=h: e.tensor_scalar_mul(
                nst[:, NS_KP:NS_KP + 128],
                big[:, BG_KT + (h // 2) * 128:BG_KT + (h // 2) * 128 + 128],
                cA[:, 80 + h:81 + h]))
            dv = k.emit("vector", lambda e, h=h: e.tensor_scalar_mul(
                nst[:, NS_KP + 128:NS_KP + 256], s["ident"][:], cA[:, 144 + h:145 + h]))
            k.wait("tensor", "dv", dv)
            MM(pb[4][:, lane:lane + 128], nst[:, NS_KP:NS_KP + 128],
               big[:, BG_UU + hh * 128:BG_UU + (hh + 1) * 128], True, False)
            MM(pb[4][:, lane:lane + 128], nst[:, NS_KP + 128:NS_KP + 256],
               big[:, BG_SH + h * 128:BG_SH + (h + 1) * 128], False, False)
            MM(pb[4][:, lane:lane + 128], nst[:, NS_KP + 128:NS_KP + 256],
               big[:, BG_SL + h * 128:BG_SL + (h + 1) * 128], False, True, pe=True)
            k.wait("vector", "pe", k.sems["pe"])
            k.emit("vector", lambda e, lane=lane, h=h: e.tensor_copy(
                big[:, BG_SH + h * 128:BG_SH + (h + 1) * 128], pb[4][:, lane:lane + 128]))
            dv = k.emit("vector", lambda e, lane=lane, h=h: e.tensor_sub(
                big[:, BG_SL + h * 128:BG_SL + (h + 1) * 128],
                pb[4][:, lane:lane + 128],
                big[:, BG_SH + h * 128:BG_SH + (h + 1) * 128]))
            k.wait("tensor", "dv", dv)


def phase_wout(k, featname, wname, nvt, pollflag, outflag):
    cfg = k.cfg
    s, d = k.sbuf, k.dram
    pb = [s[f"pb{i}"] for i in range(8)]
    big = s["bigb"]
    nw = cfg.TCH
    feat = dview(d[featname])
    wo = d[wname].ap().rearrange("(v p) d -> p v d", p=128)
    shn = dview(d["sh_n1"])
    shr = dview(d["sh_nrs"])

    def MM(out, lhsT, rhs, start, stop, pe=False):
        return k.emit("tensor", lambda e, out=out, lhsT=lhsT, rhs=rhs,
                      start=start, stop=stop:
                      e.matmul(out, lhsT, rhs, start=start, stop=stop),
                      "pe" if pe else None)

    k.flag_poll(pollflag, "xs")
    k.wait("sync", "xs", k.sems["xs"])

    for co in range(cfg.nTCo):
        k.wait("sync", "rwb", k.sems.get("rwb", 0))
        lr = k.dma("sync", s["rch"][:].rearrange("p (n w) -> p n w", n=16),
                   dview(d["resid"])[:, :, co * nw:(co + 1) * nw], sem="lnld")
        fa = k.pdma("sync", [(
            big[:, 0:nvt * 512].rearrange("p (n w) -> p n w", n=nvt),
            feat[:, :, co * nw:(co + 1) * nw],
            big[:, 0:nvt * 512].rearrange("p (n w) -> p n w", n=nvt),
            feat[:, :, cfg.To + co * nw:cfg.To + (co + 1) * nw],
        )], sem="fld")
        k.wait("tensor", "fld", fa)
        k.wait("vector", "lnld", lr)
        pes_dt = {}
        for dt_ in range(cfg.nD):
            wsl = dt_ % 3
            wq_ = ("sync", "scalar")[wsl % 2]
            if dt_ >= 3:
                k.wait(wq_, "pe", pes_dt[dt_ - 3])
            elif co > 0 and dt_ <= 1:
                k.wait(wq_, "pe", k.sems["pe"])
            lw = k.dma(wq_,
                       s["wbuf"][:, wsl * 4096:wsl * 4096 + nvt * 128].rearrange(
                           "p (v c) -> p v c", v=nvt),
                       wo[:, :, dt_ * 128:(dt_ + 1) * 128], sem=f"wo{wsl}")
            k.wait("tensor", f"wo{wsl}", lw)
            pbank = pb[dt_ % 2]
            if dt_ >= 2:
                k.wait("tensor", "dv", k.sems["dv"])
            for vt in range(nvt):
                pes_dt[dt_] = MM(
                    pbank[:, 0:512],
                    s["wbuf"][:, wsl * 4096 + vt * 128:wsl * 4096 + (vt + 1) * 128],
                    big[:, vt * 512:(vt + 1) * 512], vt == 0, vt == nvt - 1,
                    pe=(vt == nvt - 1))
            k.wait("vector", "pe", k.sems["pe"])
            k.emit("vector", lambda e, dt_=dt_, pbank=pbank: e.tensor_add(
                s["rch"][:, dt_ * nw:(dt_ + 1) * nw],
                s["rch"][:, dt_ * nw:(dt_ + 1) * nw], pbank[:, 0:512]))
        dv = k.emit("vector", lambda e: e.engine_nop())
        k.wait("scalar", "dv", dv)
        k.wait("tensor", "dv", dv)
        rw = k.dma("scalar", dview(d["resid"])[:, :, co * nw:(co + 1) * nw],
                   s["rch"][:].rearrange("p (n w) -> p n w", n=16), sem="rwb")
        dv = emit_ln_chunk(k, nw, with_rs=True)
        k.wait("gpsimd", "dv", dv)
        st = k.pdma("gpsimd", [(
            shn[:, :, co * nw:(co + 1) * nw],
            s["nst"][:].rearrange("p (n w) -> p n w", n=16),
            shn[:, :, cfg.To + co * nw:cfg.To + (co + 1) * nw],
            s["nst"][:].rearrange("p (n w) -> p n w", n=16),
        )], sem="nsh")
        dv = emit_nrs_tiles(k, nw, lambda t: big[:, 16384 + t * nw:16384 + (t + 1) * nw])
        k.wait("gpsimd", "dv", dv)
        st = k.pdma("gpsimd", [(
            shr[:, :, co * nw:(co + 1) * nw],
            big[:, 16384:24576].rearrange("p (n w) -> p n w", n=16),
            shr[:, :, cfg.To + co * nw:cfg.To + (co + 1) * nw],
            big[:, 16384:24576].rearrange("p (n w) -> p n w", n=16),
        )], sem="nsh")
        k.wait("vector", "nsh", st)
        k.wait("sync", "nsh", st)
    k.wait("sync", "rwb", k.sems["rwb"])
    k.wait("gpsimd", "nsh", k.sems["nsh"])
    k.flag_write(outflag)




def phase_ffn(k, ci):
    cfg = k.cfg
    s, d = k.sbuf, k.dram
    pb = [s[f"pb{i}"] for i in range(8)]
    big, nst, rch = s["bigb"], s["nst"], s["rch"]
    T, To, nD = cfg.T, cfg.To, cfg.nD
    nQC = T // 512
    KR, VR = 0, 2 * T
    QS, PS, OS = 4 * T, 4 * T + 1024, 4 * T + 2048
    QR = 4 * T + 3584
    RSQ = float(128 ** -0.5)

    def MM(out, lhsT, rhs, start, stop, pe=False):
        return k.emit("tensor", lambda e, out=out, lhsT=lhsT, rhs=rhs,
                      start=start, stop=stop:
                      e.matmul(out, lhsT, rhs, start=start, stop=stop),
                      "pe" if pe else None)

    woav = big[:, 28672:36864].bitcast(F32)
    shr = dview(d["sh_nrs"])
    shn1 = dview(d["sh_n1"])
    wqv, wkv, wvv = dview(d["fw_q"]), dview(d["fw_k"]), dview(d["fw_v"])
    qsc = dview(d["qT_scr"])
    woa_rows = d["fw_woa"].ap()

    if ci == 0:
        k.tap("dp1", s["bdpc"][:])
    k.flag_poll(f"n1_f{ci + 1}", "xs")
    k.wait("sync", "xs", k.sems["xs"])
    k.emit("vector", lambda e: e.memset(woav[0:16, 0:T], 0.0))

    for r in range(4):
        k.wait("sync", "pe", k.sems["pe"])
        lw = k.dma("sync", s["wbuf"][:, 0:4096].rearrange("p (n c) -> p n c", n=16),
                   wqv[:, :, r * 256:(r + 1) * 256], sem="wa0")
        lw = k.dma("sync", s["wbuf"][:, 4096:8192].rearrange("p (n c) -> p n c", n=16),
                   wkv[:, :, r * 256:(r + 1) * 256], sem="wa0")
        lw = k.dma("sync", s["wbuf"][:, 8192:12288].rearrange("p (n c) -> p n c", n=16),
                   wvv[:, :, r * 256:(r + 1) * 256], sem="wa0")
        cbv = s["colsB"][:, 0:32].rearrange("p (n c) -> p n c", n=2)
        wrv = woa_rows.rearrange("(n p) c -> p n c", p=128)
        fa = k.pdma("sync", [(
            cbv, wrv[:, 2 * r:2 * r + 2, :],
            cbv, wrv[:, 8 + 2 * r:8 + 2 * r + 2, :])], sem="fld")
        k.wait("tensor", "wa0", lw)
        k.wait("tensor", "fld", fa)
        k.emit("vector", lambda e: e.memset(s["rows"][0:1, 16:20], 0.0))

        pe_tc = {}
        for tc in range(cfg.nTCf):
            slot = tc % 2
            nview = s["nch"][:, slot * 8192: slot * 8192 + 8192].rearrange(
                "p (n w) -> p n w", n=16)
            if tc >= 2:
                k.wait("sync", "pe", pe_tc[tc - 2])
            ln = k.dma("sync", nview, shr[:, :, tc * 512:(tc + 1) * 512],
                       sem=f"na{slot}")
            k.wait("tensor", f"na{slot}", ln)
            k.wait("tensor", "dv", k.sems["dv"])
            k.wait("tensor", "ac", k.sems["ac"])
            for h2 in range(2):
                for dt_ in range(nD):
                    MM(pb[h2][:, 0:512],
                       s["wbuf"][:, 4096 + dt_ * 256 + h2 * 128:
                                 4096 + dt_ * 256 + (h2 + 1) * 128],
                       nview[:, dt_, :], dt_ == 0, dt_ == nD - 1, pe=(dt_ == nD - 1))
                k.wait("vector", "pe", k.sems["pe"])
                k.emit("vector", lambda e, h2=h2, tc=tc: e.tensor_copy(
                    big[:, KR + h2 * T + tc * 512: KR + h2 * T + (tc + 1) * 512],
                    pb[h2][:, 0:512]))
                for dt_ in range(nD):
                    MM(pb[2 + h2][:, 0:512],
                       s["wbuf"][:, dt_ * 256 + h2 * 128: dt_ * 256 + (h2 + 1) * 128],
                       nview[:, dt_, :], dt_ == 0, dt_ == nD - 1, pe=(dt_ == nD - 1))
                k.wait("vector", "pe", k.sems["pe"])
                dv = k.emit("vector", lambda e, h2=h2, tc=tc: e.tensor_copy(
                    big[:, QR + h2 * T + tc * 512: QR + h2 * T + (tc + 1) * 512],
                    pb[2 + h2][:, 0:512]))
                k.wait("scalar", "dv", dv)
                ac = k.emit("scalar", lambda e, h2=h2, tc=tc: e.activation(
                    out=nst[:, 0:512],
                    in_=big[:, QR + h2 * T + tc * 512: QR + h2 * T + (tc + 1) * 512],
                    func=AF.Square))
                k.wait("tensor", "ac", ac)
                pe = MM(pb[6][0:1, 0:512], s["ones"][:, 0:1], nst[:, 0:512],
                        True, True, pe=True)
                k.wait("vector", "pe", pe)
                k.emit("vector", lambda e: e.reduce_max(
                    s["rows"][0:1, 24:25], pb[6][0:1, 0:512], AX.X))
                k.emit("vector", lambda e, h2=h2: e.tensor_max(
                    s["rows"][0:1, 16 + h2:17 + h2], s["rows"][0:1, 16 + h2:17 + h2],
                    s["rows"][0:1, 24:25]))
            for vp in range(2):
                bank = pb[4 + vp]
                for vt in (2 * vp, 2 * vp + 1):
                    off = (vt % 2) * 256
                    for dt_ in range(nD):
                        MM(bank[:, off:off + 256],
                           nview[:, dt_, vt * 128:(vt + 1) * 128],
                           s["wbuf"][:, 8192 + dt_ * 256: 8192 + (dt_ + 1) * 256],
                           dt_ == 0, dt_ == nD - 1,
                           pe=(dt_ == nD - 1 and vt == 2 * vp + 1))
                k.wait("vector", "pe", k.sems["pe"])
                for vt in (2 * vp, 2 * vp + 1):
                    off = (vt % 2) * 256
                    for h2 in range(2):
                        k.emit("vector",
                               lambda e, h2=h2, vt=vt, tc=tc, bank=bank, off=off:
                               e.tensor_copy(
                                   big[:, VR + h2 * T + (tc * 4 + vt) * 128:
                                       VR + h2 * T + (tc * 4 + vt + 1) * 128],
                                   bank[:, off + h2 * 128: off + (h2 + 1) * 128]))
            pe_tc[tc] = k.sems["pe"]

        for h2 in range(2):
            for tc in range(cfg.nTCf):
                k.wait("scalar", "dv", k.sems["dv"])
                ac = k.emit("scalar", lambda e, h2=h2, tc=tc: e.activation(
                    out=nst[:, 0:512],
                    in_=big[:, KR + h2 * T + tc * 512: KR + h2 * T + (tc + 1) * 512],
                    func=AF.Square))
                k.wait("tensor", "ac", ac)
                pe = MM(pb[6][0:1, 0:512], s["ones"][:, 0:1], nst[:, 0:512],
                        True, True, pe=True)
                k.wait("vector", "pe", pe)
                k.emit("vector", lambda e: e.reduce_max(
                    s["rows"][0:1, 24:25], pb[6][0:1, 0:512], AX.X))
                k.emit("vector", lambda e, h2=h2: e.tensor_max(
                    s["rows"][0:1, 18 + h2:19 + h2], s["rows"][0:1, 18 + h2:19 + h2],
                    s["rows"][0:1, 24:25]))
        for h2 in range(2):
            dv = k.emit("vector", lambda e, h2=h2: e.tensor_mul(
                s["rows"][0:1, 24:25], s["rows"][0:1, 16 + h2:17 + h2],
                s["rows"][0:1, 18 + h2:19 + h2]))
            k.wait("scalar", "dv", dv)
            k.emit("scalar", lambda e: e.activation(
                out=s["rows"][0:1, 24:25], in_=s["rows"][0:1, 24:25], func=AF.Ln))
            ac = k.emit("scalar", lambda e: e.activation(
                out=s["rows"][0:1, 24:25], in_=s["rows"][0:1, 24:25], func=AF.Exp,
                scale=0.5))
            k.wait("vector", "ac", ac)
            dv = k.emit("vector", lambda e: e.tensor_copy(
                s["rows2"][0:1, 0:1], s["rows"][0:1, 24:25]))
            k.wait("tensor", "dv", dv)
            pe = MM(pb[6][:, 0:1], s["ones"][0:1, :], s["rows2"][0:1, 0:1],
                    True, True, pe=True)
            k.wait("vector", "pe", pe)
            k.emit("vector", lambda e, h2=h2: e.tensor_scalar_mul(
                s["colsA"][:, h2:h2 + 1], pb[6][:, 0:1], -RSQ))

        for qc in range(nQC):
            nkt = 4 * qc + 4
            for h2 in range(2):
                qsl = big[:, QR + h2 * T + qc * 512: QR + h2 * T + (qc + 1) * 512]
                k.wait("tensor", "dv", k.sems["dv"])
                acq = {}
                pes = {}
                for kt in range(nkt):
                    bank = pb[kt % 2]
                    if kt >= 2:
                        k.wait("tensor", "ac", acq[kt - 2])
                    r_off = kt - 4 * qc
                    if r_off >= 0:
                        MM(bank[:, 0:512], big[:, KR + h2 * T + kt * 128:
                                               KR + h2 * T + (kt + 1) * 128],
                           qsl, True, False)
                        pes[kt] = MM(bank[:, 0:512], s["ident"][:],
                                     s["cmask"][:, 384 - r_off * 128:896 - r_off * 128],
                                     False, True, pe=True)
                    else:
                        pes[kt] = MM(bank[:, 0:512], big[:, KR + h2 * T + kt * 128:
                                                         KR + h2 * T + (kt + 1) * 128],
                                     qsl, True, True, pe=True)
                    if kt >= 1:
                        k.wait("tensor", "ac", acq[kt - 1])
                        MM(pb[3][0:1, 0:512], s["ones"][:, 0:1],
                           big[:, PS + ((kt - 1) % 2) * 512: PS + ((kt - 1) % 2 + 1) * 512],
                           kt - 1 == 0, False)
                        MM(pb[2][:, 0:512], big[:, VR + h2 * T + (kt - 1) * 128:
                                                VR + h2 * T + kt * 128],
                           big[:, PS + ((kt - 1) % 2) * 512: PS + ((kt - 1) % 2 + 1) * 512],
                           kt - 1 == 0, False)
                    k.wait("scalar", "pe", pes[kt])
                    acq[kt] = k.emit("scalar", lambda e, kt=kt, bank=bank, h2=h2:
                                     e.activation(
                                         out=big[:, PS + (kt % 2) * 512:
                                                 PS + (kt % 2 + 1) * 512],
                                         in_=bank[:, 0:512], func=AF.Exp,
                                         bias=s["colsA"][:, h2:h2 + 1], scale=RSQ))
                kt = nkt - 1
                k.wait("tensor", "ac", acq[kt])
                MM(pb[3][0:1, 0:512], s["ones"][:, 0:1],
                   big[:, PS + (kt % 2) * 512: PS + (kt % 2 + 1) * 512],
                   kt == 0, True)
                pvs = MM(pb[2][:, 0:512], big[:, VR + h2 * T + kt * 128:
                                              VR + h2 * T + (kt + 1) * 128],
                         big[:, PS + (kt % 2) * 512: PS + (kt % 2 + 1) * 512],
                         kt == 0, True, pe=True)
                k.wait("vector", "pe", pvs)
                dv = k.emit("vector", lambda e: e.reciprocal(
                    s["rows"][0:1, 512:1024], pb[3][0:1, 0:512]))
                k.emit("vector", lambda e: e.tensor_copy(
                    s["rows2"][0:1, 0:512], s["rows"][0:1, 512:1024]))
                dv = k.emit("vector", lambda e: e.engine_nop())
                k.wait("tensor", "dv", dv)
                pe = MM(pb[5][:, 0:512], s["ones"][0:1, :], s["rows2"][0:1, 0:512],
                        True, True, pe=True)
                k.wait("vector", "pe", pe)
                k.emit("vector", lambda e: e.tensor_copy(s["f32s"][:, 0:512],
                                                         pb[5][:, 0:512]))
                dv = k.emit("vector", lambda e: e.tensor_mul(
                    big[:, OS:OS + 512], pb[2][:, 0:512], s["f32s"][:, 0:512]))
                k.wait("tensor", "dv", dv)
                was = MM(pb[4][0:16, 0:512], s["colsB"][:, h2 * 16:(h2 + 1) * 16],
                         big[:, OS:OS + 512], h2 == 0, h2 == 1, pe=(h2 == 1))
                if h2 == 1:
                    k.wait("vector", "pe", was)
                    k.emit("vector", lambda e, qc=qc: e.tensor_add(
                        woav[0:16, qc * 512:(qc + 1) * 512],
                        woav[0:16, qc * 512:(qc + 1) * 512], pb[4][0:16, 0:512]))

    k.wait("gpsimd", "dv", k.sems["dv"])
    shw = d["sh_woa"].ap()
    st = k.pdma("gpsimd", [(
        shw[0:1, :].rearrange("o (r t) -> (o r) t", r=16), woav[0:16, 0:T],
        shw[1:2, :].rearrange("o (r t) -> (o r) t", r=16), woav[0:16, 0:T])],
        sem="fsh")
    k.wait("gpsimd", "fsh", st)
    k.flag_write(f"woa{ci}")
    k.flag_poll(f"woa{ci}", "xs")
    k.wait("gpsimd", "xs", k.sems["xs"])
    w0 = shw[0:1, :].rearrange("o (r t) -> (o r) t", r=16)
    w1 = shw[1:2, :].rearrange("o (r t) -> (o r) t", r=16)
    k.wait("gpsimd", "rwb", k.sems.get("rwb", 0))
    ld = k.pdma("gpsimd", [
        (rch[0:16, 0:To], w0[:, 0:To], rch[0:16, 0:To], w0[:, To:T]),
        (rch[0:16, 8192 - To:8192], w1[:, 0:To], rch[0:16, 8192 - To:8192], w1[:, To:T]),
    ], sem="gld")
    k.wait("vector", "gld", ld)
    dv = k.emit("vector", lambda e: e.tensor_add(
        rch[0:16, 0:To], rch[0:16, 0:To], rch[0:16, 8192 - To:8192]))
    k.wait("scalar", "dv", dv)
    ac = k.emit("scalar", lambda e: e.activation(
        out=rch[0:16, 8192 - To:8192], in_=rch[0:16, 0:To], func=AF.Tanh))
    k.wait("vector", "ac", ac)
    dvm = k.emit("vector", lambda e: e.tensor_scalar_mul(
        s["sel"][0:16, 0:To], rch[0:16, 8192 - To:8192], 0.1))
    if ci == 0:
        k.tap("mod", s["sel"][0:16, 0:To])

    k.wait("sync", "pe", k.sems["pe"])
    la = k.dma("sync", s["colsB"][:, 32:288].rearrange("p (n c) -> p n c", n=16),
               dview(d["fw_A"]), sem="wa0")
    wiv = dview(d["fw_in"])
    wuv = dview(d["fw_up"])
    wdv = d["fw_down"].ap().rearrange("(ft p) c -> p ft c", p=128)
    resv = dview(d["resid"])
    shn0 = dview(d["sh_n0"])
    for co in range(cfg.nTCo):
        nchv = s["nch"][:, 0:8192].rearrange("p (n w) -> p n w", n=16)
        if co == 0:
            k.wait("sync", "dv", dvm)
        k.wait("sync", "nsh", k.sems.get("nsh", 0))
        k.wait("sync", "rwb", k.sems.get("rwb", 0))
        k.wait("sync", "oput", k.sems.get("oput", 0))
        k.wait("sync", "pe", k.sems["pe"])
        k.wait("scalar", "pe", k.sems["pe"])
        ln = k.pdma("sync", [(
            nchv, shr[:, :, co * 512:(co + 1) * 512],
            nchv, shr[:, :, To + co * 512:To + (co + 1) * 512])], sem="na0")
        lr = k.dma("sync", rch[:].rearrange("p (n w) -> p n w", n=16),
                   resv[:, :, co * 512:(co + 1) * 512], sem="lnld")
        lb = k.dma("sync", nst[0:16, 0:8192], d["fw_B"].ap(), sem="wba")
        k.wait("tensor", "na0", ln)
        k.wait("vector", "lnld", lr)
        k.wait("tensor", "wa0", k.sems["wa0"])
        k.wait("tensor", "dv", k.sems["dv"])
        k.wait("tensor", "ac", k.sems["ac"])
        for dt_ in range(nD):
            MM(pb[0][0:16, 0:512], s["colsB"][:, 32 + dt_ * 16: 32 + (dt_ + 1) * 16],
               nchv[:, dt_, :], dt_ == 0, dt_ == nD - 1, pe=(dt_ == nD - 1))
        k.wait("vector", "pe", k.sems["pe"])
        dvh = k.emit("vector", lambda e, co=co: e.tensor_mul(
            s["wbuf"][0:16, 9216:9728], pb[0][0:16, 0:512],
            s["sel"][0:16, co * 512:(co + 1) * 512]))
        k.wait("tensor", "dv", dvh)
        k.wait("tensor", "wba", lb)
        ld0 = k.dma("sync", s["nch"][:, 8192:16384].rearrange("p (n c) -> p n c", n=64),
                    wdv[:, :, 0:128], sem="ws0")
        ac_silu = {}
        dv_gu = {}
        pe_up = {}
        for f in range(64):
            sl = f % 2
            wq_ = "sync"
            if f >= 2:
                k.wait(wq_, "pe", pe_up[f - 2])
            lw = k.dma(wq_, s["wbuf"][:, sl * 4096: sl * 4096 + 2048]
                       .rearrange("p (n c) -> p n c", n=16),
                       wiv[:, :, f * 128:(f + 1) * 128], sem=f"wm{sl}")
            lw = k.dma(wq_, s["wbuf"][:, sl * 4096 + 2048: sl * 4096 + 4096]
                       .rearrange("p (n c) -> p n c", n=16),
                       wuv[:, :, f * 128:(f + 1) * 128], sem=f"wm{sl}")
            k.wait("tensor", f"wm{sl}", lw)
            gbank = pb[1 + sl]
            if f >= 2:
                k.wait("tensor", "ac", ac_silu[f - 2])
            for dt_ in range(nD):
                MM(gbank[:, 0:512], s["wbuf"][:, sl * 4096 + dt_ * 128:
                                              sl * 4096 + (dt_ + 1) * 128],
                   nchv[:, dt_, :], dt_ == 0, False)
            peg = MM(gbank[:, 0:512], nst[0:16, f * 128:(f + 1) * 128],
                     s["wbuf"][0:16, 9216:9728], False, True, pe=True)
            k.wait("scalar", "pe", peg)
            if f >= 2:
                k.wait("scalar", "dv", dv_gu[f - 2])
            ac_silu[f] = k.emit("scalar", lambda e, sl=sl, gbank=gbank: e.activation(
                out=s["wbuf"][:, 8192 + sl * 512: 8192 + (sl + 1) * 512],
                in_=gbank[:, 0:512], func=AF.Silu))
            if f >= 1:
                k.wait("tensor", "dv", dv_gu[f - 1])
            for dt_ in range(nD):
                MM(pb[3][:, 0:512], s["wbuf"][:, sl * 4096 + 2048 + dt_ * 128:
                                              sl * 4096 + 2048 + (dt_ + 1) * 128],
                   nchv[:, dt_, :], dt_ == 0, dt_ == nD - 1, pe=(dt_ == nD - 1))
            pe_up[f] = k.sems["pe"]
            k.wait("vector", "pe", pe_up[f])
            k.wait("vector", "ac", ac_silu[f])
            dv_gu[f] = k.emit("vector", lambda e, f=f, sl=sl: e.tensor_mul(
                big[:, f * 512:(f + 1) * 512], pb[3][:, 0:512],
                s["wbuf"][:, 8192 + sl * 512: 8192 + (sl + 1) * 512]))
        k.wait("gpsimd", "pe", pe_up[63])
        lnr = k.pdma("gpsimd", [(
            nst[:].rearrange("p (n w) -> p n w", n=16),
            shn1[:, :, co * 512:(co + 1) * 512],
            nst[:].rearrange("p (n w) -> p n w", n=16),
            shn1[:, :, To + co * 512:To + (co + 1) * 512])], sem="gld")
        k.wait("tensor", "dv", dv_gu[63])
        pe_dn = {}
        for dt_ in range(nD):
            ssl = dt_ % 2
            swq = ("sync", "scalar")[ssl]
            if dt_ >= 1:
                if dt_ == 1:
                    k.wait(swq, "pe", pe_up[63])
                else:
                    k.wait(swq, "pe", pe_dn[dt_ - 2])
                stv = (s["nch"][:, 8192:16384] if ssl == 0 else s["wbuf"][:, 0:8192])
                lw = k.dma(swq, stv.rearrange("p (n c) -> p n c", n=64),
                           wdv[:, :, dt_ * 128:(dt_ + 1) * 128], sem=f"ws{ssl}")
            stripe = s["nch"][:, 8192:16384] if ssl == 0 else s["wbuf"][:, 0:8192]
            k.wait("tensor", f"ws{ssl}", k.sems[f"ws{ssl}"])
            bank = pb[4 + ssl]
            if dt_ >= 2:
                k.wait("tensor", "dv", k.sems["dv"])
            for ft in range(64):
                MM(bank[:, 0:512], stripe[:, ft * 128:(ft + 1) * 128],
                   big[:, ft * 512:(ft + 1) * 512], ft == 0, ft == 63, pe=(ft == 63))
            pe_dn[dt_] = k.sems["pe"]
            k.wait("vector", "pe", pe_dn[dt_])
            k.emit("vector", lambda e, dt_=dt_, bank=bank: e.tensor_add(
                rch[:, dt_ * 512:(dt_ + 1) * 512], rch[:, dt_ * 512:(dt_ + 1) * 512],
                bank[:, 0:512]))
        k.wait("vector", "gld", lnr)
        for dt_ in range(nD):
            MMdv = k.emit("vector", lambda e, dt_=dt_: e.tensor_add(
                rch[:, dt_ * 512:(dt_ + 1) * 512], rch[:, dt_ * 512:(dt_ + 1) * 512],
                nst[:, dt_ * 512:(dt_ + 1) * 512]))
        dv = k.emit("vector", lambda e: e.engine_nop())
        k.wait("scalar", "dv", dv)
        k.wait("tensor", "dv", dv)
        if ci == 0:
            rw = k.dma("scalar", resv[:, :, co * 512:(co + 1) * 512],
                       rch[:].rearrange("p (n w) -> p n w", n=16), sem="rwb")
            dv = emit_ln_chunk(k, 512, with_rs=False)
            k.wait("gpsimd", "dv", dv)
            st = k.pdma("gpsimd", [(
                shn0[:, :, co * 512:(co + 1) * 512],
                nst[:].rearrange("p (n w) -> p n w", n=16),
                shn0[:, :, To + co * 512:To + (co + 1) * 512],
                nst[:].rearrange("p (n w) -> p n w", n=16))], sem="nsh")
            k.wait("vector", "nsh", st)
            k.wait("sync", "nsh", st)
            k.wait("scalar", "nsh", st)
        else:
            k.dma("scalar", d["outT"].ap().rearrange("(n p) t -> p n t", p=128)
                  [:, :, co * 512:(co + 1) * 512],
                  rch[:].rearrange("p (n w) -> p n w", n=16), sem="oput")
    if ci == 0:
        k.wait("gpsimd", "nsh", k.sems["nsh"])
        k.flag_write("n0_b")



def phase_attn_b(k):
    cfg = k.cfg
    s, d = k.sbuf, k.dram
    pb = [s[f"pb{i}"] for i in range(8)]
    nst, rch = s["nst"], s["rch"]
    T, nD = cfg.T, cfg.nD
    cA = s["colsA"]

    def MM(out, lhsT, rhs, start, stop, pe=False):
        return k.emit("tensor", lambda e, out=out, lhsT=lhsT, rhs=rhs,
                      start=start, stop=stop:
                      e.matmul(out, lhsT, rhs, start=start, stop=stop),
                      "pe" if pe else None)

    k.tap("dp2", s["bdpc"][:])
    k.flag_poll("n0_b", "xs")
    k.wait("sync", "xs", k.sems["xs"])
    k.wait("sync", "pe", k.sems["pe"])
    k.wait("sync", "dv", k.sems["dv"])
    lx = k.dma("sync", s["wbuf"][:, 0:4352].rearrange("p (n c) -> p n c", n=16),
               dview(d["bw_x"]), sem="wa0")
    k.wait("tensor", "wa0", lx)
    k.wait("vector", "rwb", k.sems.get("rwb", 0))
    k.emit("vector", lambda e: e.memset(cA[:, 0:16], 0.0))
    k.emit("vector", lambda e: e.memset(s["rows"][0:1, 3072:3200], 1.0))
    k.wait("tensor", "dv", k.sems["dv"])
    pe = MM(pb[5][0:16, 0:16], s["ident"][0:16, 0:16], s["ident"][0:16, 0:16],
            True, True, pe=True)
    k.wait("vector", "pe", pe)
    k.emit("vector", lambda e: e.tensor_copy(cA[0:16, 16:32], pb[5][0:16, 0:16]))
    shn = dview(d["sh_n0"])
    wgv = dview(d["bw_g"])
    featB = dview(d["sh_featB"])

    pe_tch = {}
    dv_tch = {}
    for tch in range(cfg.nTCf):
        slot = tch % 2
        nchv = s["nch"][:, slot * 8192: slot * 8192 + 8192].rearrange(
            "p (n w) -> p n w", n=16)
        if tch >= 2:
            k.wait("sync", "pe", pe_tch[tch - 2])
            k.wait("sync", "dv", dv_tch[tch - 2])
        cols = slice(tch * 512, (tch + 1) * 512)
        ln = k.pdma("sync", [
            (nchv[:, 0:8, :], shn[:, 0:8, cols], nchv[:, 0:8, :], shn[:, 8:16, cols]),
            (nchv[:, 8:16, :], shn[:, 8:16, cols], nchv[:, 8:16, :], shn[:, 0:8, cols]),
        ], sem=f"na{slot}")
        k.wait("tensor", f"na{slot}", ln)
        k.wait("tensor", "dv", k.sems["dv"])
        k.wait("tensor", "ac", k.sems["ac"])
        for dt_ in range(nD):
            MM(pb[0][:, 0:512], s["wbuf"][:, dt_ * 272: dt_ * 272 + 128],
               nchv[:, dt_, :], dt_ == 0, dt_ == nD - 1, pe=(dt_ == nD - 1))
        for dt_ in range(nD):
            MM(pb[1][:, 0:512], s["wbuf"][:, dt_ * 272 + 128: dt_ * 272 + 256],
               nchv[:, dt_, :], dt_ == 0, dt_ == nD - 1, pe=(dt_ == nD - 1))
        for dt_ in range(nD):
            MM(pb[2][0:16, 0:512], s["wbuf"][:, dt_ * 272 + 256: dt_ * 272 + 272],
               nchv[:, dt_, :], dt_ == 0, dt_ == nD - 1, pe=(dt_ == nD - 1))
        k.wait("vector", "pe", k.sems["pe"])
        k.emit("vector", lambda e: e.tensor_copy(rch[:, 3072:3584], pb[0][:, 0:512]))
        k.emit("vector", lambda e: e.tensor_copy(nst[:, 1024:1536], pb[1][:, 0:512]))
        dv = k.emit("vector", lambda e: e.tensor_copy(rch[0:16, 3584:4096],
                                                      pb[2][0:16, 0:512]))
        k.wait("tensor", "dv", dv)
        if tch == 0:
            k.tap("bproj", rch[:, 3072:4096])
        for h in range(16):
            k.wait("vector", "pe", k.sems["pe"])
            pe = MM(pb[3][0:1, 0:512], cA[0:16, 16 + h:17 + h],
                    rch[0:16, 3584:4096], True, True, pe=True)
            k.wait("vector", "pe", pe)
            dv0 = k.emit("vector", lambda e: e.tensor_copy(
                s["rows"][0:1, 0:512], pb[3][0:1, 0:512]))
            k.wait("tensor", "dv", dv0)
            pe = MM(pb[3][:, 0:512], s["rows"][0:1, 3072:3200],
                    s["rows"][0:1, 0:512], True, True, pe=True)
            k.wait("vector", "pe", pe)
            dv = k.emit("vector", lambda e: e.tensor_add(
                rch[:, 0:512], pb[3][:, 0:512], rch[:, 3072:3584]))
            k.wait("scalar", "dv", dv)
            k.emit("scalar", lambda e: e.activation(
                out=rch[:, 512:1024], in_=rch[:, 0:512], func=AF.Abs))
            k.emit("scalar", lambda e: e.activation(
                out=rch[:, 512:1024], in_=rch[:, 512:1024], func=AF.Exp, scale=-1.0))
            ac = k.emit("scalar", lambda e: e.activation(
                out=rch[:, 512:1024], in_=rch[:, 512:1024], func=AF.Ln, bias=1.0))
            k.wait("vector", "ac", ac)
            k.emit("vector", lambda e: e.tensor_scalar_max(
                rch[:, 1024:1536], rch[:, 0:512], 0.0))
            k.emit("vector", lambda e: e.tensor_add(
                rch[:, 1024:1536], rch[:, 1024:1536], rch[:, 512:1024]))
            dv = k.emit("vector", lambda e, h=h: e.tensor_scalar_mul(
                rch[:, 0:512], rch[:, 1024:1536], s["bAT"][:, h:h + 1]))
            k.wait("scalar", "dv", dv)
            k.emit("scalar", lambda e: e.activation(
                out=rch[:, 1536:2048], in_=rch[:, 0:512], func=AF.Exp))
            ac = k.emit("scalar", lambda e: e.activation(
                out=rch[:, 2048:2560], in_=rch[:, 1024:1536], func=AF.Square))
            k.wait("vector", "ac", ac)
            if tch == 0 and h == 0:
                k.tap("bdelta", rch[:, 1024:1536])
                k.tap("ba", rch[:, 1536:2048])
                k.tap("bdd", rch[:, 2048:2560])
            k.emit("vector", lambda e, h=h: e.tensor_tensor_scan(
                rch[:, 2560:3072], rch[:, 1536:2048], rch[:, 2048:2560],
                cA[:, h:h + 1], ALU.mult, ALU.add))
            k.emit("vector", lambda e, h=h: e.tensor_copy(
                cA[:, h:h + 1], rch[:, 3071:3072]))
            dv = k.emit("vector", lambda e: e.tensor_mul(
                nst[:, 1536:2048], rch[:, 2560:3072], nst[:, 1024:1536]))
            if tch == 0 and h == 0:
                k.tap("bscan", rch[:, 2560:3072])
                k.tap("btt", nst[:, 1536:2048])
            if tch == 0 and h == 7:
                k.tap("bscan7", rch[:, 2560:3072])
            k.wait("tensor", "dv", dv)
            lhsT = (s["selk"][:, h * 16:(h + 1) * 16] if h < 8
                    else s["selq"][:, (h - 8) * 16:(h - 7) * 16])
            MM(pb[4][0:16, 0:512], lhsT, nst[:, 1536:2048], h == 0, h == 15,
               pe=(h == 15))
        if tch == 0:
            k.tap("selgv", s["selg"][:])
        k.wait("vector", "pe", k.sems["pe"])
        dv = k.emit("vector", lambda e: e.tensor_copy(
            nst[0:16, 2048:2560], pb[4][0:16, 0:512]))
        k.wait("tensor", "dv", dv)
        if tch < 2:
            k.tap(f"by{tch}", nst[0:16, 2048:2560])
        dvf = {}
        acf = {}
        pe_gf = {}
        for ft in range(8):
            wsl = ft % 2
            wsl3 = ft % 3
            wq_ = "sync"
            yb = pb[5] if wsl == 0 else pb[3]
            gb = pb[6] if wsl == 0 else pb[0]
            if ft >= 3:
                k.wait(wq_, "pe", pe_gf[ft - 3])
            else:
                k.wait(wq_, "pe", k.sems["pe"])
            lw = k.dma(wq_, s["wbuf"][:, 4352 + wsl3 * 2048: 4352 + wsl3 * 2048 + 2048]
                       .rearrange("p (n c) -> p n c", n=16),
                       wgv[:, :, ft * 128:(ft + 1) * 128], sem=f"bg{wsl3}")
            k.wait("tensor", f"bg{wsl3}", lw)
            if ft >= 2:
                k.wait("tensor", "dv", dvf[ft - 2])
                k.wait("tensor", "ac", acf[ft - 2])
            MM(yb[:, 0:512], s["selg"][0:16, ft * 128:(ft + 1) * 128],
               nst[0:16, 2048:2560], True, True)
            for dt_ in range(nD):
                pe_gf[ft] = MM(gb[:, 0:512],
                               s["wbuf"][:, 4352 + wsl3 * 2048 + dt_ * 128:
                                         4352 + wsl3 * 2048 + (dt_ + 1) * 128],
                               nchv[:, dt_, :], dt_ == 0, dt_ == nD - 1,
                               pe=(dt_ == nD - 1))
            k.wait("scalar", "pe", k.sems["pe"])
            if ft >= 1:
                k.wait("scalar", "dv", dvf[ft - 1])
            acf[ft] = k.emit("scalar", lambda e, gb=gb, ft=ft: e.activation(
                out=nst[:, 2560:3072], in_=gb[:, 0:512], func=AF.Sigmoid,
                bias=s["bbg"][:, ft:ft + 1]))
            k.wait("vector", "ac", acf[ft])
            k.emit("vector", lambda e, ft=ft, yb=yb, nchv=nchv: e.scalar_tensor_tensor(
                out=s["f32s"][:, 0:512], in0=nchv[:, ft, :],
                scalar=s["bdpc"][:, ft:ft + 1], in1=yb[:, 0:512],
                op0=ALU.mult, op1=ALU.add))
            dvf[ft] = k.emit("vector", lambda e, ft=ft: e.tensor_mul(
                nst[:, 3072 + ft * 512: 3072 + (ft + 1) * 512],
                s["f32s"][:, 0:512], nst[:, 2560:3072]))
            if tch == 0 and ft == 7:
                k.tap("bsig7", nst[:, 2560:3072])
                k.tap("bf327", s["f32s"][:, 0:512])
        if tch == 0:
            k.tap("bfst", nst[:, 3072:7168])
            k.tap("bxg0", s["nch"][:, slot * 8192: slot * 8192 + 8192])
        k.wait("gpsimd", "dv", k.sems["dv"])
        st = k.pdma("gpsimd", [(
            featB[:, 0:8, cols], nst[:, 3072:7168].rearrange("p (n w) -> p n w", n=8),
            featB[:, 8:16, cols], nst[:, 3072:7168].rearrange("p (n w) -> p n w", n=8),
        )], sem="fsh")
        k.wait("vector", "fsh", st)
        k.wait("scalar", "fsh", st)
        pe_tch[tch] = k.sems["pe"]
        dv_tch[tch] = k.sems["dv"]
    k.wait("gpsimd", "fsh", k.sems["fsh"])
    k.flag_write("featB")


def kernel_build(cfg):
    k = K(cfg)
    declare(k)
    load_consts(k)
    ldv = k.sems["cld"]
    for e in ("tensor", "vector", "scalar", "gpsimd"):
        k.wait(e, "cld", ldv)
    k.wait("sync", "rcp", k.sems["rcp"])
    cast_weights(k)
    phase_ln0(k)
    phase_attn_a(k)
    phase_wout(k, "sh_featA", "aw_out", 32, "featA", "n1_f1")
    phase_ffn(k, 0)
    phase_attn_b(k)
    phase_wout(k, "sh_featB", "bw_o", 16, "featB", "n1_f2")
    phase_ffn(k, 1)
    finale(k)
    return k


def finale(k):
    cfg = k.cfg
    s, d = k.sbuf, k.dram
    if "n0" in cfg.taps:
        k.flag_poll("n0_a", "xs")
        k.wait("gpsimd", "xs", k.sems["xs"])
        st = k.dma("gpsimd", d["tap_n0"].ap(), d["sh_n0"].ap(), sem="tput")
        k.wait("gpsimd", "tput", st)
    if "featA" in cfg.taps:
        st = k.dma("gpsimd", d["tap_featA"].ap(), d["sh_featA"].ap(), sem="tput")
        k.wait("gpsimd", "tput", st)
    if "n1" in cfg.taps:
        k.flag_poll("n1_f1", "xs")
        k.wait("gpsimd", "xs", k.sems["xs"])
        st = k.dma("gpsimd", d["tap_n1"].ap(), d["sh_n1"].ap(), sem="tput")
        st = k.dma("gpsimd", d["tap_nrs"].ap(), d["sh_nrs"].ap(), sem="tput")
        k.wait("gpsimd", "tput", st)
    if "n0b" in cfg.taps:
        k.flag_poll("n0_b", "xs")
        k.wait("gpsimd", "xs", k.sems["xs"])
        st = k.dma("gpsimd", d["tap_n0b"].ap(), d["sh_n0"].ap(), sem="tput")
        k.wait("gpsimd", "tput", st)
    if "featB" in cfg.taps:
        k.flag_poll("featB", "xs")
        k.wait("gpsimd", "xs", k.sems["xs"])
        st = k.dma("gpsimd", d["tap_featB"].ap(), d["sh_featB"].ap(), sem="tput")
        k.wait("gpsimd", "tput", st)
    k.wait("gpsimd", "oput", k.sems["oput"])
    k.wait("sync", "oput", k.sems["oput"])


def kernel(**inputs):
    cfg = Cfg()
    k = kernel_build(cfg)
    nc = k.build()
    nonce = int(time.time_ns() // 1000 & 0x0FFFFFFF) | 1
    in_maps = [host_pack(inputs, cfg, core, nonce) for core in range(8)]
    res = run_bass_kernel_spmd(nc, in_maps, core_ids=list(range(8)))
    T, To, D = cfg.T, cfg.To, cfg.D
    out = np.zeros((4, T, D), np.float32)
    for core in range(8):
        s, bidx = core & 1, core // 2
        out[bidx, s * To:(s + 1) * To, :] = np.asarray(res.results[core]["outT"]).T
    return out
